# version 09
# baseline (speedup 1.0000x reference)
import numpy as np
from contextlib import ExitStack
import concourse.bass as bass
import concourse.mybir as mybir
from concourse.bass_utils import run_bass_kernel_spmd

F32 = mybir.dt.float32
BF16 = mybir.dt.bfloat16
AF = mybir.ActivationFunctionType
ALU = mybir.AluOpType
AX = mybir.AxisListType

T = 2048
D = 1024
NTT = T // 128
NDC = D // 128
NTB = T // 512
NEG = -3.0e38


class Sem:
    def __init__(self, h):
        self.h = h
        self.v = 0

    def ti(self, instr, n=1):
        instr.then_inc(self.h, n)
        self.v += n
        return self.v


_NCTR = [0]


def uname(s):
    _NCTR[0] += 1
    return "%s%d" % (s, _NCTR[0])


class EC:
    def __init__(self, eng, sem):
        self.e = eng
        self.s = sem

    def do(self, instr):
        self.s.ti(instr)
        self.e.wait_ge(self.s.h, self.s.v)
        return instr


class P:
    def __init__(self, name):
        self.nc = bass.Bass("TRN2", target_bir_lowering=False)
        self.ctx = ExitStack()
        self.name = name
        self.sems = {}
        self.cur_ctx = None

    def sbuf(self, name, shape, dt, ctx=None):
        return (ctx or self.cur_ctx or self.ctx).enter_context(
            self.nc.sbuf_tensor(uname(name), list(shape), dt))

    def psum(self, name, shape, dt=F32, ctx=None):
        return (ctx or self.cur_ctx or self.ctx).enter_context(
            self.nc.psum_tensor(uname(name), list(shape), dt))

    def begin_pool(self):
        self._local = {}
        self._next = 0

    def sem(self, name):
        if not hasattr(self, "_local"):
            self.begin_pool()
        if name not in self._local:
            key = "pool%d" % self._next
            self._next += 1
            if key not in self.sems:
                self.sems[key] = Sem(self.ctx.enter_context(self.nc.semaphore(uname(key))))
            self._local[name] = self.sems[key]
        return self._local[name]

    def dram_in(self, name, shape):
        return self.nc.dram_tensor(name, list(shape), F32, kind="ExternalInput").ap()

    def dram_out(self, name, shape):
        return self.nc.dram_tensor(name, list(shape), F32, kind="ExternalOutput").ap()

    def common(self, max_nk=8, max_bias=4096):
        nc = self.nc
        self.max_nk = max_nk
        self.ident = self.sbuf("c_ident", (128, 128), BF16)
        self.identf = self.sbuf("c_identf", (128, 128), F32)
        self.ones_row = self.sbuf("c_ones", (1, 512), BF16)
        self.zrow = self.sbuf("c_zrow", (1, 64), BF16)
        self.ident_in = self.nc.dram_tensor("c_ident_in", [128, 128], F32,
                                            kind="ExternalInput").ap()
        self.s1 = self.sbuf("st_s1", (128, NTT), F32)
        self.s2 = self.sbuf("st_s2", (128, NTT), F32)
        self.sa = self.sbuf("st_sa", (128, NTT), F32)
        self.sb = self.sbuf("st_sb", (128, NTT), F32)
        self.junk = self.sbuf("st_junk", (128, 1024), F32)
        self.pw = [self.sbuf("pool_wf", (128, max_nk * 128), F32) for _ in range(2)]
        self.pwb = [self.sbuf("pool_wb", (128, max_nk * 128), BF16) for _ in range(2)]
        self.brf = self.sbuf("pool_brf", (1, max_bias), F32)
        self.brb = self.sbuf("pool_brb", (1, max_bias), BF16)
        self.cst = self.sbuf("c_cst", (128, 8), F32)
        self._cst_map = {}
        with nc.Block(no_gpsimd_drain=True) as blk:
            @blk.sync
            def _(sync):
                s = self.sem("c_ld")
                s.ti(sync.dma_start(out=self.identf[:], in_=self.ident_in[:, :]), 16)
                sync.wait_ge(s.h, s.v)

            @blk.vector
            def _(vector):
                vector.memset(self.ones_row[:], 1.0)
                vector.memset(self.zrow[:], 0.0)
        with nc.Block(no_gpsimd_drain=True) as blk:
            @blk.vector
            def _(vector):
                vector.tensor_copy(self.ident[:], self.identf[:])

    def const_col(self, val):
        if val not in self._cst_map:
            idx = len(self._cst_map)
            assert idx < 8
            self._cst_map[val] = idx
            with self.nc.Block() as blk:
                @blk.vector
                def _(vector):
                    vector.memset(self.cst[:, idx:idx + 1], float(val))
        return self.cst[:, self._cst_map[val]:self._cst_map[val] + 1]

    def transpose_bf(self, items, n_psum=4, add_to=False):
        nc = self.nc
        with ExitStack() as lctx:
            pt = [self.psum("tp", (128, 128), BF16, ctx=lctx) for k in range(n_psum)]
            s_t = self.sem("tr_t")
            s_v = self.sem("tr_v")
            bt, bv = s_t.v, s_v.v
            with nc.Block(no_gpsimd_drain=True) as blk:
                @blk.tensor
                def _(tensor):
                    for k, (src, dst) in enumerate(items):
                        if k >= n_psum:
                            tensor.wait_ge(s_v.h, bv + k - n_psum + 1)
                        s_t.ti(tensor.transpose(pt[k % n_psum][:, :], src, self.ident[:]))

                @blk.vector
                def _(vector):
                    for k, (src, dst) in enumerate(items):
                        vector.wait_ge(s_t.h, bt + k + 1)
                        if add_to:
                            s_v.ti(vector.tensor_tensor(dst, dst, pt[k % n_psum][:, :], ALU.add))
                        else:
                            s_v.ti(vector.tensor_copy(dst, pt[k % n_psum][:, :]))

    def ln_stats_stream(self, x_hbm, n=NTT, width=D, eps=1e-5):
        nc = self.nc
        with ExitStack() as lctx:
            xb = [self.sbuf("lnx", (128, width), F32, ctx=lctx) for _ in range(3)]
            s_d = [self.sem("lnst_d%d" % j) for j in range(3)]
            s_s = self.sem("lnst_s")
            s_v = self.sem("lnst_v")
            bd = [sd.v for sd in s_d]
            bs, bv = s_s.v, s_v.v
            with nc.Block(no_gpsimd_drain=True) as blk:
                @blk.sync
                def _(sync):
                    for i in range(n):
                        if i >= 3:
                            sync.wait_ge(s_s.h, bs + i - 2)
                            sync.wait_ge(s_v.h, bv + i - 2)
                        s_d[i % 3].ti(sync.dma_start(out=xb[i % 3][:],
                                      in_=x_hbm[i * 128:(i + 1) * 128, :]), 16)

                @blk.scalar
                def _(scalar):
                    for i in range(n):
                        scalar.wait_ge(s_d[i % 3].h, bd[i % 3] + (i // 3 + 1) * 16)
                        s_s.ti(scalar.activation(out=self.junk[:, 0:width], in_=xb[i % 3][:],
                                                 func=AF.Square, accum_out=self.s2[:, i:i + 1]))
                        scalar.wait_ge(s_s.h, s_s.v)

                @blk.vector
                def _(vector):
                    for i in range(n):
                        vector.wait_ge(s_d[i % 3].h, bd[i % 3] + (i // 3 + 1) * 16)
                        s_v.ti(vector.reduce_sum(self.s1[:, i:i + 1], xb[i % 3][:], axis=AX.X))
        self.ln_combine(n, width, eps)

    def ln_combine(self, n=NTT, width=D, eps=1e-5):
        nc = self.nc
        with nc.Block(no_gpsimd_drain=True) as blk:
            @blk.vector
            def _(vector):
                c = EC(vector, self.sem("vchain"))
                c.do(vector.tensor_scalar_mul(self.s1[:, 0:n], self.s1[:, 0:n], 1.0 / width))
                c.do(vector.tensor_tensor(self.sa[:, 0:n], self.s1[:, 0:n], self.s1[:, 0:n], ALU.mult))
                c.do(vector.tensor_scalar(self.sb[:, 0:n], self.s2[:, 0:n], 1.0 / width, None, ALU.mult))
                c.do(vector.tensor_tensor(self.sa[:, 0:n], self.sb[:, 0:n], self.sa[:, 0:n], ALU.subtract))
        epsc = self.const_col(eps)
        with nc.Block(no_gpsimd_drain=True) as blk:
            @blk.scalar
            def _(scalar):
                scalar.activation(out=self.sa[:, 0:n], in_=self.sa[:, 0:n], func=AF.Sqrt, bias=epsc)
        with nc.Block(no_gpsimd_drain=True) as blk:
            @blk.vector
            def _(vector):
                vector.reciprocal(self.sa[:, 0:n], self.sa[:, 0:n])
        with nc.Block(no_gpsimd_drain=True) as blk:
            @blk.vector
            def _(vector):
                c = EC(vector, self.sem("vchain"))
                c.do(vector.tensor_tensor(self.sb[:, 0:n], self.s1[:, 0:n], self.sa[:, 0:n], ALU.mult))
                c.do(vector.tensor_scalar_mul(self.sb[:, 0:n], self.sb[:, 0:n], -1.0))

    def ln_apply_to_fm(self, x_hbm, y_fm, n=NTT, fm_w=T, keep_tm_bf=None):
        nc = self.nc
        with ExitStack() as lctx:
            xb = [self.sbuf("lax", (128, D), F32, ctx=lctx) for _ in range(2)]
            yb = [self.sbuf("lay", (128, D), F32, ctx=lctx) for _ in range(2)]
            cb = [self.sbuf("lac", (128, D), BF16, ctx=lctx) for _ in range(2)]
            pt = [self.psum("latp", (128, 128), BF16, ctx=lctx) for _ in range(4)]
            s_d = [self.sem("la_d%d" % j) for j in range(2)]
            s_y = self.sem("la_y")
            s_c = self.sem("la_c")
            s_t = self.sem("la_t")
            s_e = self.sem("la_e")
            bd = [sd.v for sd in s_d]
            by, bc, bt, be = s_y.v, s_c.v, s_t.v, s_e.v
            with nc.Block(no_gpsimd_drain=True) as blk:
                @blk.sync
                def _(sync):
                    for i in range(n):
                        if i >= 2:
                            sync.wait_ge(s_y.h, by + i - 1)
                        s_d[i % 2].ti(sync.dma_start(out=xb[i % 2][:],
                                      in_=x_hbm[i * 128:(i + 1) * 128, :]), 16)

                @blk.vector
                def _(vector):
                    def affine(i):
                        vector.wait_ge(s_d[i % 2].h, bd[i % 2] + (i // 2 + 1) * 16)
                        if i >= 2:
                            vector.wait_ge(s_c.h, bc + i - 1)
                        s_y.ti(vector.tensor_scalar(yb[i % 2][:], xb[i % 2][:],
                                                    self.sa[:, i:i + 1], self.sb[:, i:i + 1],
                                                    ALU.mult, ALU.add))

                    def evacs(i):
                        for c in range(NDC):
                            k = i * NDC + c
                            vector.wait_ge(s_t.h, bt + k + 1)
                            s_e.ti(vector.tensor_copy(
                                y_fm[:, c * fm_w + i * 128: c * fm_w + (i + 1) * 128],
                                pt[k % 4][:, :]))
                    affine(0)
                    if n > 1:
                        affine(1)
                    for i in range(n):
                        evacs(i)
                        if i + 2 < n:
                            affine(i + 2)

                @blk.scalar
                def _(scalar):
                    for i in range(n):
                        scalar.wait_ge(s_y.h, by + i + 1)
                        if i >= 2:
                            scalar.wait_ge(s_e.h, be + (i - 1) * NDC)
                        s_c.ti(scalar.copy(cb[i % 2][:], yb[i % 2][:]))
                        if keep_tm_bf is not None:
                            scalar.copy(keep_tm_bf[:, i * D:(i + 1) * D], yb[i % 2][:])
                            scalar.wait_ge(s_c.h, s_c.v)

                @blk.tensor
                def _(tensor):
                    for i in range(n):
                        tensor.wait_ge(s_c.h, bc + i + 1)
                        for c in range(NDC):
                            k = i * NDC + c
                            if k >= 4:
                                tensor.wait_ge(s_e.h, be + k - 3)
                            s_t.ti(tensor.transpose(pt[k % 4][:, :],
                                                    cb[i % 2][:, c * 128:(c + 1) * 128],
                                                    self.ident[:]))

    def finish_stream(self, x_hbm, acc_fm, out_hbm, n=NTT, a_cols=None, b_cols=None):
        nc = self.nc
        with ExitStack() as lctx:
            xb = [self.sbuf("fsx", (128, D), F32, ctx=lctx) for _ in range(3)]
            ob = [self.sbuf("fso", (128, D), F32, ctx=lctx) for _ in range(2)]
            cb = self.sbuf("fsc", (128, NDC * 128), BF16, ctx=lctx)
            pt = [self.psum("fstp", (128, 128), BF16, ctx=lctx) for _ in range(4)]
            s_d = [self.sem("fs_d%d" % j) for j in range(3)]
            s_a = self.sem("fs_a")
            s_cc = self.sem("fs_c")
            s_t = self.sem("fs_t")
            s_ad = self.sem("fs_ad")
            s_o = [self.sem("fs_o%d" % j) for j in range(2)]
            bd = [sd.v for sd in s_d]
            bo = [so.v for so in s_o]
            ba, bc2, bt, bad = s_a.v, s_cc.v, s_t.v, s_ad.v
            with nc.Block() as blk:
                @blk.sync
                def _(sync):
                    for i in range(n):
                        if i >= 3:
                            sync.wait_ge(s_a.h, ba + i - 2)
                        s_d[i % 3].ti(sync.dma_start(out=xb[i % 3][:],
                                      in_=x_hbm[i * 128:(i + 1) * 128, :]), 16)

                @blk.scalar
                def _(scalar):
                    for i in range(n):
                        if i >= 1:
                            scalar.wait_ge(s_t.h, bt + i * NDC)
                        last = None
                        for c in range(NDC):
                            last = scalar.copy(cb[:, c * 128:(c + 1) * 128],
                                               acc_fm[:, c * T + i * 128: c * T + (i + 1) * 128])
                        s_cc.ti(last)

                @blk.vector
                def _(vector):
                    for i in range(n):
                        vector.wait_ge(s_d[i % 3].h, bd[i % 3] + (i // 3 + 1) * 16)
                        if i >= 2:
                            vector.wait_ge(s_o[i % 2].h, bo[i % 2] + (i // 2) * 16)
                        if a_cols is not None:
                            v = vector.tensor_scalar(ob[i % 2][:], xb[i % 3][:],
                                                     a_cols[:, i:i + 1], b_cols[:, i:i + 1],
                                                     ALU.mult, ALU.add)
                        else:
                            v = vector.tensor_copy(ob[i % 2][:], xb[i % 3][:])
                        s_a.ti(v)
                        for c in range(NDC):
                            k = i * NDC + c
                            vector.wait_ge(s_t.h, bt + k + 1)
                            s_ad.ti(vector.tensor_tensor(ob[i % 2][:, c * 128:(c + 1) * 128],
                                                         ob[i % 2][:, c * 128:(c + 1) * 128],
                                                         pt[k % 4][:, :], ALU.add))

                @blk.tensor
                def _(tensor):
                    for i in range(n):
                        tensor.wait_ge(s_cc.h, bc2 + i + 1)
                        tensor.wait_ge(s_a.h, ba + i + 1)
                        for c in range(NDC):
                            k = i * NDC + c
                            if k >= 4:
                                tensor.wait_ge(s_ad.h, bad + k - 3)
                            s_t.ti(tensor.transpose(pt[k % 4][:, :],
                                                    cb[:, c * 128:(c + 1) * 128],
                                                    self.ident[:]))

                @blk.gpsimd
                def _(gpsimd):
                    for i in range(n):
                        gpsimd.wait_ge(s_ad.h, bad + (i + 1) * NDC)
                        s_o[i % 2].ti(gpsimd.dma_start(out=out_hbm[i * 128:(i + 1) * 128, :],
                                      in_=ob[i % 2][:]), 16)
                    gpsimd.wait_ge(s_o[0].h, s_o[0].v)
                    gpsimd.wait_ge(s_o[1].h, s_o[1].v)

    def proj_fm(self, w_hbm, din, dout, x_fm, evac, x_w=T, n_tb=NTB,
                bias_hbm_row=None, evac_on="scalar", pre_dma=None,
                post_scalar=None, post_vector=None, gpsimd_body=None):
        nc = self.nc
        nk = din // 128
        assert nk <= self.max_nk
        nofb = dout // 128
        tbw = x_w // n_tb
        assert tbw <= 512
        s_dma = [self.sem("p_dma%d" % j) for j in range(2)]
        s_bias = self.sem("p_bias")
        s_cast, s_mm, s_ev = (self.sem("p_cast"),
                              self.sem("p_mm"), self.sem("p_ev"))
        b_dma = [sd.v for sd in s_dma]
        b_bias = s_bias.v
        b_cast, b_mm, b_ev = s_cast.v, s_mm.v, s_ev.v
        has_bias = bias_hbm_row is not None
        with ExitStack() as lctx:
            ps = [[self.psum("pp", (128, tbw), F32, ctx=lctx) for _ in range(n_tb)] for _ in range(2)]
            with (nc.Block() if gpsimd_body is not None else nc.Block(no_gpsimd_drain=True)) as blk:
                if gpsimd_body is not None:
                    blk.gpsimd(gpsimd_body)

                @blk.sync
                def _(sync):
                    if pre_dma is not None:
                        for dst, src in pre_dma:
                            s_bias.ti(sync.dma_start(out=dst, in_=src), 16)
                    if has_bias:
                        s_bias.ti(sync.dma_start(out=self.brf[:, 0:dout], in_=bias_hbm_row), 16)
                    for ofb in range(nofb):
                        if ofb >= 2:
                            sync.wait_ge(s_ev.h, b_ev + ofb - 1)
                        for k in range(nk):
                            s_dma[ofb % 2].ti(sync.dma_start(
                                out=self.pw[ofb % 2][:, k * 128:(k + 1) * 128],
                                in_=w_hbm[k * 128:(k + 1) * 128, ofb * 128:(ofb + 1) * 128]), 16)

                @blk.vector
                def _(vector):
                    if has_bias:
                        vector.wait_ge(s_bias.h, b_bias + 16)
                        vector.tensor_copy(self.brb[:, 0:dout], self.brf[:, 0:dout])

                    def cast(ofb):
                        vector.wait_ge(s_dma[ofb % 2].h, b_dma[ofb % 2] + (ofb // 2 + 1) * nk * 16)
                        if ofb >= 2:
                            vector.wait_ge(s_ev.h, b_ev + ofb - 1)
                        s_cast.ti(vector.tensor_copy(self.pwb[ofb % 2][:, 0:nk * 128],
                                                     self.pw[ofb % 2][:, 0:nk * 128]))
                    if evac_on == "vector":
                        cast(0)
                        if nofb > 1:
                            cast(1)
                        for ofb in range(nofb):
                            vector.wait_ge(s_mm.h, b_mm + ofb + 1)
                            last = evac(ofb, ps[ofb % 2], dict(vector=vector, p=self),
                                        s_ev=s_ev)
                            if last is not None:
                                s_ev.ti(last)
                            if ofb + 2 < nofb:
                                cast(ofb + 2)
                    else:
                        for ofb in range(nofb):
                            cast(ofb)

                @blk.tensor
                def _(tensor):
                    for ofb in range(nofb):
                        tensor.wait_ge(s_cast.h, b_cast + ofb + 1)
                        if ofb >= 2:
                            tensor.wait_ge(s_ev.h, b_ev + ofb - 1)
                        last = None
                        for k in range(nk):
                            for tb in range(n_tb):
                                last = tensor.matmul(ps[ofb % 2][tb][:, :],
                                                     self.pwb[ofb % 2][:, k * 128:(k + 1) * 128],
                                                     x_fm[:, k * x_w + tb * tbw: k * x_w + (tb + 1) * tbw],
                                                     start=(k == 0), stop=(k == nk - 1 and not has_bias))
                        if has_bias:
                            for tb in range(n_tb):
                                last = tensor.matmul(ps[ofb % 2][tb][:, :],
                                                     self.brb[0:1, ofb * 128:(ofb + 1) * 128],
                                                     self.ones_row[0:1, 0:tbw],
                                                     start=False, stop=True)
                        s_mm.ti(last)

                if evac_on == "scalar":
                    @blk.scalar
                    def _(scalar):
                        for ofb in range(nofb):
                            scalar.wait_ge(s_mm.h, b_mm + ofb + 1)
                            last = evac(ofb, ps[ofb % 2], dict(scalar=scalar, p=self))
                            s_ev.ti(last)
                            if post_scalar is not None:
                                post_scalar(ofb, scalar)
                    if post_vector is not None:
                        @blk.vector
                        def _(vector):
                            post_vector(vector)

    def proj_tm(self, w_hbm, din, dout, x_fm, out_tm, bias_hbm_row=None, x_w=T,
                out_stride=None, n=NTT):
        nc = self.nc
        if out_stride is None:
            out_stride = dout
        nk = din // 128
        s_dma, s_mm, s_ev = self.sem("t_dma"), self.sem("t_mm"), self.sem("t_ev")
        b_dma, b_mm, b_ev = s_dma.v, s_mm.v, s_ev.v
        has_bias = bias_hbm_row is not None
        with ExitStack() as lctx:
            wb = self.sbuf("wtm", (128, nk * dout), F32, ctx=lctx)
            wbb = self.sbuf("wtmb", (128, nk * dout), BF16, ctx=lctx)
            pt = [self.psum("ptm", (128, dout), F32, ctx=lctx) for _ in range(4)]
            onec = self.sbuf("onec", (1, 128), BF16, ctx=lctx)
            with nc.Block(no_gpsimd_drain=True) as blk:
                @blk.sync
                def _(sync):
                    for k in range(nk):
                        s_dma.ti(sync.dma_start(out=wb[:, k * dout:(k + 1) * dout],
                                                in_=w_hbm[k * 128:(k + 1) * 128, 0:dout]), 16)
                    if has_bias:
                        s_dma.ti(sync.dma_start(out=self.brf[:, 0:dout], in_=bias_hbm_row), 16)
                    sync.wait_ge(s_dma.h, s_dma.v)

                @blk.vector
                def _(vector):
                    vector.memset(onec[:], 1.0)
            with nc.Block(no_gpsimd_drain=True) as blk:
                @blk.vector
                def _(vector):
                    vector.tensor_copy(wbb[:], wb[:])
                    if has_bias:
                        vector.tensor_copy(self.brb[:, 0:dout], self.brf[:, 0:dout])
            with nc.Block(no_gpsimd_drain=True) as blk:
                @blk.tensor
                def _(tensor):
                    for i in range(n):
                        if i >= 4:
                            tensor.wait_ge(s_ev.h, b_ev + i - 3)
                        for k in range(nk):
                            mm = tensor.matmul(pt[i % 4][:, :],
                                               x_fm[:, k * x_w + i * 128: k * x_w + (i + 1) * 128],
                                               wbb[:, k * dout:(k + 1) * dout],
                                               start=(k == 0), stop=(k == nk - 1 and not has_bias))
                        if has_bias:
                            mm = tensor.matmul(pt[i % 4][:, :], onec[0:1, 0:128],
                                               self.brb[0:1, 0:dout],
                                               start=False, stop=True)
                        s_mm.ti(mm)

                @blk.vector
                def _(vector):
                    for i in range(n):
                        vector.wait_ge(s_mm.h, b_mm + i + 1)
                        s_ev.ti(vector.tensor_copy(out_tm[:, i * out_stride:i * out_stride + dout],
                                                   pt[i % 4][:, :]))

    def softmax_tm(self, buf, n, width, stride=None):
        if stride is None:
            stride = width
        nc = self.nc
        with nc.Block(no_gpsimd_drain=True) as blk:
            @blk.vector
            def _(vector):
                for i in range(n):
                    vector.reduce_max(self.s1[:, i:i + 1], buf[:, i * stride:i * stride + width],
                                      axis=AX.X, negate=True)
        with nc.Block(no_gpsimd_drain=True) as blk:
            @blk.scalar
            def _(scalar):
                for i in range(n):
                    scalar.activation(out=buf[:, i * stride:i * stride + width],
                                      in_=buf[:, i * stride:i * stride + width],
                                      func=AF.Exp, bias=self.s1[:, i:i + 1],
                                      accum_out=self.s2[:, i:i + 1])
        with nc.Block(no_gpsimd_drain=True) as blk:
            @blk.vector
            def _(vector):
                vector.reciprocal(self.s2[:, 0:n], self.s2[:, 0:n])
        with nc.Block(no_gpsimd_drain=True) as blk:
            @blk.vector
            def _(vector):
                for i in range(n):
                    vector.tensor_scalar_mul(buf[:, i * stride:i * stride + width],
                                             buf[:, i * stride:i * stride + width],
                                             self.s2[:, i:i + 1])


def declare_l1_ffn(p):
    nc = p.nc
    return (p.dram_in("l1f_ws1", (D, D)), p.dram_in("l1f_bs1", (1, D)),
            p.dram_in("l1f_ws2", (D, D)), p.dram_in("l1f_bs2", (1, D)),
            p.dram_in("l1f_we1", (31, D, D)), p.dram_in("l1f_be1", (31, D)),
            p.dram_in("l1f_we2", (31, D, D)), p.dram_in("l1f_be2", (31, D)),
            p.dram_in("l1f_wr32", (D, 32)), p.dram_in("l1f_br32", (1, 32)),
            nc.dram_tensor("l1f_gates_scratch", [31, T], F32, kind="Internal").ap())


def build_l1_ffn(p=None, xio=None, aps=None):
    standalone = p is None
    if standalone:
        p = P("l1ffn")
    nc = p.nc
    if aps is None:
        aps = declare_l1_ffn(p)
    (ws1, bs1, ws2, bs2, we1, be1, we2, be2, wr, br, gdram_) = aps
    x_in = xio[0] if xio else p.dram_in("x", (T, D))
    x_out = xio[1] if xio else p.dram_out("y", (T, D))
    gdram = aps[-1]

    cm = p.ctx if standalone else ExitStack()
    with cm:
        if standalone:
            p.common(max_bias=1024)
        p.cur_ctx = cm
        p.begin_pool()
        acc_fm = p.sbuf("acc_fm", (128, NDC * T), F32)
        p.sa2 = p.sbuf("st_sa2", (128, NTT), F32)
        icm = ExitStack()
        p.cur_ctx = icm
        y_fm = p.sbuf("y_fm", (128, NDC * T), BF16)
        h_fm = p.sbuf("h_fm", (128, NDC * T), BF16)
        aff_tm = p.sbuf("aff_tm", (128, NTT * 32), F32)
        gat_tm = p.sbuf("gat_tm", (128, NTT * 32), F32)
        m8 = p.sbuf("m8", (128, NTT * 8), F32)
        gat_fm = p.sbuf("gat_fm", (32, T), F32)
        g_bc = p.sbuf("g_bc", (128, T), F32)
        evs = p.sbuf("evs", (128, 512), F32)

        p.ln_stats_stream(x_in)
        with nc.Block(no_gpsimd_drain=True) as blk:
            @blk.vector
            def _(vector):
                vector.tensor_scalar_add(p.sa2[:, 0:NTT], p.sa[:, 0:NTT], 1.0)
        p.ln_apply_to_fm(x_in, y_fm)

        def evac_gelu(ofb, pstiles, eng):
            sc = eng["scalar"]
            last = None
            for tb in range(NTB):
                last = sc.activation(out=h_fm[:, ofb * T + tb * 512: ofb * T + (tb + 1) * 512],
                                     in_=pstiles[tb][:, :], func=AF.Gelu)
            return last
        p.proj_fm(ws1, D, D, y_fm, evac_gelu, bias_hbm_row=bs1[0:1, :])

        def evac_acc(ofb, pstiles, eng):
            sc = eng["scalar"]
            last = None
            for tb in range(NTB):
                last = sc.copy(acc_fm[:, ofb * T + tb * 512: ofb * T + (tb + 1) * 512],
                               pstiles[tb][:, :])
            return last
        p.proj_fm(ws2, D, D, h_fm, evac_acc, bias_hbm_row=bs2[0:1, :])

        p.proj_tm(wr, D, 32, y_fm, aff_tm, bias_hbm_row=br[0:1, :])
        with nc.Block(no_gpsimd_drain=True) as blk:
            @blk.vector
            def _(vector):
                for i in range(NTT):
                    vector.memset(aff_tm[:, i * 32 + 31: i * 32 + 32], NEG)
        p.softmax_tm(aff_tm, NTT, 32)
        with nc.Block(no_gpsimd_drain=True) as blk:
            @blk.vector
            def _(vector):
                for i in range(NTT):
                    vector.max(m8[:, i * 8:(i + 1) * 8], aff_tm[:, i * 32:(i + 1) * 32])
        with nc.Block(no_gpsimd_drain=True) as blk:
            @blk.vector
            def _(vector):
                for i in range(NTT):
                    vector.tensor_scalar(gat_tm[:, i * 32:(i + 1) * 32],
                                         aff_tm[:, i * 32:(i + 1) * 32],
                                         m8[:, i * 8 + 2:i * 8 + 3], None, ALU.is_ge)
        with nc.Block(no_gpsimd_drain=True) as blk:
            @blk.vector
            def _(vector):
                for i in range(NTT):
                    vector.tensor_tensor(gat_tm[:, i * 32:(i + 1) * 32],
                                         gat_tm[:, i * 32:(i + 1) * 32],
                                         aff_tm[:, i * 32:(i + 1) * 32], ALU.mult)
        with ExitStack() as lctx:
            gtp = p.psum("gtp", (32, 128), F32, ctx=lctx)
            s_g, s_gv = p.sem("s_g"), p.sem("s_gv")
            bg, bgv = s_g.v, s_gv.v
            with nc.Block(no_gpsimd_drain=True) as blk:
                @blk.tensor
                def _(tensor):
                    for i in range(NTT):
                        if i >= 1:
                            tensor.wait_ge(s_gv.h, bgv + i)
                        s_g.ti(tensor.transpose(gtp[:, :], gat_tm[:, i * 32:(i + 1) * 32],
                                                p.identf[:]))

                @blk.vector
                def _(vector):
                    for i in range(NTT):
                        vector.wait_ge(s_g.h, bg + i + 1)
                        s_gv.ti(vector.tensor_copy(gat_fm[0:32, i * 128:(i + 1) * 128], gtp[:, :]))
        sdr = p.sem("sdr")
        with nc.Block(no_gpsimd_drain=True) as blk:
            @blk.sync
            def _(sync):
                sdr.ti(sync.dma_start(out=gdram[:, :], in_=gat_fm[0:31, :]), 16)
                sync.wait_ge(sdr.h, sdr.v)

        evs8 = p.sbuf("evs8", (128, 8 * 512), F32)
        for e in range(31):
            p.proj_fm(we1[e], D, D, y_fm, evac_gelu, bias_hbm_row=be1[e:e + 1, :],
                      pre_dma=[(g_bc[:, :],
                                bass.AP(gdram.tensor, gdram.offset + e * T, [[0, 128], [1, T]]))])

            s_mu = p.sem("s_mu")
            s_gp = p.sem("s_gp")
            bmu, bgp = s_mu.v, s_gp.v
            bev0 = p.sem("p_ev").v

            def evac_mult(ofb, pstiles, eng, s_ev=None, bmu=bmu, bgp=bgp):
                vec = eng["vector"]
                last = None
                for tb in range(NTB):
                    k = ofb * NTB + tb
                    if k >= 8:
                        vec.wait_ge(s_gp.h, bgp + (k - 8) // NTB + 1)
                    m = vec.tensor_tensor(evs8[:, (k % 8) * 512:(k % 8 + 1) * 512],
                                          pstiles[tb][:, :],
                                          g_bc[:, tb * 512:(tb + 1) * 512], ALU.mult)
                    if tb == NTB - 1:
                        last = m
                    else:
                        s_mu.ti(m)
                return last

            def gp_body(gpsimd, bmu=bmu, bgp=bgp, bev0=bev0):
                sev = p.sem("p_ev")
                for ofb in range(NDC):
                    for tb in range(NTB):
                        k = ofb * NTB + tb
                        if tb < NTB - 1:
                            gpsimd.wait_ge(s_mu.h, bmu + ofb * (NTB - 1) + tb + 1)
                        else:
                            gpsimd.wait_ge(sev.h, bev0 + ofb + 1)
                        op = gpsimd.tensor_tensor(
                            acc_fm[:, ofb * T + tb * 512: ofb * T + (tb + 1) * 512],
                            acc_fm[:, ofb * T + tb * 512: ofb * T + (tb + 1) * 512],
                            evs8[:, (k % 8) * 512:(k % 8 + 1) * 512], ALU.add)
                        if tb == NTB - 1:
                            s_gp.ti(op)
            p.proj_fm(we2[e], D, D, h_fm, evac_mult,
                      bias_hbm_row=be2[e:e + 1, :], evac_on="vector",
                      gpsimd_body=gp_body)

        icm.close()
        p.cur_ctx = cm
        p.begin_pool()
        p.finish_stream(x_in, acc_fm, x_out, a_cols=p.sa2[:, 0:NTT], b_cols=p.sb[:, 0:NTT])
    p.cur_ctx = None
    return p


IDENT128 = np.eye(128, dtype=np.float32)


def l1_ffn_inmaps(x_batch, inp):
    ws1 = np.ascontiguousarray(inp["L1_ffn_ws1"], np.float32)
    ws2 = np.ascontiguousarray(inp["L1_ffn_ws2"], np.float32)
    we1 = np.ascontiguousarray(inp["L1_ffn_We1"], np.float32)
    we2 = np.ascontiguousarray(inp["L1_ffn_We2"], np.float32)
    wr32 = np.zeros((D, 32), np.float32)
    wr32[:, 0:31] = inp["L1_ffn_Wr"]
    br32 = np.zeros((1, 32), np.float32)
    br32[0, 0:31] = inp["L1_ffn_br"]
    base = dict(l1f_ws1=ws1, l1f_bs1=np.ascontiguousarray(inp["L1_ffn_bs1"], np.float32).reshape(1, D),
                l1f_ws2=ws2, l1f_bs2=np.ascontiguousarray(inp["L1_ffn_bs2"], np.float32).reshape(1, D),
                l1f_we1=we1, l1f_be1=np.ascontiguousarray(inp["L1_ffn_Be1"], np.float32),
                l1f_we2=we2, l1f_be2=np.ascontiguousarray(inp["L1_ffn_Be2"], np.float32),
                l1f_wr32=wr32, l1f_br32=br32, c_ident_in=IDENT128)
    return [dict(base, x=np.ascontiguousarray(xb, np.float32)) for xb in x_batch]




def declare_l0_ffn(p):
    nc = p.nc
    return (
        p.dram_in("l0f_cwT", (D, 40)),
        p.dram_in("l0f_bias40", (1, 40)),
        p.dram_in("l0f_wgu", (40, D, D)),
        p.dram_in("l0f_wd", (32, 512, D)),
        nc.dram_tensor("l0f_gates_scratch", [32, T], F32, kind="Internal").ap(),
    )


def build_l0_ffn(p=None, xio=None, aps=None):
    standalone = p is None
    if standalone:
        p = P("l0ffn")
    nc = p.nc
    if aps is None:
        aps = declare_l0_ffn(p)
    (cwT, bias40, wgu, wd, gdram) = aps
    x_in = xio[0] if xio else p.dram_in("x", (T, D))
    x_out = xio[1] if xio else p.dram_out("y", (T, D))
    cm = p.ctx if standalone else ExitStack()
    with cm:
        if standalone:
            p.common(max_bias=128)
        p.cur_ctx = cm
        p.begin_pool()
        y_fm = p.sbuf("y_fm", (128, NDC * T), BF16)
        acc_fm = p.sbuf("acc_fm", (128, NDC * T), F32)
        sc_tm = p.sbuf("sc_tm", (128, NTT * 40), F32)
        sbp_tm = p.sbuf("sbp_tm", (128, NTT * 40), F32)
        wf_tm = p.sbuf("wf_tm", (128, NTT * 40), F32)
        m8 = p.sbuf("m8", (128, NTT * 8), F32)
        bias_bc = p.sbuf("bias_bc", (128, 40), F32)
        zw = p.sbuf("zw", (128, NTT), F32)
        za = p.sbuf("za", (128, NTT), F32)
        zb = p.sbuf("zb", (128, NTT), F32)

        p.ln_stats_stream(x_in)
        p.ln_apply_to_fm(x_in, y_fm)

        p.proj_tm(cwT, D, 40, y_fm, sc_tm)
        p.softmax_tm(sc_tm, NTT, 40)
        sdr = p.sem("sdr")
        with nc.Block(no_gpsimd_drain=True) as blk:
            @blk.sync
            def _(sync):
                src = bass.AP(bias40.tensor, bias40.offset, [[0, 128], [1, 40]])
                sdr.ti(sync.dma_start(out=bias_bc[:, :], in_=src), 16)
                sync.wait_ge(sdr.h, sdr.v)
        with nc.Block(no_gpsimd_drain=True) as blk:
            @blk.vector
            def _(vector):
                for i in range(NTT):
                    vector.tensor_tensor(sbp_tm[:, i * 40:(i + 1) * 40],
                                         sc_tm[:, i * 40:(i + 1) * 40],
                                         bias_bc[:, :], ALU.add)
        with nc.Block(no_gpsimd_drain=True) as blk:
            @blk.vector
            def _(vector):
                for i in range(NTT):
                    vector.max(m8[:, i * 8:(i + 1) * 8], sbp_tm[:, i * 40:(i + 1) * 40])
        with nc.Block(no_gpsimd_drain=True) as blk:
            @blk.vector
            def _(vector):
                for i in range(NTT):
                    vector.tensor_scalar(wf_tm[:, i * 40:(i + 1) * 40],
                                         sbp_tm[:, i * 40:(i + 1) * 40],
                                         m8[:, i * 8 + 5:i * 8 + 6], None, ALU.is_ge)
        with nc.Block(no_gpsimd_drain=True) as blk:
            @blk.vector
            def _(vector):
                c = EC(vector, p.sem("vchain"))
                for i in range(NTT):
                    c.do(vector.tensor_scalar_mul(sc_tm[:, i * 40:(i + 1) * 40],
                                                  sc_tm[:, i * 40:(i + 1) * 40], 2.5))
                    c.do(vector.tensor_tensor(wf_tm[:, i * 40:(i + 1) * 40],
                                              wf_tm[:, i * 40:(i + 1) * 40],
                                              sc_tm[:, i * 40:(i + 1) * 40], ALU.mult))
                    c.do(vector.reduce_sum(zw[:, i:i + 1],
                                           wf_tm[:, i * 40 + 32:i * 40 + 40], axis=AX.X))
        with nc.Block(no_gpsimd_drain=True) as blk:
            @blk.vector
            def _(vector):
                c = EC(vector, p.sem("vchain"))
                c.do(vector.tensor_tensor(za[:, 0:NTT], p.sa[:, 0:NTT], zw[:, 0:NTT], ALU.mult))
                c.do(vector.tensor_scalar_add(za[:, 0:NTT], za[:, 0:NTT], 1.0))
                c.do(vector.tensor_tensor(zb[:, 0:NTT], p.sb[:, 0:NTT], zw[:, 0:NTT], ALU.mult))
        with ExitStack() as lctx:
            wf_fm = p.sbuf("wf_fm", (32, T), F32, ctx=lctx)
            gtp = p.psum("gtp", (32, 128), F32, ctx=lctx)
            s_g, s_gv = p.sem("s_g"), p.sem("s_gv")
            bg, bgv = s_g.v, s_gv.v
            with nc.Block(no_gpsimd_drain=True) as blk:
                @blk.tensor
                def _(tensor):
                    for i in range(NTT):
                        if i >= 1:
                            tensor.wait_ge(s_gv.h, bgv + i)
                        s_g.ti(tensor.transpose(gtp[:, :], wf_tm[:, i * 40:i * 40 + 32],
                                                p.identf[:]))

                @blk.vector
                def _(vector):
                    for i in range(NTT):
                        vector.wait_ge(s_g.h, bg + i + 1)
                        s_gv.ti(vector.tensor_copy(wf_fm[0:32, i * 128:(i + 1) * 128], gtp[:, :]))
            with nc.Block(no_gpsimd_drain=True) as blk:
                @blk.sync
                def _(sync):
                    sdr.ti(sync.dma_start(out=gdram[:, :], in_=wf_fm[0:32, :]), 16)
                    sync.wait_ge(sdr.h, sdr.v)

        ectx = ExitStack()
        gu_bf = p.sbuf("gu_bf", (128, NDC * T), BF16, ctx=ectx)
        h_fm = p.sbuf("h_fm", (128, 4 * T), BF16, ctx=ectx)
        g_bc = p.sbuf("g_bc", (128, T), F32, ctx=ectx)
        tf2 = p.sbuf("tf2", (128, T), BF16, ctx=ectx)

        def evac_gu(ofb, pstiles, eng):
            sc = eng["scalar"]
            last = None
            for tb in range(NTB):
                last = sc.copy(gu_bf[:, ofb * T + tb * 512: ofb * T + (tb + 1) * 512],
                               pstiles[tb][:, :])
            return last

        for e in range(32):
            s_si = p.sem("s_si")
            bsi = s_si.v

            def post_sc(ofb, scalar, bsi=bsi):
                if ofb < 4:
                    return
                c = ofb - 4
                op = scalar.activation(out=gu_bf[:, c * T:(c + 1) * T],
                                       in_=gu_bf[:, c * T:(c + 1) * T],
                                       func=AF.Silu)
                s_si.ti(op)

            def post_vec(vector, bsi=bsi):
                vch = p.sem("vchain")
                for c in range(4):
                    vector.wait_ge(s_si.h, bsi + c + 1)
                    vch.ti(vector.tensor_tensor(tf2[:, :], gu_bf[:, c * T:(c + 1) * T],
                                                gu_bf[:, (c + 4) * T:(c + 5) * T], ALU.mult))
                    vector.wait_ge(vch.h, vch.v)
                    vch.ti(vector.tensor_tensor(h_fm[:, c * T:(c + 1) * T],
                                                tf2[:, :], g_bc[:, :], ALU.mult))
                    vector.wait_ge(vch.h, vch.v)
            p.proj_fm(wgu[e], D, D, y_fm, evac_gu,
                      pre_dma=[(g_bc[:, :],
                                bass.AP(gdram.tensor, gdram.offset + e * T, [[0, 128], [1, T]]))],
                      post_scalar=post_sc, post_vector=post_vec)

            def evac_dn(ofb, pstiles, eng, s_ev=None, e=e):
                vec = eng["vector"]
                last = None
                for tb in range(NTB):
                    sl = acc_fm[:, ofb * T + tb * 512: ofb * T + (tb + 1) * 512]
                    if e == 0:
                        last = vec.tensor_copy(sl, pstiles[tb][:, :])
                    else:
                        last = vec.tensor_tensor(sl, sl, pstiles[tb][:, :], ALU.add)
                return last
            p.proj_fm(wd[e], 512, D, h_fm, evac_dn, evac_on="vector")

        ectx.close()
        p.finish_stream(x_in, acc_fm, x_out, a_cols=za[:, 0:NTT], b_cols=zb[:, 0:NTT])
    p.cur_ctx = None
    return p


def l0_ffn_inmaps(x_batch, inp):
    cwT = np.ascontiguousarray(inp["L0_ffn_classifier_weight"].T, np.float32)
    bias40 = np.ascontiguousarray(inp["L0_ffn_e_score_correction_bias"], np.float32).reshape(1, 40)
    wgu = np.ascontiguousarray(np.transpose(inp["L0_ffn_gate_up_proj"], (0, 2, 1)), np.float32)
    wd = np.ascontiguousarray(np.transpose(inp["L0_ffn_down_proj"], (0, 2, 1)), np.float32)
    base = dict(l0f_cwT=cwT, l0f_bias40=bias40, l0f_wgu=wgu, l0f_wd=wd, c_ident_in=IDENT128)
    return [dict(base, x=np.ascontiguousarray(xb, np.float32)) for xb in x_batch]


def declare_l0_attn_b(p):
    nc = p.nc
    return (
        p.dram_in("l0b_wqT", (D, D)),
        p.dram_in("l0b_bq", (1, D)),
        p.dram_in("l0b_wkT", (D, D)),
        p.dram_in("l0b_bk", (1, D)),
        p.dram_in("l0b_wvT", (D, D)),
        p.dram_in("l0b_bv", (1, D)),
        p.dram_in("l0b_woT", (D, D)),
        p.dram_in("l0b_bo", (1, D)),
        p.dram_in("l0b_orfT", (64, 256)),
    )


def build_l0_attn_b(p=None, xio=None, aps=None):
    standalone = p is None
    if standalone:
        p = P("l0attnb")
    nc = p.nc
    if aps is None:
        aps = declare_l0_attn_b(p)
    (wqT, bq, wkT, bk, wvT, bv, woT, bo, orfT) = aps
    x_in = xio[0] if xio else p.dram_in("x", (T, D))
    x_out = xio[1] if xio else p.dram_out("y", (T, D))
    DKQ = 0.3535533905932738
    EPS = 1e-6

    cm = p.ctx if standalone else ExitStack()
    with cm:
        if standalone:
            p.common(max_bias=1024)
        p.cur_ctx = cm
        p.begin_pool()
        o_fm = p.sbuf("o_fm", (128, NDC * T), BF16)
        orf_f = p.sbuf("orf_f", (128, 256), F32)
        orf_b = p.sbuf("orf_b", (128, 256), BF16)
        n16 = p.sbuf("n16", (128, 1), BF16)
        onecol = p.sbuf("onecol", (128, 1), BF16)
        sld = p.sem("sld")

        with nc.Block(no_gpsimd_drain=True) as blk:
            @blk.sync
            def _(sync):
                sld.ti(sync.dma_start(out=orf_f[0:64, :], in_=orfT[:, :]), 16)
                sld.ti(sync.dma_start(out=orf_f[64:128, :], in_=orfT[:, :]), 16)
                sync.wait_ge(sld.h, sld.v)

            @blk.vector
            def _(vector):
                vector.memset(n16[:], -1.0 / 16.0)
                vector.memset(onecol[:], 1.0)
        with nc.Block(no_gpsimd_drain=True) as blk:
            @blk.scalar
            def _(scalar):
                scalar.mul(orf_f[:, :], orf_f[:, :], DKQ)
        with nc.Block(no_gpsimd_drain=True) as blk:
            @blk.vector
            def _(vector):
                vector.tensor_copy(orf_b[:, :], orf_f[:, :])

        with ExitStack() as lnctx:
            y_fm = p.sbuf("y_fm", (128, NDC * T), BF16, ctx=lnctx)
            p.ln_stats_stream(x_in)
            p.ln_apply_to_fm(x_in, y_fm)

            with ExitStack() as pctx:
                q_bf = p.sbuf("q_bf", (128, T), BF16, ctx=pctx)
                k_bf = p.sbuf("k_bf", (128, T), BF16, ctx=pctx)
                v_bf = p.sbuf("v_bf", (128, T), BF16, ctx=pctx)
                v_tm = p.sbuf("v_tm", (128, NTT * 128), BF16, ctx=pctx)
                q2 = p.sbuf("q2", (128, T), BF16, ctx=pctx)
                phik = p.sbuf("phik", (128, NTT * 256), BF16, ctx=pctx)
                phiq = p.sbuf("phiq", (128, NTT * 256), BF16, ctx=pctx)
                phiqT = p.sbuf("phiqT", (128, 2 * T), BF16, ctx=pctx)
                bufh = p.sbuf("bufh", (128, 2 * 65), BF16, ctx=pctx)
                exps = [p.sbuf("exps", (128, 256), F32, ctx=pctx) for _ in range(2)]
                rc = p.sbuf("rc", (128, 1), F32, ctx=pctx)
                ob = p.sbuf("ob", (128, 128), BF16, ctx=pctx)

                def evac_to(dst):
                    def ev(ofb, pstiles, eng):
                        sc = eng["scalar"]
                        last = None
                        for tb in range(NTB):
                            last = sc.copy(dst[:, tb * 512:(tb + 1) * 512], pstiles[tb][:, :])
                        return last
                    return ev

                def feature_map(qk, hb, phi_dst):
                    with nc.Block(no_gpsimd_drain=True) as blk:
                        @blk.vector
                        def _(vector):
                            vector.tensor_tensor(q2[hb:hb + 64, :], qk[hb:hb + 64, :],
                                                 qk[hb:hb + 64, :], ALU.mult)
                    with ExitStack() as actx:
                        pp = [p.psum("ppb", (128, 256), F32, ctx=actx) for _ in range(2)]
                        pn = [p.psum("pnb", (128, 1), F32, ctx=actx) for _ in range(2)]
                        ncol = [p.sbuf("ncol", (128, 1), F32, ctx=actx) for _ in range(2)]
                        s_m = p.sem("ab_m")
                        s_n = p.sem("ab_n")
                        s_x = p.sem("ab_x")
                        s_v = p.sem("ab_v")
                        bm, bn, bx, bv_ = s_m.v, s_n.v, s_x.v, s_v.v
                        with nc.Block(no_gpsimd_drain=True) as blk:
                            @blk.tensor
                            def _(tensor):
                                for i in range(NTT):
                                    if i >= 2:
                                        tensor.wait_ge(s_x.h, bx + i - 1)
                                        tensor.wait_ge(s_n.h, bn + 2 * (i - 1))
                                    tensor.matmul(pn[i % 2][:, :],
                                                  q2[hb:hb + 64, i * 128:(i + 1) * 128],
                                                  n16[hb:hb + 64, :], start=True, stop=True)
                                    s_m.ti(tensor.matmul(pp[i % 2][:, :],
                                                         qk[hb:hb + 64, i * 128:(i + 1) * 128],
                                                         orf_b[hb:hb + 64, :], start=True, stop=True))

                            @blk.vector
                            def _(vector):
                                for i in range(NTT):
                                    vector.wait_ge(s_m.h, bm + i + 1)
                                    if i >= 2:
                                        vector.wait_ge(s_x.h, bx + i - 1)
                                    s_n.ti(vector.tensor_copy(ncol[i % 2][:, :], pn[i % 2][:, :]))
                                    vector.wait_ge(s_x.h, bx + i + 1)
                                    s_n.ti(vector.tensor_scalar(
                                        phi_dst[:, i * 256:(i + 1) * 256],
                                        exps[i % 2][:, :], EPS, None, ALU.add))

                            @blk.scalar
                            def _(scalar):
                                for i in range(NTT):
                                    scalar.wait_ge(s_n.h, bn + 2 * i + 1)
                                    if i >= 2:
                                        scalar.wait_ge(s_n.h, bn + 2 * (i - 1) + 2)
                                    s_x.ti(scalar.activation(out=exps[i % 2][:, :],
                                                             in_=pp[i % 2][:, :], func=AF.Exp,
                                                             bias=ncol[i % 2][:, :]))

                for hp in range(8):
                    sl = slice(hp * 128, (hp + 1) * 128)
                    p.proj_fm(wqT[:, sl], D, 128, y_fm, evac_to(q_bf), bias_hbm_row=bq[0:1, sl])
                    p.proj_fm(wkT[:, sl], D, 128, y_fm, evac_to(k_bf), bias_hbm_row=bk[0:1, sl])
                    p.proj_fm(wvT[:, sl], D, 128, y_fm, evac_to(v_bf), bias_hbm_row=bv[0:1, sl])
                    items = [(v_bf[:, i * 128:(i + 1) * 128], v_tm[:, i * 128:(i + 1) * 128])
                             for i in range(NTT)]
                    p.transpose_bf(items)

                    for h2 in range(2):
                        hb = 64 * h2
                        feature_map(k_bf, hb, phik)
                        feature_map(q_bf, hb, phiq)
                        with ExitStack() as actx:
                            pk = [p.psum("pkv", (128, 65), F32, ctx=actx) for _ in range(2)]
                            s_m = p.sem("ab_m2")
                            bm = s_m.v
                            with nc.Block(no_gpsimd_drain=True) as blk:
                                @blk.tensor
                                def _(tensor):
                                    for mh in range(2):
                                        for i in range(NTT):
                                            tensor.matmul(pk[mh][:, 0:64],
                                                          phik[:, i * 256 + mh * 128: i * 256 + (mh + 1) * 128],
                                                          v_tm[:, i * 128:(i + 1) * 128][:, hb:hb + 64],
                                                          start=(i == 0), stop=(i == NTT - 1))
                                        for i in range(NTT):
                                            last = tensor.matmul(pk[mh][:, 64:65],
                                                                 phik[:, i * 256 + mh * 128: i * 256 + (mh + 1) * 128],
                                                                 onecol[:, :],
                                                                 start=(i == 0), stop=(i == NTT - 1))
                                        s_m.ti(last)

                                @blk.vector
                                def _(vector):
                                    for mh in range(2):
                                        vector.wait_ge(s_m.h, bm + mh + 1)
                                        vector.tensor_copy(bufh[:, mh * 65:(mh + 1) * 65], pk[mh][:, :])
                        items = [(phiq[:, i * 256 + mh * 128: i * 256 + (mh + 1) * 128],
                                  phiqT[:, mh * T + i * 128: mh * T + (i + 1) * 128])
                                 for i in range(NTT) for mh in range(2)]
                        p.transpose_bf(items)
                        with ExitStack() as actx:
                            po = [p.psum("po", (128, 65), F32, ctx=actx) for _ in range(2)]
                            pt = [p.psum("pot", (64, 128), BF16, ctx=actx) for _ in range(2)]
                            obr = [p.sbuf("obr", (128, 64), BF16, ctx=actx) for _ in range(2)]
                            s_m = p.sem("ab_m3")
                            s_dv = p.sem("ab_dv")
                            s_tp = p.sem("ab_tp")
                            s_ev = p.sem("ab_ev")
                            bm, bdv, btp, bev = s_m.v, s_dv.v, s_tp.v, s_ev.v
                            with nc.Block(no_gpsimd_drain=True) as blk:
                                @blk.tensor
                                def _(tensor):
                                    for i in range(NTT):
                                        if i >= 2:
                                            tensor.wait_ge(s_dv.h, bdv + i - 1)
                                        for mh in range(2):
                                            mm = tensor.matmul(po[i % 2][:, :],
                                                               phiqT[:, mh * T + i * 128: mh * T + (i + 1) * 128],
                                                               bufh[:, mh * 65:(mh + 1) * 65],
                                                               start=(mh == 0), stop=(mh == 1))
                                        s_m.ti(mm)
                                        tensor.wait_ge(s_dv.h, bdv + i + 1)
                                        if i >= 2:
                                            tensor.wait_ge(s_ev.h, bev + i - 1)
                                        s_tp.ti(tensor.transpose(pt[i % 2][:, :], obr[i % 2][:, :],
                                                                 p.ident[:]))

                                @blk.vector
                                def _(vector):
                                    vch = p.sem("vchain")
                                    for i in range(NTT):
                                        vector.wait_ge(s_m.h, bm + i + 1)
                                        vch.ti(vector.reciprocal(rc[:, :], po[i % 2][:, 64:65]))
                                        vector.wait_ge(vch.h, vch.v)
                                        if i >= 2:
                                            vector.wait_ge(s_tp.h, btp + i - 1)
                                        s_dv.ti(vector.tensor_scalar(obr[i % 2][:, :], po[i % 2][:, 0:64],
                                                                     rc[:, 0:1], None, ALU.mult))
                                        vector.wait_ge(s_tp.h, btp + i + 1)
                                        s_ev.ti(vector.tensor_copy(
                                            o_fm[hb:hb + 64, hp * T + i * 128: hp * T + (i + 1) * 128],
                                            pt[i % 2][:, :]))

            with ExitStack() as wctx:
                acc_fm = p.sbuf("acc_fm", (128, NDC * T), F32, ctx=wctx)

                def evac_acc2(ofb, pstiles, eng):
                    sc = eng["scalar"]
                    last = None
                    for tb in range(NTB):
                        last = sc.copy(acc_fm[:, ofb * T + tb * 512: ofb * T + (tb + 1) * 512],
                                       pstiles[tb][:, :])
                    return last
                p.proj_fm(woT, D, D, o_fm, evac_acc2, bias_hbm_row=bo[0:1, :])
                p.finish_stream(x_in, acc_fm, x_out)
    p.cur_ctx = None
    return p


def l0_attn_b_inmaps(x_batch, inp):
    def t(a):
        return np.ascontiguousarray(np.asarray(a, np.float32).T)
    base = dict(l0b_wqT=t(inp["L0_attn_b_Wq"]), l0b_bq=inp["L0_attn_b_bq"].reshape(1, D).astype(np.float32),
                l0b_wkT=t(inp["L0_attn_b_Wk"]), l0b_bk=inp["L0_attn_b_bk"].reshape(1, D).astype(np.float32),
                l0b_wvT=t(inp["L0_attn_b_Wv"]), l0b_bv=inp["L0_attn_b_bv"].reshape(1, D).astype(np.float32),
                l0b_woT=t(inp["L0_attn_b_Wo"]), l0b_bo=inp["L0_attn_b_bo"].reshape(1, D).astype(np.float32),
                l0b_orfT=t(inp["L0_attn_b_orf"]), c_ident_in=IDENT128)
    return [dict(base, x=np.ascontiguousarray(xb, np.float32)) for xb in x_batch]


def declare_l1_attn_a(p):
    nc = p.nc
    return (
        p.dram_in("l1a_a_row", (1, 64)),
        p.dram_in("l1a_bT", (D, 64)),
        p.dram_in("l1a_cmat", (D, 64)),
        p.dram_in("l1a_dcols", (128, NDC)),
        p.dram_in("l1a_ldt_row", (1, D)),
    )


def build_l1_attn_a(p=None, xio=None, aps=None):
    standalone = p is None
    if standalone:
        p = P("l1attna")
    nc = p.nc
    if aps is None:
        aps = declare_l1_attn_a(p)
    (a_row, bT, cmat, dcols, ldt_row) = aps
    x_in = xio[0] if xio else p.dram_in("x", (T, D))
    x_out = xio[1] if xio else p.dram_out("y", (T, D))
    L = 64

    cm = p.ctx if standalone else ExitStack()
    with cm:
        if standalone:
            p.common(max_bias=128)
        p.cur_ctx = cm
        p.begin_pool()
        acc_fm = p.sbuf("acc_fm", (128, NDC * T), F32)
        u_fm = p.sbuf("u_fm", (128, NDC * T), BF16)
        kT = p.sbuf("kT", (128, NDC * L), F32)
        dcol_sb = p.sbuf("dcol_sb", (128, NDC), F32)
        abT = p.sbuf("abT", (128, NDC * 64), F32)
        pw_ = p.sbuf("pw_", (128, NDC * 64), F32)
        gT = p.sbuf("gT", (128, NDC * 64), F32)
        csb = p.sbuf("csb", (128, 64), F32)
        dt_row = p.sbuf("dt_row", (1, D), F32)
        a_sb = p.sbuf("a_sb", (1, 64), F32)
        onec = p.sbuf("onec", (128, 1), F32)
        onesr = p.sbuf("onesr", (1, 128), F32)
        tscr = p.sbuf("tscr", (128, 64), F32)
        sld = p.sem("sld")

        with nc.Block(no_gpsimd_drain=True) as blk:
            @blk.sync
            def _(sync):
                sld.ti(sync.dma_start(out=dt_row[:, :], in_=ldt_row[:, :]), 16)
                sld.ti(sync.dma_start(out=a_sb[:, :], in_=a_row[:, :]), 16)
                sld.ti(sync.dma_start(out=dcol_sb[:, :], in_=dcols[:, :]), 16)
                sync.wait_ge(sld.h, sld.v)

            @blk.vector
            def _(vector):
                vector.memset(onec[:], 1.0)
                vector.memset(onesr[:], 1.0)
        with nc.Block(no_gpsimd_drain=True) as blk:
            @blk.scalar
            def _(scalar):
                scalar.activation(out=dt_row[:, :], in_=dt_row[:, :], func=AF.Exp)
        with ExitStack() as actx:
            pab = p.psum("pab", (128, 64), F32, ctx=actx)
            s_m, s_x = p.sem("aa_m"), p.sem("aa_x")
            bm, bx = s_m.v, s_x.v
            with nc.Block(no_gpsimd_drain=True) as blk:
                @blk.tensor
                def _(tensor):
                    for c in range(NDC):
                        if c >= 1:
                            tensor.wait_ge(s_x.h, bx + c)
                        s_m.ti(tensor.matmul(pab[:, :], dt_row[0:1, c * 128:(c + 1) * 128],
                                             a_sb[0:1, :], start=True, stop=True))

                @blk.scalar
                def _(scalar):
                    for c in range(NDC):
                        scalar.wait_ge(s_m.h, bm + c + 1)
                        s_x.ti(scalar.activation(out=abT[:, c * 64:(c + 1) * 64],
                                                 in_=pab[:, :], func=AF.Exp))
        with ExitStack() as actx:
            wc8 = p.sbuf("wc8", (128, NDC * 64), F32, ctx=actx)
            pcs = p.psum("pcs2", (1, 64), F32, ctx=actx)
            pcb = p.psum("pcb2", (128, 64), F32, ctx=actx)
            csr = p.sbuf("csr2", (1, 64), F32, ctx=actx)
            s_m = p.sem("aa_m3")
            bm = s_m.v
            with nc.Block(no_gpsimd_drain=True) as blk:
                @blk.sync
                def _(sync):
                    for c in range(NDC):
                        sld.ti(sync.dma_start(out=wc8[:, c * 64:(c + 1) * 64],
                                              in_=cmat[c * 128:(c + 1) * 128, :]), 16)
                    sync.wait_ge(sld.h, sld.v)
            with nc.Block(no_gpsimd_drain=True) as blk:
                @blk.tensor
                def _(tensor):
                    for c in range(NDC):
                        s_m.ti(tensor.matmul(pcs[:, :], onec[:, :], wc8[:, c * 64:(c + 1) * 64],
                                             start=(c == 0), stop=(c == NDC - 1)))

                @blk.vector
                def _(vector):
                    vector.wait_ge(s_m.h, bm + NDC)
                    vector.tensor_copy(csr[:, :], pcs[:, :])
            with nc.Block(no_gpsimd_drain=True) as blk:
                @blk.tensor
                def _(tensor):
                    tensor.matmul(pcb[:, :], onesr[0:1, :], csr[0:1, :], start=True, stop=True)
            with nc.Block(no_gpsimd_drain=True) as blk:
                @blk.vector
                def _(vector):
                    vector.tensor_copy(csb[:, :], pcb[:, :])
        with ExitStack() as actx:
            wb8 = p.sbuf("wb8", (128, NDC * 64), F32, ctx=actx)
            with nc.Block(no_gpsimd_drain=True) as blk:
                @blk.sync
                def _(sync):
                    for c in range(NDC):
                        sld.ti(sync.dma_start(out=wb8[:, c * 64:(c + 1) * 64],
                                              in_=bT[c * 128:(c + 1) * 128, :]), 16)
                    sync.wait_ge(sld.h, sld.v)
            with nc.Block(no_gpsimd_drain=True) as blk:
                @blk.vector
                def _(vector):
                    for c in range(NDC):
                        vector.tensor_tensor(gT[:, c * 64:(c + 1) * 64],
                                             wb8[:, c * 64:(c + 1) * 64], csb[:, :], ALU.mult)
                    vector.memset(pw_[:], 1.0)
        with nc.Block(no_gpsimd_drain=True) as blk:
            @blk.vector
            def _(vector):
                vch = p.sem("vchain")
                for l in range(L):
                    for c in range(NDC):
                        vch.ti(vector.tensor_tensor(tscr[:, :], gT[:, c * 64:(c + 1) * 64],
                                                    pw_[:, c * 64:(c + 1) * 64], ALU.mult))
                        vector.wait_ge(vch.h, vch.v)
                        vch.ti(vector.reduce_sum(kT[:, c * L + l: c * L + l + 1], tscr[:, :], axis=AX.X))
                        vector.wait_ge(vch.h, vch.v)
                    if l < L - 1:
                        for c in range(NDC):
                            vch.ti(vector.tensor_tensor(pw_[:, c * 64:(c + 1) * 64],
                                                        pw_[:, c * 64:(c + 1) * 64],
                                                        abT[:, c * 64:(c + 1) * 64], ALU.mult))
                            vector.wait_ge(vch.h, vch.v)

        p.ln_stats_stream(x_in)
        p.ln_apply_to_fm(x_in, u_fm)

        with nc.Block(no_gpsimd_drain=True) as blk:
            @blk.vector
            def _(vector):
                for c in range(NDC):
                    vector.memset(acc_fm[:, c * T:(c + 1) * T], 0.0)
        with nc.Block(no_gpsimd_drain=True) as blk:
            @blk.vector
            def _(vector):
                vch = p.sem("vchain")
                for l in range(L):
                    sh = 31 - l
                    lo = max(0, -sh)
                    hi = min(T, T - sh)
                    for c in range(NDC):
                        vch.ti(vector.scalar_tensor_tensor(
                            out=acc_fm[:, c * T + lo: c * T + hi],
                            in0=u_fm[:, c * T + lo + sh: c * T + hi + sh],
                            scalar=kT[:, c * L + l: c * L + l + 1],
                            in1=acc_fm[:, c * T + lo: c * T + hi],
                            op0=ALU.mult, op1=ALU.add))
                        vector.wait_ge(vch.h, vch.v)
                for c in range(NDC):
                    vch.ti(vector.scalar_tensor_tensor(
                        out=acc_fm[:, c * T:(c + 1) * T],
                        in0=u_fm[:, c * T:(c + 1) * T],
                        scalar=dcol_sb[:, c:c + 1],
                        in1=acc_fm[:, c * T:(c + 1) * T],
                        op0=ALU.mult, op1=ALU.add))
                    vector.wait_ge(vch.h, vch.v)

        p.finish_stream(x_in, acc_fm, x_out)
    p.cur_ctx = None
    return p


def l1_attn_a_inmaps(x_batch, inp):
    base = dict(l1a_a_row=np.asarray(inp["L1_attn_a_A"], np.float32).reshape(1, 64),
                l1a_bT=np.ascontiguousarray(np.asarray(inp["L1_attn_a_B"], np.float32).T),
                l1a_cmat=np.ascontiguousarray(np.asarray(inp["L1_attn_a_C"], np.float32)),
                l1a_dcols=np.ascontiguousarray(np.asarray(inp["L1_attn_a_D"], np.float32).reshape(NDC, 128).T),
                l1a_ldt_row=np.asarray(inp["L1_attn_a_log_dt"], np.float32).reshape(1, D),
                c_ident_in=IDENT128)
    return [dict(base, x=np.ascontiguousarray(xb, np.float32)) for xb in x_batch]


DI = 2048
NDI = DI // 128


def declare_l1_attn_b(p):
    nc = p.nc
    return (
        p.dram_in("l1b_inpT", (D, 2 * DI)),
        p.dram_in("l1b_wconv", (128, NDI * 4)),
        p.dram_in("l1b_bconv", (128, NDI)),
        p.dram_in("l1b_w264", (DI, 264)),
        p.dram_in("l1b_wdtrow", (1, DI)),
        p.dram_in("l1b_bdtcols", (128, NDI)),
        p.dram_in("l1b_alog", (1, 128)),
        p.dram_in("l1b_nwcols", (128, NDI)),
        p.dram_in("l1b_outT", (DI, D)),
        nc.dram_tensor("l1b_xin_scr", [128, NDI * T], BF16, kind="Internal").ap(),
        nc.dram_tensor("l1b_z_scr", [128, NDI * T], BF16, kind="Internal").ap(),
        nc.dram_tensor("l1b_xc_scr", [128, NDI * T], BF16, kind="Internal").ap(),
        nc.dram_tensor("l1b_bc_scr", [256, T], BF16, kind="Internal").ap(),
        nc.dram_tensor("l1b_rs_scr", [1, T], F32, kind="Internal").ap(),
        nc.dram_tensor("l1b_yh_scr", [128, NDI * T], F32, kind="Internal").ap(),
    )


def build_l1_attn_b(p=None, xio=None, aps=None):
    standalone = p is None
    if standalone:
        p = P("l1attnb")
    nc = p.nc
    if aps is None:
        aps = declare_l1_attn_b(p)
    (inpT, wconv, bconv, w264, wdtrow, bdtcols, alog, nwcols, outT, xind, zd, xcd, bcd, rsd, yhd) = aps
    x_in = xio[0] if xio else p.dram_in("x", (T, D))
    x_out = xio[1] if xio else p.dram_out("y", (T, D))
    F32EPS = 1.1920929e-07

    cm = p.ctx if standalone else ExitStack()
    with cm:
        if standalone:
            p.common(max_nk=16, max_bias=128)
        p.cur_ctx = cm
        p.begin_pool()
        dtr_row = p.sbuf("dtr_row", (1, T), F32)
        dtr_bf = p.sbuf("dtr_bf", (1, T), BF16)
        wdt_bf = p.sbuf("wdt_bf", (1, DI), BF16)
        a_bc = p.sbuf("a_bc", (128, 128), F32)
        bdt_sb = p.sbuf("bdt_sb", (128, NDI), F32)
        bcv_sb = p.sbuf("bcv_sb", (128, NDI), F32)
        wcv_sb = p.sbuf("wcv_sb", (128, NDI * 4), F32)
        nw_sb = p.sbuf("nw_sb", (128, NDI), F32)
        sums = p.sbuf("sums", (1, T), F32)
        onesr = p.sbuf("onesr2", (1, 128), F32)
        onec = p.sbuf("onec2", (128, 1), F32)
        sld = p.sem("sld")

        with nc.Block(no_gpsimd_drain=True) as blk:
            @blk.sync
            def _(sync):
                sld.ti(sync.dma_start(out=bdt_sb[:, :], in_=bdtcols[:, :]), 16)
                sld.ti(sync.dma_start(out=bcv_sb[:, :], in_=bconv[:, :]), 16)
                sld.ti(sync.dma_start(out=wcv_sb[:, :], in_=wconv[:, :]), 16)
                sld.ti(sync.dma_start(out=nw_sb[:, :], in_=nwcols[:, :]), 16)
                sld.ti(sync.dma_start(out=p.junk[0:1, 0:128], in_=alog[:, :]), 16)
                sync.wait_ge(sld.h, sld.v)

            @blk.vector
            def _(vector):
                vector.memset(onesr[:], 1.0)
                vector.memset(onec[:], 1.0)
        with nc.Block(no_gpsimd_drain=True) as blk:
            @blk.scalar
            def _(scalar):
                scalar.activation(out=p.junk[0:1, 128:256], in_=p.junk[0:1, 0:128], func=AF.Exp)
        with nc.Block(no_gpsimd_drain=True) as blk:
            @blk.vector
            def _(vector):
                vector.tensor_scalar_mul(p.junk[0:1, 128:256], p.junk[0:1, 128:256], -1.0)
        with ExitStack() as actx:
            pab = p.psum("pab2", (128, 128), F32, ctx=actx)
            with nc.Block(no_gpsimd_drain=True) as blk:
                @blk.tensor
                def _(tensor):
                    tensor.matmul(pab[:, :], onesr[0:1, :], p.junk[0:1, 128:256],
                                  start=True, stop=True)
            with nc.Block(no_gpsimd_drain=True) as blk:
                @blk.vector
                def _(vector):
                    vector.tensor_copy(a_bc[:, :], pab[:, :])

        with ExitStack() as s1:
            y_fm = p.sbuf("y_fm", (128, NDC * T), BF16, ctx=s1)
            p.ln_stats_stream(x_in)
            p.ln_apply_to_fm(x_in, y_fm)
            z_bf = p.sbuf("z_bf", (128, NDI * T // 2), BF16, ctx=s1)
            xin_bf = p.sbuf("xin_bf", (128, NDI * T // 2), BF16, ctx=s1)

            def mk_evac(dst):
                def ev(ofb, pstiles, eng):
                    sc = eng["scalar"]
                    last = None
                    for tb in range(NTB):
                        last = sc.copy(dst[:, ofb * T + tb * 512: ofb * T + (tb + 1) * 512],
                                       pstiles[tb][:, :])
                    return last
                return ev
            for half in range(2):
                p.proj_fm(inpT[:, half * 1024:(half + 1) * 1024], D, 1024, y_fm,
                          mk_evac(xin_bf))
                with nc.Block(no_gpsimd_drain=True) as blk:
                    @blk.sync
                    def _(sync):
                        for c in range(8):
                            sld.ti(sync.dma_start(out=xind[:, (half * 8 + c) * T:(half * 8 + c + 1) * T],
                                                  in_=xin_bf[:, c * T:(c + 1) * T]), 16)
                        sync.wait_ge(sld.h, sld.v)
                p.proj_fm(inpT[:, DI + half * 1024: DI + (half + 1) * 1024], D, 1024, y_fm,
                          mk_evac(z_bf))
                with nc.Block(no_gpsimd_drain=True) as blk:
                    @blk.sync
                    def _(sync):
                        for c in range(8):
                            sld.ti(sync.dma_start(out=zd[:, (half * 8 + c) * T:(half * 8 + c + 1) * T],
                                                  in_=z_bf[:, c * T:(c + 1) * T]), 16)
                        sync.wait_ge(sld.h, sld.v)

        with ExitStack() as s2:
            xc_bf = p.sbuf("xc_bf", (128, NDI * T), BF16, ctx=s2)
            cvin = [p.sbuf("cvin", (128, T), BF16, ctx=s2) for _ in range(2)]
            cvac = [p.sbuf("cvac", (128, T), F32, ctx=s2) for _ in range(2)]
            s_cd = [p.sem("cv_d%d" % j) for j in range(2)]
            s_cv = p.sem("cv_v")
            s_cs = p.sem("cv_s")
            bcd_ = [x.v for x in s_cd]
            bcv, bcs = s_cv.v, s_cs.v
            with nc.Block(no_gpsimd_drain=True) as blk:
                @blk.sync
                def _(sync):
                    for c in range(NDI):
                        if c >= 2:
                            sync.wait_ge(s_cv.h, bcv + c - 1)
                        s_cd[c % 2].ti(sync.dma_start(out=cvin[c % 2][:, :],
                                                      in_=xind[:, c * T:(c + 1) * T]), 16)

                @blk.vector
                def _(vector):
                    vch = p.sem("vchain")
                    for c in range(NDI):
                        vector.wait_ge(s_cd[c % 2].h, bcd_[c % 2] + (c // 2 + 1) * 16)
                        if c >= 2:
                            vector.wait_ge(s_cs.h, bcs + c - 1)
                        vch.ti(vector.tensor_scalar(cvac[c % 2][:, :], cvin[c % 2][:, :],
                                                    wcv_sb[:, c * 4 + 3:c * 4 + 4], None, ALU.mult))
                        vector.wait_ge(vch.h, vch.v)
                        for j in range(3):
                            sh = 3 - j
                            op = vector.scalar_tensor_tensor(
                                out=cvac[c % 2][:, sh:T],
                                in0=cvin[c % 2][:, 0:T - sh],
                                scalar=wcv_sb[:, c * 4 + j:c * 4 + j + 1],
                                in1=cvac[c % 2][:, sh:T],
                                op0=ALU.mult, op1=ALU.add)
                            if j < 2:
                                vch.ti(op)
                                vector.wait_ge(vch.h, vch.v)
                            else:
                                s_cv.ti(op)

                @blk.scalar
                def _(scalar):
                    for c in range(NDI):
                        scalar.wait_ge(s_cv.h, bcv + c + 1)
                        s_cs.ti(scalar.activation(out=xc_bf[:, c * T:(c + 1) * T],
                                                  in_=cvac[c % 2][:, :], func=AF.Silu,
                                                  bias=bcv_sb[:, c:c + 1]))

            xdb = p.sbuf("xdb", (128, NTT * 264), F32, ctx=s2)
            p.proj_tm(w264, DI, 264, xc_bf, xdb, x_w=T)
            with ExitStack() as s3:
                ptf = [p.psum("ptf", (128, 128), F32, ctx=s3) for _ in range(2)]
                bfm = p.sbuf("bfm", (128, T), BF16, ctx=s3)
                cfm = p.sbuf("cfm", (128, T), BF16, ctx=s3)
                s_t3 = p.sem("s_t3")
                s_v3 = p.sem("s_v3")
                bt3, bv3 = s_t3.v, s_v3.v
                items = []
                for i in range(NTT):
                    items.append((xdb[:, i * 264 + 8:i * 264 + 136], bfm[:, i * 128:(i + 1) * 128], None))
                    items.append((xdb[:, i * 264 + 136:i * 264 + 264], cfm[:, i * 128:(i + 1) * 128], None))
                    items.append((xdb[:, i * 264 + 0:i * 264 + 128], dtr_row[0:1, i * 128:(i + 1) * 128], "row0"))
                with nc.Block(no_gpsimd_drain=True) as blk:
                    @blk.tensor
                    def _(tensor):
                        for k, (src, dst, mode) in enumerate(items):
                            if k >= 2:
                                tensor.wait_ge(s_v3.h, bv3 + k - 1)
                            s_t3.ti(tensor.transpose(ptf[k % 2][:, :], src, p.identf[:]))

                    @blk.vector
                    def _(vector):
                        for k, (src, dst, mode) in enumerate(items):
                            vector.wait_ge(s_t3.h, bt3 + k + 1)
                            if mode == "row0":
                                s_v3.ti(vector.tensor_copy(dst, ptf[k % 2][0:1, :]))
                            else:
                                s_v3.ti(vector.tensor_copy(dst, ptf[k % 2][:, :]))
                with nc.Block(no_gpsimd_drain=True) as blk:
                    @blk.sync
                    def _(sync):
                        sld.ti(sync.dma_start(out=bcd[0:128, :], in_=bfm[:, :]), 16)
                        sld.ti(sync.dma_start(out=bcd[128:256, :], in_=cfm[:, :]), 16)
                        sync.wait_ge(sld.h, sld.v)

                    @blk.vector
                    def _(vector):
                        vector.tensor_copy(dtr_bf[:, :], dtr_row[:, :])
            with nc.Block(no_gpsimd_drain=True) as blk:
                @blk.sync
                def _(sync):
                    for c in range(NDI):
                        sld.ti(sync.dma_start(out=xcd[:, c * T:(c + 1) * T],
                                              in_=xc_bf[:, c * T:(c + 1) * T]), 16)
                    sync.wait_ge(sld.h, sld.v)
        with ExitStack() as swd:
            wdt_f = p.sbuf("wdt_f", (1, DI), F32, ctx=swd)
            with nc.Block(no_gpsimd_drain=True) as blk:
                @blk.sync
                def _(sync):
                    sld.ti(sync.dma_start(out=wdt_f[0:1, :], in_=wdtrow[:, :]), 16)
                    sync.wait_ge(sld.h, sld.v)
            with nc.Block(no_gpsimd_drain=True) as blk:
                @blk.vector
                def _(vector):
                    vector.tensor_copy(wdt_bf[0:1, :], wdt_f[0:1, :])

        ln1c = p.const_col(1.0)
        HT = T // 2
        for dh in range(2):
            c0 = dh * 8
            with ExitStack() as sh:
                h_bound = p.sbuf("h_bound", (128, 8 * 128), F32, ctx=sh)
                for th in range(2):
                    t0 = th * HT
                    with ExitStack() as sq:
                        dt_bf = p.sbuf("dt_bf", (128, 8 * HT), BF16, ctx=sq)
                        dtu_bf = p.sbuf("dtu_bf", (128, 8 * HT), BF16, ctx=sq)
                        yh = p.sbuf("yh", (128, 8 * HT), F32, ctx=sq)
                        bbc = [p.sbuf("bbc", (128, HT), BF16, ctx=sq) for _ in range(2)]
                        cbc = [p.sbuf("cbc", (128, HT), BF16, ctx=sq) for _ in range(2)]
                        dar = [p.sbuf("dar", (128, HT), F32, ctx=sq) for _ in range(2)]
                        dbr = [p.sbuf("dbr", (128, HT), BF16, ctx=sq) for _ in range(2)]
                        hsr = [p.sbuf("hsr", (128, HT), F32, ctx=sq) for _ in range(2)]
                        t1r = [p.sbuf("t1r", (128, HT), BF16, ctx=sq) for _ in range(2)]
                        spa = p.sbuf("spa", (128, 512), F32, ctx=sq)
                        spb = p.sbuf("spb", (128, 512), F32, ctx=sq)
                        xcl = [p.sbuf("xcl", (128, HT), BF16, ctx=sq) for _ in range(2)]

                        with ExitStack() as sdt:
                            pdt = [p.psum("pdt", (128, 512), F32, ctx=sdt) for _ in range(2)]
                            s_m4, s_x4 = p.sem("dt_m"), p.sem("dt_x")
                            s_vd = p.sem("dt_vd")
                            bm4, bx4, bvd = s_m4.v, s_x4.v, s_vd.v
                            units = [(c, tb) for c in range(8) for tb in range(HT // 512)]
                            with nc.Block(no_gpsimd_drain=True) as blk:
                                @blk.tensor
                                def _(tensor):
                                    for k, (c, tb) in enumerate(units):
                                        if k >= 2:
                                            tensor.wait_ge(s_x4.h, bx4 + k - 1)
                                        s_m4.ti(tensor.matmul(
                                            pdt[k % 2][:, :],
                                            wdt_bf[0:1, (c0 + c) * 128:(c0 + c + 1) * 128],
                                            dtr_bf[0:1, t0 + tb * 512: t0 + (tb + 1) * 512],
                                            start=True, stop=True))

                                @blk.scalar
                                def _(scalar):
                                    sch = p.sem("schain")
                                    for k, (c, tb) in enumerate(units):
                                        scalar.wait_ge(s_m4.h, bm4 + k + 1)
                                        if k >= 1:
                                            scalar.wait_ge(s_vd.h, bvd + k)
                                        bcol = bdt_sb[:, c0 + c:c0 + c + 1]
                                        sch.ti(scalar.activation(out=spa[:, :], in_=pdt[k % 2][:, :],
                                                                 func=AF.Abs, bias=bcol))
                                        scalar.wait_ge(sch.h, sch.v)
                                        sch.ti(scalar.activation(out=spa[:, :], in_=spa[:, :],
                                                                 func=AF.Exp, scale=-1.0))
                                        scalar.wait_ge(sch.h, sch.v)
                                        sch.ti(scalar.activation(out=spa[:, :], in_=spa[:, :],
                                                                 func=AF.Ln, bias=ln1c))
                                        scalar.wait_ge(sch.h, sch.v)
                                        s_x4.ti(scalar.activation(out=spb[:, :], in_=pdt[k % 2][:, :],
                                                                  func=AF.Relu, bias=bcol))

                                @blk.vector
                                def _(vector):
                                    for k, (c, tb) in enumerate(units):
                                        vector.wait_ge(s_x4.h, bx4 + k + 1)
                                        s_vd.ti(vector.tensor_tensor(
                                            dt_bf[:, c * HT + tb * 512: c * HT + (tb + 1) * 512],
                                            spa[:, :], spb[:, :], ALU.add))
                        with ExitStack() as sdu:
                            s_d5 = [p.sem("du_d%d" % j) for j in range(2)]
                            s_v5 = p.sem("du_v")
                            bd5 = [x.v for x in s_d5]
                            bv5 = s_v5.v
                            with nc.Block(no_gpsimd_drain=True) as blk:
                                @blk.sync
                                def _(sync):
                                    for c in range(8):
                                        if c >= 2:
                                            sync.wait_ge(s_v5.h, bv5 + c - 1)
                                        s_d5[c % 2].ti(sync.dma_start(out=xcl[c % 2][:, :],
                                                       in_=xcd[:, (c0 + c) * T + t0:(c0 + c) * T + t0 + HT]), 16)

                                @blk.vector
                                def _(vector):
                                    for c in range(8):
                                        vector.wait_ge(s_d5[c % 2].h, bd5[c % 2] + (c // 2 + 1) * 16)
                                        s_v5.ti(vector.tensor_tensor(dtu_bf[:, c * HT:(c + 1) * HT],
                                                                     dt_bf[:, c * HT:(c + 1) * HT],
                                                                     xcl[c % 2][:, :], ALU.mult))
                        hbt = p.sbuf("hbt", (128, 1), F32, ctx=sq)
                        s_bc = [p.sem("sc_bc%d" % j) for j in range(2)]
                        s_da = p.sem("sc_da")
                        s_db = p.sem("sc_db")
                        s_hs = p.sem("sc_hs")
                        s_t1 = p.sem("sc_t1")
                        s_ya = p.sem("sc_ya")
                        bbc_ = [x.v for x in s_bc]
                        bda, bdb, bhs, bt1, bya = s_da.v, s_db.v, s_hs.v, s_t1.v, s_ya.v
                        with nc.Block() as blk:
                            @blk.sync
                            def _(sync):
                                for si in range(128):
                                    if si >= 2:
                                        sync.wait_ge(s_db.h, bdb + 8 * (si - 1))
                                        sync.wait_ge(s_t1.h, bt1 + 8 * (si - 1))
                                    src_b = bass.AP(bcd.tensor, bcd.offset + si * T + t0, [[0, 128], [1, HT]])
                                    src_c = bass.AP(bcd.tensor, bcd.offset + (128 + si) * T + t0, [[0, 128], [1, HT]])
                                    s_bc[si % 2].ti(sync.dma_start(out=bbc[si % 2][:, :], in_=src_b), 16)
                                    s_bc[si % 2].ti(sync.dma_start(out=cbc[si % 2][:, :], in_=src_c), 16)

                            @blk.scalar
                            def _(scalar):
                                for si in range(128):
                                    for c in range(8):
                                        k = si * 8 + c
                                        if k >= 2:
                                            scalar.wait_ge(s_hs.h, bhs + k - 1)
                                        s_da.ti(scalar.activation(out=dar[k % 2][:, :],
                                                                  in_=dt_bf[:, c * HT:(c + 1) * HT],
                                                                  func=AF.Exp,
                                                                  scale=a_bc[:, si:si + 1]))

                            @blk.gpsimd
                            def _(gpsimd):
                                for si in range(128):
                                    gpsimd.wait_ge(s_bc[si % 2].h, bbc_[si % 2] + (si // 2 + 1) * 32)
                                    for c in range(8):
                                        k = si * 8 + c
                                        if k >= 2:
                                            gpsimd.wait_ge(s_hs.h, bhs + k - 1)
                                        s_db.ti(gpsimd.tensor_tensor(dbr[k % 2][:, :],
                                                                     dtu_bf[:, c * HT:(c + 1) * HT],
                                                                     bbc[si % 2][:, :], ALU.mult))
                                        gpsimd.wait_ge(s_db.h, s_db.v)
                                        gpsimd.wait_ge(s_t1.h, bt1 + k + 1)
                                        if si == 0:
                                            s_ya.ti(gpsimd.tensor_copy(yh[:, c * HT:(c + 1) * HT],
                                                                       t1r[k % 2][:, :]))
                                        else:
                                            s_ya.ti(gpsimd.tensor_tensor(yh[:, c * HT:(c + 1) * HT],
                                                                         yh[:, c * HT:(c + 1) * HT],
                                                                         t1r[k % 2][:, :], ALU.add))
                                        gpsimd.wait_ge(s_ya.h, s_ya.v)

                            @blk.vector
                            def _(vector):
                                for si in range(128):
                                    vector.wait_ge(s_bc[si % 2].h, bbc_[si % 2] + (si // 2 + 1) * 32)
                                    for c in range(8):
                                        k = si * 8 + c
                                        vector.wait_ge(s_da.h, bda + k + 1)
                                        vector.wait_ge(s_db.h, bdb + k + 1)
                                        if k >= 2:
                                            vector.wait_ge(s_ya.h, bya + k - 1)
                                        init = 0.0 if th == 0 else h_bound[:, c * 128 + si:c * 128 + si + 1]
                                        s_hs.ti(vector.tensor_tensor_scan(
                                            out=hsr[k % 2][:, :], data0=dar[k % 2][:, :],
                                            data1=dbr[k % 2][:, :], initial=init,
                                            op0=ALU.mult, op1=ALU.add))
                                        vector.wait_ge(s_hs.h, s_hs.v)
                                        if th == 0:
                                            vector.tensor_copy(h_bound[:, c * 128 + si:c * 128 + si + 1],
                                                               hsr[k % 2][:, HT - 1:HT])
                                        s_t1.ti(vector.tensor_tensor(t1r[k % 2][:, :],
                                                                     hsr[k % 2][:, :],
                                                                     cbc[si % 2][:, :], ALU.mult))
                                        vector.wait_ge(s_t1.h, s_t1.v)

                        with ExitStack() as sy:
                            y2 = p.sbuf("y2", (128, HT), F32, ctx=sy)
                            psm = p.psum("psm", (1, HT), F32, ctx=sy)
                            s_q = p.sem("yq")
                            s_qm = p.sem("yqm")
                            bq, bqm = s_q.v, s_qm.v
                            with nc.Block(no_gpsimd_drain=True) as blk:
                                @blk.scalar
                                def _(scalar):
                                    for c in range(8):
                                        if c >= 1:
                                            scalar.wait_ge(s_qm.h, bqm + c)
                                        s_q.ti(scalar.activation(out=y2[:, :], in_=yh[:, c * HT:(c + 1) * HT],
                                                                 func=AF.Square))

                                @blk.tensor
                                def _(tensor):
                                    for c in range(8):
                                        tensor.wait_ge(s_q.h, bq + c + 1)
                                        for tb in range(HT // 512):
                                            mm = tensor.matmul(psm[:, tb * 512:(tb + 1) * 512],
                                                               onec[:, :],
                                                               y2[:, tb * 512:(tb + 1) * 512],
                                                               start=(c == 0), stop=(c == 7))
                                        s_qm.ti(mm)
                            with nc.Block(no_gpsimd_drain=True) as blk:
                                @blk.vector
                                def _(vector):
                                    if dh == 0:
                                        vector.tensor_copy(sums[0:1, t0:t0 + HT], psm[0:1, :])
                                    else:
                                        vector.tensor_tensor(sums[0:1, t0:t0 + HT],
                                                             sums[0:1, t0:t0 + HT],
                                                             psm[0:1, :], ALU.add)
                        with nc.Block(no_gpsimd_drain=True) as blk:
                            @blk.sync
                            def _(sync):
                                for c in range(8):
                                    sld.ti(sync.dma_start(
                                        out=yhd[:, (c0 + c) * T + t0:(c0 + c) * T + t0 + HT],
                                        in_=yh[:, c * HT:(c + 1) * HT]), 16)
                                sync.wait_ge(sld.h, sld.v)

        epsc2 = p.const_col(F32EPS)
        with nc.Block(no_gpsimd_drain=True) as blk:
            @blk.vector
            def _(vector):
                vector.tensor_scalar_mul(sums[0:1, :], sums[0:1, :], 1.0 / DI)
        with nc.Block(no_gpsimd_drain=True) as blk:
            @blk.scalar
            def _(scalar):
                scalar.activation(out=sums[0:1, :], in_=sums[0:1, :], func=AF.Sqrt,
                                  bias=epsc2[0:1, :])
        with nc.Block(no_gpsimd_drain=True) as blk:
            @blk.vector
            def _(vector):
                vector.reciprocal(sums[0:1, :], sums[0:1, :])
        with nc.Block(no_gpsimd_drain=True) as blk:
            @blk.sync
            def _(sync):
                sld.ti(sync.dma_start(out=rsd[:, :], in_=sums[0:1, :]), 16)
                sync.wait_ge(sld.h, sld.v)

        with ExitStack() as sg:
            acc_fm = p.sbuf("acc_fm", (128, NDC * T), F32, ctx=sg)
            for gh in range(2):
                g0 = gh * 8
                with ExitStack() as sg2:
                    y_bf = p.sbuf("y_bf", (128, 8 * T), BF16, ctx=sg2)
                    rbc = p.sbuf("rbc", (128, T), F32, ctx=sg2)
                    yhl = [p.sbuf("yhl", (128, T), F32, ctx=sg2) for _ in range(2)]
                    zl = [p.sbuf("zl", (128, T), BF16, ctx=sg2) for _ in range(2)]
                    zs = [p.sbuf("zs", (128, T), BF16, ctx=sg2) for _ in range(2)]
                    t2 = p.sbuf("t2", (128, T), BF16, ctx=sg2)
                    s_gd = [p.sem("g_d%d" % j) for j in range(3)]
                    s_gs = p.sem("g_s")
                    s_gv = p.sem("g_v")
                    bgd = [x.v for x in s_gd]
                    bgs, bgv = s_gs.v, s_gv.v
                    with nc.Block(no_gpsimd_drain=True) as blk:
                        @blk.sync
                        def _(sync):
                            src_r = bass.AP(rsd.tensor, rsd.offset, [[0, 128], [1, T]])
                            s_gd[2].ti(sync.dma_start(out=rbc[:, :], in_=src_r), 16)
                            for c8 in range(8):
                                c = g0 + c8
                                if c8 >= 2:
                                    sync.wait_ge(s_gv.h, bgv + 3 * (c8 - 1))
                                s_gd[c8 % 2].ti(sync.dma_start(out=yhl[c8 % 2][:, :],
                                                in_=yhd[:, c * T:(c + 1) * T]), 16)
                                s_gd[c8 % 2].ti(sync.dma_start(out=zl[c8 % 2][:, :],
                                                in_=zd[:, c * T:(c + 1) * T]), 16)

                        @blk.scalar
                        def _(scalar):
                            for c8 in range(8):
                                scalar.wait_ge(s_gd[c8 % 2].h, bgd[c8 % 2] + (c8 // 2 + 1) * 32)
                                if c8 >= 2:
                                    scalar.wait_ge(s_gv.h, bgv + 3 * (c8 - 1))
                                s_gs.ti(scalar.activation(out=zs[c8 % 2][:, :], in_=zl[c8 % 2][:, :],
                                                          func=AF.Silu))

                        @blk.vector
                        def _(vector):
                            vch = p.sem("vchain")
                            vector.wait_ge(s_gd[2].h, bgd[2] + 16)
                            for c8 in range(8):
                                c = g0 + c8
                                vector.wait_ge(s_gd[c8 % 2].h, bgd[c8 % 2] + (c8 // 2 + 1) * 32)
                                vector.wait_ge(s_gs.h, bgs + c8 + 1)
                                vch.ti(vector.tensor_tensor(t2[:, :], yhl[c8 % 2][:, :],
                                                            rbc[:, :], ALU.mult))
                                vector.wait_ge(vch.h, vch.v)
                                vch.ti(vector.tensor_scalar(t2[:, :], t2[:, :],
                                                            nw_sb[:, c:c + 1], None, ALU.mult))
                                vector.wait_ge(vch.h, vch.v)
                                s_gv.ti(vector.tensor_tensor(y_bf[:, c8 * T:(c8 + 1) * T],
                                                             t2[:, :], zs[c8 % 2][:, :], ALU.mult), 3)
                                vector.wait_ge(s_gv.h, s_gv.v)

                    def evac_ob(ofb, pstiles, eng, s_ev=None, gh=gh):
                        vec = eng["vector"]
                        last = None
                        for tb in range(NTB):
                            sl = acc_fm[:, ofb * T + tb * 512: ofb * T + (tb + 1) * 512]
                            if gh == 0:
                                last = vec.tensor_copy(sl, pstiles[tb][:, :])
                            else:
                                last = vec.tensor_tensor(sl, sl, pstiles[tb][:, :], ALU.add)
                        return last
                    p.proj_fm(outT[g0 * 128:(g0 + 8) * 128, :], 1024, D, y_bf, evac_ob,
                              evac_on="vector")
            p.finish_stream(x_in, acc_fm, x_out)
    p.cur_ctx = None
    return p




def l1_attn_b_inmaps(x_batch, inp):
    f = np.float32
    cw = np.asarray(inp["L1_attn_b_conv_w"], f)[:, 0, :]
    wconv = np.ascontiguousarray(cw.reshape(NDI, 128, 4).transpose(1, 0, 2).reshape(128, NDI * 4))
    xpT = np.ascontiguousarray(np.asarray(inp["L1_attn_b_x_proj_w"], f).T)
    w264 = np.zeros((DI, 264), f)
    w264[:, 0] = xpT[:, 0]
    w264[:, 8:136] = xpT[:, 1:129]
    w264[:, 136:264] = xpT[:, 129:257]
    base = dict(
        l1b_inpT=np.ascontiguousarray(np.asarray(inp["L1_attn_b_in_proj_w"], f).T),
        l1b_wconv=wconv,
        l1b_bconv=np.ascontiguousarray(np.asarray(inp["L1_attn_b_conv_b"], f).reshape(NDI, 128).T),
        l1b_w264=w264,
        l1b_wdtrow=np.asarray(inp["L1_attn_b_dt_proj_w"], f).reshape(1, DI),
        l1b_bdtcols=np.ascontiguousarray(np.asarray(inp["L1_attn_b_dt_proj_b"], f).reshape(NDI, 128).T),
        l1b_alog=np.asarray(inp["L1_attn_b_A_log"], f).reshape(1, 128),
        l1b_nwcols=np.ascontiguousarray(np.asarray(inp["L1_attn_b_norm_w"], f).reshape(NDI, 128).T),
        l1b_outT=np.ascontiguousarray(np.asarray(inp["L1_attn_b_out_proj_w"], f).T),
        c_ident_in=IDENT128)
    return [dict(base, x=np.ascontiguousarray(xb, f)) for xb in x_batch]


def declare_l0_attn_a(p):
    nc = p.nc
    return (
        p.dram_in("l0a_wqkvT", (6, D, 3 * D)),
        p.dram_in("l0a_bqkv", (6, 3 * D)),
        p.dram_in("l0a_woT", (6, D, D)),
        p.dram_in("l0a_bo", (6, D)),
        p.dram_in("l0a_w1T", (6, D, 4 * D)),
        p.dram_in("l0a_b1r", (6, 4 * D)),
        p.dram_in("l0a_w2T", (6, 4 * D, D)),
        p.dram_in("l0a_b2r", (6, D)),
        p.dram_in("l0a_ln1g", (6, D)),
        p.dram_in("l0a_ln1b", (6, D)),
        p.dram_in("l0a_ln2g", (6, D)),
        p.dram_in("l0a_ln2b", (6, D)),
        p.dram_in("l0a_ngrow", (1, D)),
        p.dram_in("l0a_nbrow", (1, D)),
        p.dram_in("l0a_posd", (T, D)),
        nc.dram_tensor("l0a_maskb", [128, 1152], mybir.dt.uint8, kind="ExternalInput").ap(),
        nc.dram_tensor("l0a_pmask7", [128, 1], mybir.dt.uint8, kind="ExternalInput").ap(),
        nc.dram_tensor("l0a_xa_scr", [T, D], F32, kind="Internal").ap(),
        nc.dram_tensor("l0a_tmp_scr", [T, D], F32, kind="Internal").ap(),
    )


def build_l0_attn_a(p=None, xio=None, aps=None):
    standalone = p is None
    if standalone:
        p = P("l0attna")
    nc = p.nc
    if aps is None:
        aps = declare_l0_attn_a(p)
    (wqkvT, bqkv, woT, bo, w1T, b1r, w2T, b2r, ln1g, ln1b, ln2g, ln2b, ngrow, nbrow, posd, maskb_in, pmask_in, xad, tmpd) = aps
    x_in = xio[0] if xio else p.dram_in("x", (T, D))
    x_out = xio[1] if xio else p.dram_out("y", (T, D))
    cm = p.ctx if standalone else ExitStack()
    with cm:
        if standalone:
            p.common(max_nk=8, max_bias=1024)
        p.cur_ctx = cm
        p.begin_pool()
        maskb = p.sbuf("maskb", (128, 1152), mybir.dt.uint8)
        pmask = p.sbuf("pmask", (128, 1), mybir.dt.uint8)
        negC = p.sbuf("negC", (128, 1152), F32)
        g_bc = p.sbuf("g_bc", (128, D), F32)
        b_bc = p.sbuf("b_bc", (128, D), F32)
        sld = p.sem("sld")

        with nc.Block(no_gpsimd_drain=True) as blk:
            @blk.sync
            def _(sync):
                sld.ti(sync.dma_start(out=maskb[:, :], in_=maskb_in[:, :]), 16)
                sld.ti(sync.dma_start(out=pmask[:, :], in_=pmask_in[:, :]), 16)
                sync.wait_ge(sld.h, sld.v)

            @blk.vector
            def _(vector):
                vector.memset(negC[:], NEG)

        def bcast(dst, row_hbm):
            with nc.Block(no_gpsimd_drain=True) as blk:
                @blk.sync
                def _(sync):
                    src = bass.AP(row_hbm.tensor, row_hbm.offset, [[0, 128], [1, D]])
                    sld.ti(sync.dma_start(out=dst[:, :], in_=src), 16)
                    sync.wait_ge(sld.h, sld.v)

        def pass1(src_hbm, dst_hbm, acc_fm=None, acc_bf=False, aff=False):
            with ExitStack() as lctx:
                xb = [p.sbuf("p1x", (128, D), F32, ctx=lctx) for _ in range(3)]
                ob = [p.sbuf("p1o", (128, D), F32, ctx=lctx) for _ in range(2)]
                cbf = p.sbuf("p1c", (128, D), BF16, ctx=lctx) if (acc_fm is not None and not acc_bf) else None
                pt = [p.psum("p1t", (128, 128), BF16, ctx=lctx) for _ in range(8)] if acc_fm is not None else []
                s_d = [p.sem("p1_d%d" % j) for j in range(3)]
                s_a = p.sem("p1_a")
                s_cc = p.sem("p1_c")
                s_t = p.sem("p1_t")
                s_ad = p.sem("p1_ad")
                s_st = p.sem("p1_st")
                s_o = [p.sem("p1_o%d" % j) for j in range(2)]
                bd = [sd.v for sd in s_d]
                bo_ = [so.v for so in s_o]
                ba, bc2, bt, bad, bst = s_a.v, s_cc.v, s_t.v, s_ad.v, s_st.v
                with nc.Block() as blk:
                    @blk.sync
                    def _(sync):
                        for i in range(NTT):
                            if i >= 3:
                                sync.wait_ge(s_a.h, ba + i - 2)
                            s_d[i % 3].ti(sync.dma_start(out=xb[i % 3][:],
                                          in_=src_hbm[i * 128:(i + 1) * 128, :]), 16)

                    @blk.scalar
                    def _(scalar):
                        for i in range(NTT):
                            if acc_fm is not None and not acc_bf:
                                if i >= 1:
                                    scalar.wait_ge(s_t.h, bt + i * NDC)
                                last = None
                                for c in range(NDC):
                                    last = scalar.copy(cbf[:, c * 128:(c + 1) * 128],
                                                       acc_fm[:, c * T + i * 128: c * T + (i + 1) * 128])
                                s_cc.ti(last)
                            scalar.wait_ge(s_ad.h, bad + i + 1)
                            s_st.ti(scalar.activation(out=p.junk[:, 0:D], in_=ob[i % 2][:],
                                                      func=AF.Square, accum_out=p.s2[:, i:i + 1]))
                            scalar.wait_ge(s_st.h, s_st.v)

                    if acc_fm is not None:
                        @blk.tensor
                        def _(tensor):
                            for i in range(NTT):
                                if not acc_bf:
                                    tensor.wait_ge(s_cc.h, bc2 + i + 1)
                                if i >= 1:
                                    tensor.wait_ge(s_ad.h, bad + i)
                                for c in range(NDC):
                                    k = i * NDC + c
                                    src_t = (acc_fm[:, c * T + i * 128: c * T + (i + 1) * 128]
                                             if acc_bf else cbf[:, c * 128:(c + 1) * 128])
                                    s_t.ti(tensor.transpose(pt[k % 8][:, :], src_t, p.ident[:]))

                    @blk.vector
                    def _(vector):
                        for i in range(NTT):
                            vector.wait_ge(s_d[i % 3].h, bd[i % 3] + (i // 3 + 1) * 16)
                            if i >= 2:
                                vector.wait_ge(s_o[i % 2].h, bo_[i % 2] + (i // 2) * 16)
                                vector.wait_ge(s_st.h, bst + i - 1)
                            if aff:
                                v = vector.tensor_scalar(ob[i % 2][:], xb[i % 3][:],
                                                         p.sa[:, i:i + 1], p.sb[:, i:i + 1],
                                                         ALU.mult, ALU.add)
                            else:
                                v = vector.tensor_copy(ob[i % 2][:], xb[i % 3][:])
                            s_a.ti(v)
                            vector.wait_ge(s_a.h, s_a.v)
                            if acc_fm is not None:
                                vch = p.sem("vchain")
                                for c in range(NDC):
                                    k = i * NDC + c
                                    vector.wait_ge(s_t.h, bt + k + 1)
                                    op = vector.tensor_tensor(ob[i % 2][:, c * 128:(c + 1) * 128],
                                                              ob[i % 2][:, c * 128:(c + 1) * 128],
                                                              pt[k % 8][:, :], ALU.add)
                                    if c == NDC - 1:
                                        vch.ti(op)
                                        vector.wait_ge(vch.h, vch.v)
                            s_ad.ti(vector.reduce_sum(p.s1[:, i:i + 1], ob[i % 2][:], axis=AX.X))

                    @blk.gpsimd
                    def _(gpsimd):
                        for i in range(NTT):
                            gpsimd.wait_ge(s_ad.h, bad + i + 1)
                            gpsimd.wait_ge(s_st.h, bst + i + 1)
                            s_o[i % 2].ti(gpsimd.dma_start(out=dst_hbm[i * 128:(i + 1) * 128, :],
                                          in_=ob[i % 2][:]), 16)
                        gpsimd.wait_ge(s_o[0].h, s_o[0].v)
                        gpsimd.wait_ge(s_o[1].h, s_o[1].v)
            p.ln_combine()

        def pass2(src_hbm, xfm, g, b, pos=False, store=None):
            with ExitStack() as lctx:
                xb = [p.sbuf("p2x", (128, D), F32, ctx=lctx) for _ in range(3)]
                pb = [p.sbuf("p2p", (128, D), F32, ctx=lctx) for _ in range(2)] if pos else None
                yb = [p.sbuf("p2y", (128, D), F32, ctx=lctx) for _ in range(2)]
                cb = [p.sbuf("p2c", (128, D), BF16, ctx=lctx) for _ in range(2)]
                pt = [p.psum("p2t", (128, 128), BF16, ctx=lctx) for _ in range(4)]
                s_d = [p.sem("p2_d%d" % j) for j in range(3)]
                s_y = p.sem("p2_y")
                s_c = p.sem("p2_c")
                s_t = p.sem("p2_t")
                s_e = p.sem("p2_e")
                s_o = [p.sem("p2_o%d" % j) for j in range(2)]
                bd = [sd.v for sd in s_d]
                bo_ = [so.v for so in s_o]
                by, bcc, bt, be = s_y.v, s_c.v, s_t.v, s_e.v
                with nc.Block() as blk:
                    @blk.sync
                    def _(sync):
                        for i in range(NTT):
                            if pos and i >= 2:
                                sync.wait_ge(s_y.h, by + i - 1)
                            elif i >= 3:
                                sync.wait_ge(s_y.h, by + i - 2)
                            s_d[i % 3].ti(sync.dma_start(out=xb[i % 3][:],
                                          in_=src_hbm[i * 128:(i + 1) * 128, :]), 16)
                            if pos:
                                s_d[i % 3].ti(sync.dma_start(out=pb[i % 2][:],
                                              in_=posd[i * 128:(i + 1) * 128, :]), 16)

                    @blk.vector
                    def _(vector):
                        vch = p.sem("vchain")
                        nper = 32 if pos else 16

                        def mk(i):
                            vector.wait_ge(s_d[i % 3].h, bd[i % 3] + (i // 3 + 1) * nper)
                            if i >= 2:
                                vector.wait_ge(s_c.h, bcc + i - 1)
                                if store is not None:
                                    vector.wait_ge(s_o[i % 2].h, bo_[i % 2] + (i // 2) * 16)
                            vch.ti(vector.tensor_scalar(yb[i % 2][:], xb[i % 3][:],
                                                        p.sa[:, i:i + 1], p.sb[:, i:i + 1],
                                                        ALU.mult, ALU.add))
                            vector.wait_ge(vch.h, vch.v)
                            vch.ti(vector.tensor_tensor(yb[i % 2][:], yb[i % 2][:], g[:, :], ALU.mult))
                            vector.wait_ge(vch.h, vch.v)
                            op = vector.tensor_tensor(yb[i % 2][:], yb[i % 2][:], b[:, :], ALU.add)
                            if pos:
                                vch.ti(op)
                                vector.wait_ge(vch.h, vch.v)
                                op = vector.tensor_tensor(yb[i % 2][:], yb[i % 2][:],
                                                          pb[i % 2][:], ALU.add)
                            s_y.ti(op)
                            vector.wait_ge(s_y.h, s_y.v)

                        mk(0)
                        if NTT > 1:
                            mk(1)
                        for i in range(NTT):
                            for c in range(NDC):
                                k = i * NDC + c
                                vector.wait_ge(s_t.h, bt + k + 1)
                                s_e.ti(vector.tensor_copy(
                                    xfm[:, c * T + i * 128: c * T + (i + 1) * 128],
                                    pt[k % 4][:, :]))
                            if i + 2 < NTT:
                                mk(i + 2)

                    @blk.scalar
                    def _(scalar):
                        for i in range(NTT):
                            scalar.wait_ge(s_y.h, by + i + 1)
                            if i >= 2:
                                scalar.wait_ge(s_e.h, be + (i - 1) * NDC)
                            s_c.ti(scalar.copy(cb[i % 2][:], yb[i % 2][:]))

                    @blk.tensor
                    def _(tensor):
                        for i in range(NTT):
                            tensor.wait_ge(s_c.h, bcc + i + 1)
                            for c in range(NDC):
                                k = i * NDC + c
                                if k >= 4:
                                    tensor.wait_ge(s_e.h, be + k - 3)
                                s_t.ti(tensor.transpose(pt[k % 4][:, :],
                                                        cb[i % 2][:, c * 128:(c + 1) * 128],
                                                        p.ident[:]))

                    if store is not None:
                        @blk.gpsimd
                        def _(gpsimd):
                            for i in range(NTT):
                                gpsimd.wait_ge(s_y.h, by + i + 1)
                                s_o[i % 2].ti(gpsimd.dma_start(out=store[i * 128:(i + 1) * 128, :],
                                              in_=yb[i % 2][:]), 16)
                            gpsimd.wait_ge(s_o[0].h, s_o[0].v)
                            gpsimd.wait_ge(s_o[1].h, s_o[1].v)

        xfm = p.sbuf("xfm", (128, NDC * T), BF16)
        p.ln_stats_stream(x_in)
        pass1(x_in, tmpd, aff=True)
        bcast(g_bc, ngrow[0:1, :])
        bcast(b_bc, nbrow[0:1, :])
        pass2(tmpd, xfm, g_bc, b_bc, pos=True, store=xad)

        for l in range(6):
            with ExitStack() as la:
                q_fm = p.sbuf("q_fm", (128, NDC * T), BF16, ctx=la)
                o_fm = p.sbuf("o_fm", (128, NDC * T), BF16, ctx=la)

                def evac_q(ofb, pstiles, eng):
                    sc = eng["scalar"]
                    last = None
                    for tb in range(NTB):
                        last = sc.mul(q_fm[:, ofb * T + tb * 512: ofb * T + (tb + 1) * 512],
                                      pstiles[tb][:, :], 0.125)
                    return last
                p.proj_fm(wqkvT[l][:, 0:D], D, D, xfm, evac_q,
                          bias_hbm_row=bqkv[l:l + 1, 0:D])

                for hp in range(8):
                    with ExitStack() as lp:
                        k_bf = p.sbuf("k_bf", (128, T), BF16, ctx=lp)
                        v_bf = p.sbuf("v_bf", (128, T), BF16, ctx=lp)
                        v_tm = p.sbuf("v_tm", (128, NTT * 128), BF16, ctx=lp)
                        a_sb = p.sbuf("a_sb", (128, 1153), F32, ctx=lp)
                        a_bf = p.sbuf("a_bf", (128, 1280), BF16, ctx=lp)
                        at_sb = p.sbuf("at_sb", (128, 10 * 128), BF16, ctx=lp)
                        ob_bf = p.sbuf("ob_bf", (128, 64), BF16, ctx=lp)
                        mcol = p.sbuf("mcol", (128, 1), F32, ctx=lp)
                        scol = p.sbuf("scol", (128, 1), F32, ctx=lp)

                        def evac_to(dst):
                            def ev(ofb, pstiles, eng):
                                sc = eng["scalar"]
                                last = None
                                for tb in range(NTB):
                                    last = sc.copy(dst[:, tb * 512:(tb + 1) * 512], pstiles[tb][:, :])
                                return last
                            return ev
                        p.proj_fm(wqkvT[l][:, D + hp * 128: D + (hp + 1) * 128], D, 128,
                                  xfm, evac_to(k_bf),
                                  bias_hbm_row=bqkv[l:l + 1, D + hp * 128: D + (hp + 1) * 128])
                        p.proj_fm(wqkvT[l][:, 2 * D + hp * 128: 2 * D + (hp + 1) * 128], D, 128,
                                  xfm, evac_to(v_bf),
                                  bias_hbm_row=bqkv[l:l + 1, 2 * D + hp * 128: 2 * D + (hp + 1) * 128])
                        items = [(v_bf[:, i * 128:(i + 1) * 128], v_tm[:, i * 128:(i + 1) * 128])
                                 for i in range(NTT)]
                        p.transpose_bf(items)
                        bk_bf = p.sbuf("bk_bf", (128, 1), BF16, ctx=lp)
                        bv_row = p.sbuf("bv_row", (1, 128), BF16, ctx=lp)
                        bkf = p.sbuf("bkf", (128, 1), F32, ctx=lp)
                        bvf = p.sbuf("bvf", (1, 128), F32, ctx=lp)
                        with nc.Block(no_gpsimd_drain=True) as blk:
                            @blk.sync
                            def _(sync):
                                src = bass.AP(bqkv.tensor,
                                              bqkv.offset + l * 3 * D + D + hp * 128, [[1, 128], [1, 1]])
                                sld.ti(sync.dma_start(out=bkf[:, :], in_=src), 16)
                                sld.ti(sync.dma_start(out=bvf[:, :],
                                                      in_=bqkv[l:l + 1, 2 * D + hp * 128: 2 * D + (hp + 1) * 128]), 16)
                                sync.wait_ge(sld.h, sld.v)
                        with nc.Block(no_gpsimd_drain=True) as blk:
                            @blk.vector
                            def _(vector):
                                vector.tensor_copy(bk_bf[:, :], bkf[:, :])
                                vector.tensor_copy(bv_row[:, :], bvf[:, :])

                        for h2 in range(2):
                            hb = 64 * h2
                            h = hp * 2 + h2
                            ch = h // 2
                            with ExitStack() as aa:
                                ps_s = p.psum("ps_s", (128, 1153), F32, ctx=aa)
                                pt_a = [p.psum("pt_a", (128, 128), BF16, ctx=aa) for _ in range(2)]
                                ps_o = p.psum("ps_o", (128, 64), F32, ctx=aa)
                                pt_o = p.psum("pt_o", (64, 128), BF16, ctx=aa)
                                s_sc = p.sem("at_sc")
                                s_sel = p.sem("at_sel")
                                s_ex = p.sem("at_ex")
                                s_tp = p.sem("at_tp")
                                s_te = p.sem("at_te")
                                s_av = p.sem("at_av")
                                s_oe = p.sem("at_oe")
                                s_ot = p.sem("at_ot")
                                s_of = p.sem("at_of")
                                bsc, bsel, bex, btp, bte, bav = (s_sc.v, s_sel.v, s_ex.v,
                                                                 s_tp.v, s_te.v, s_av.v)
                                boe, bot, bof = s_oe.v, s_ot.v, s_of.v
                                s_rc = p.sem("at_rc")
                                brc = s_rc.v
                                qts = list(range(NTT))
                                meta = []
                                tp_before = []
                                tot = 0
                                for qt in qts:
                                    nkt = min(qt + 1, 9)
                                    W = nkt * 128
                                    haspast = qt <= 7
                                    meta.append((qt, nkt, W, haspast))
                                    tp_before.append(tot)
                                    tot += nkt + (1 if haspast else 0)
                                with nc.Block(no_gpsimd_drain=True) as blk:
                                    @blk.tensor
                                    def _(tensor):
                                        for mi, (qt, nkt, W, haspast) in enumerate(meta):
                                            ks = max(0, qt - 8) * 128
                                            q_sl = q_fm[hb:hb + 64,
                                                        ch * T + qt * 128: ch * T + (qt + 1) * 128]
                                            if mi >= 1:
                                                tensor.wait_ge(s_sel.h, bsel + mi)
                                            nmm = (W + 511) // 512
                                            for mmi in range(nmm):
                                                w0 = mmi * 512
                                                w1 = min(W, w0 + 512)
                                                mm = tensor.matmul(
                                                    ps_s[:, w0:w1],
                                                    q_sl,
                                                    k_bf[hb:hb + 64, ks + w0: ks + w1],
                                                    start=True, stop=True)
                                            if haspast:
                                                mm = tensor.matmul(ps_s[:, W:W + 1], q_sl,
                                                                   bk_bf[hb:hb + 64, :],
                                                                   start=True, stop=True)
                                            s_sc.ti(mm)
                                            nblk = nkt + (1 if haspast else 0)
                                            tensor.wait_ge(s_ex.h, bex + mi + 1)
                                            for bkt in range(nblk):
                                                kg = tp_before[mi] + bkt
                                                if kg >= 2:
                                                    tensor.wait_ge(s_te.h, bte + kg - 1)
                                                s_tp.ti(tensor.transpose(
                                                    pt_a[kg % 2][:, :],
                                                    a_bf[:, bkt * 128:(bkt + 1) * 128],
                                                    p.ident[:]))
                                            tensor.wait_ge(s_te.h, bte + tp_before[mi] + nblk)
                                            if mi >= 1:
                                                tensor.wait_ge(s_oe.h, boe + mi)
                                            for kt in range(nkt):
                                                mm = tensor.matmul(
                                                    ps_o[:, :],
                                                    at_sb[:, kt * 128:(kt + 1) * 128],
                                                    v_tm[:, (ks // 128 + kt) * 128:
                                                         (ks // 128 + kt + 1) * 128][:, hb:hb + 64],
                                                    start=(kt == 0), stop=(kt == nkt - 1 and not haspast))
                                            if haspast:
                                                mm = tensor.matmul(ps_o[:, :],
                                                                   at_sb[0:1, nkt * 128: nkt * 128 + 128],
                                                                   bv_row[0:1, hb:hb + 64],
                                                                   start=False, stop=True)
                                            s_av.ti(mm)
                                            tensor.wait_ge(s_oe.h, boe + mi + 1)
                                            if mi >= 1:
                                                tensor.wait_ge(s_of.h, bof + mi)
                                            s_ot.ti(tensor.transpose(pt_o[:, :], ob_bf[:, :],
                                                                     p.ident[:]))

                                    @blk.vector
                                    def _(vector):
                                        vch = p.sem("vchain")
                                        for mi, (qt, nkt, W, haspast) in enumerate(meta):
                                            Wp = W + (1 if haspast else 0)
                                            off = (1024 - qt * 128) if qt <= 7 else 0
                                            vector.wait_ge(s_sc.h, bsc + mi + 1)
                                            vch.ti(vector.select(a_sb[:, 0:W], maskb[:, off:off + W],
                                                                 ps_s[:, 0:W], negC[:, 0:W], add_drain=True))
                                            vector.wait_ge(vch.h, vch.v)
                                            if haspast:
                                                if qt == 7:
                                                    vch.ti(vector.select(a_sb[:, W:W + 1], pmask[:, :],
                                                                         ps_s[:, W:W + 1], negC[:, 0:1], add_drain=True))
                                                else:
                                                    vch.ti(vector.tensor_copy(a_sb[:, W:W + 1],
                                                                              ps_s[:, W:W + 1]))
                                                vector.wait_ge(vch.h, vch.v)
                                            if mi >= 1:
                                                vector.wait_ge(s_ex.h, bex + mi)
                                            s_sel.ti(vector.reduce_max(mcol[:, :], a_sb[:, 0:Wp],
                                                                       axis=AX.X, negate=True))
                                            vector.wait_ge(s_sel.h, s_sel.v)
                                            vector.wait_ge(s_ex.h, bex + mi + 1)
                                            s_rc.ti(vector.reciprocal(scol[:, :], p.s2[:, mi % 2:mi % 2 + 1]))
                                            vector.wait_ge(s_rc.h, s_rc.v)
                                            vector.wait_ge(s_av.h, bav + mi + 1)
                                            s_oe.ti(vector.tensor_scalar(ob_bf[:, :], ps_o[:, :],
                                                                         scol[:, 0:1], None, ALU.mult))
                                            vector.wait_ge(s_oe.h, s_oe.v)
                                            vector.wait_ge(s_ot.h, bot + mi + 1)
                                            s_of.ti(vector.tensor_copy(
                                                o_fm[hb:hb + 64, ch * T + qt * 128: ch * T + (qt + 1) * 128],
                                                pt_o[:, :]))

                                    @blk.scalar
                                    def _(scalar):
                                        for mi, (qt, nkt, W, haspast) in enumerate(meta):
                                            Wp = W + (1 if haspast else 0)
                                            scalar.wait_ge(s_sel.h, bsel + mi + 1)
                                            if mi >= 1:
                                                scalar.wait_ge(s_tp.h, btp + tp_before[mi])
                                            if mi >= 2:
                                                scalar.wait_ge(s_rc.h, brc + mi - 1)
                                            s_ex.ti(scalar.activation(out=a_bf[:, 0:Wp],
                                                                      in_=a_sb[:, 0:Wp],
                                                                      func=AF.Exp, bias=mcol[:, 0:1],
                                                                      accum_out=p.s2[:, mi % 2:mi % 2 + 1]))
                                            nblk = nkt + (1 if haspast else 0)
                                            if mi >= 1:
                                                scalar.wait_ge(s_av.h, bav + mi)
                                            for bkt in range(nblk):
                                                kg = tp_before[mi] + bkt
                                                scalar.wait_ge(s_tp.h, btp + kg + 1)
                                                s_te.ti(scalar.copy(
                                                    at_sb[:, bkt * 128:(bkt + 1) * 128],
                                                    pt_a[kg % 2][:, :]))

                with ExitStack() as lw:
                    wo_bf = p.sbuf("wo_bf", (128, NDC * T), BF16, ctx=lw)
                    p.proj_fm(woT[l], D, D, o_fm, _mk_evac_copy(p, wo_bf),
                              bias_hbm_row=bo[l:l + 1, :])
                    pass1(xad, tmpd, acc_fm=wo_bf, acc_bf=True)
                bcast(g_bc, ln1g[l:l + 1, :])
                bcast(b_bc, ln1b[l:l + 1, :])
                pass2(tmpd, xfm, g_bc, b_bc, store=xad)

            with ExitStack() as lf:
                ff_acc = p.sbuf("ff_acc", (128, NDC * T), F32, ctx=lf)
                h_q = p.sbuf("h_q", (128, NDC * T), BF16, ctx=lf)

                def evac_gelu_f(ofb, pstiles, eng):
                    sc = eng["scalar"]
                    last = None
                    for tb in range(NTB):
                        last = sc.activation(out=h_q[:, ofb * T + tb * 512: ofb * T + (tb + 1) * 512],
                                             in_=pstiles[tb][:, :], func=AF.Gelu)
                    return last
                for qr in range(4):
                    p.proj_fm(w1T[l][:, qr * D:(qr + 1) * D], D, D, xfm, evac_gelu_f,
                              bias_hbm_row=b1r[l:l + 1, qr * D:(qr + 1) * D])

                    def evac_ff(ofb, pstiles, eng, s_ev=None, qr=qr):
                        vec = eng["vector"]
                        last = None
                        for tb in range(NTB):
                            sl = ff_acc[:, ofb * T + tb * 512: ofb * T + (tb + 1) * 512]
                            if qr == 0:
                                last = vec.tensor_copy(sl, pstiles[tb][:, :])
                            else:
                                last = vec.tensor_tensor(sl, sl, pstiles[tb][:, :], ALU.add)
                        return last
                    p.proj_fm(w2T[l][qr * D:(qr + 1) * D, :], D, D, h_q, evac_ff,
                              bias_hbm_row=(b2r[l:l + 1, :] if qr == 0 else None),
                              evac_on="vector")
                pass1(xad, tmpd, acc_fm=ff_acc)
            bcast(g_bc, ln2g[l:l + 1, :])
            bcast(b_bc, ln2b[l:l + 1, :])
            pass2(tmpd, xfm, g_bc, b_bc, store=xad)

        with ExitStack() as lz:
            xb = [p.sbuf("fzx", (128, D), F32, ctx=lz) for _ in range(2)]
            ab = [p.sbuf("fza", (128, D), F32, ctx=lz) for _ in range(2)]
            s_d = [p.sem("fz_d%d" % j) for j in range(2)]
            s_v = p.sem("fz_v")
            s_o = [p.sem("fz_o%d" % j) for j in range(2)]
            bd = [sd.v for sd in s_d]
            bo_ = [so.v for so in s_o]
            bv = s_v.v
            with nc.Block() as blk:
                @blk.sync
                def _(sync):
                    for i in range(NTT):
                        if i >= 2:
                            sync.wait_ge(s_v.h, bv + i - 1)
                            sync.wait_ge(s_o[i % 2].h, bo_[i % 2] + (i // 2) * 16)
                        s_d[i % 2].ti(sync.dma_start(out=xb[i % 2][:],
                                      in_=x_in[i * 128:(i + 1) * 128, :]), 16)
                        s_d[i % 2].ti(sync.dma_start(out=ab[i % 2][:],
                                      in_=xad[i * 128:(i + 1) * 128, :]), 16)

                @blk.vector
                def _(vector):
                    for i in range(NTT):
                        vector.wait_ge(s_d[i % 2].h, bd[i % 2] + (i // 2 + 1) * 32)
                        if i >= 2:
                            vector.wait_ge(s_o[i % 2].h, bo_[i % 2] + (i // 2) * 16)
                        s_v.ti(vector.tensor_tensor(ab[i % 2][:], ab[i % 2][:],
                                                    xb[i % 2][:], ALU.add))
                        vector.wait_ge(s_v.h, s_v.v)

                @blk.gpsimd
                def _(gpsimd):
                    for i in range(NTT):
                        gpsimd.wait_ge(s_v.h, bv + i + 1)
                        s_o[i % 2].ti(gpsimd.dma_start(out=x_out[i * 128:(i + 1) * 128, :],
                                      in_=ab[i % 2][:]), 16)
                    gpsimd.wait_ge(s_o[0].h, s_o[0].v)
                    gpsimd.wait_ge(s_o[1].h, s_o[1].v)
    p.cur_ctx = None
    return p


def _mk_evac_copy(p, dst):
    def ev(ofb, pstiles, eng):
        sc = eng["scalar"]
        last = None
        for tb in range(NTB):
            last = sc.copy(dst[:, ofb * T + tb * 512: ofb * T + (tb + 1) * 512],
                           pstiles[tb][:, :])
        return last
    return ev


def _l0a_masks():
    mask = np.zeros((128, 1152), np.uint8)
    for i in range(128):
        lo = i + 25
        hi = i + 1024
        mask[i, max(0, lo):min(1152, hi + 1)] = 1
    pm = np.zeros((128, 1), np.uint8)
    pm[0:104, 0] = 1
    return mask, pm


def _l0a_pos():
    t = np.arange(T, dtype=np.float64)[:, None]
    a = np.arange(512, dtype=np.float64)[None, :]
    phase = t / (10000.0 ** (a / 511.0))
    return np.concatenate([np.cos(phase), np.sin(phase)], axis=1).astype(np.float32)


def l0_attn_a_inmaps(x_batch, inp):
    f = np.float32
    mask, pm = _l0a_masks()
    base = dict(
        l0a_wqkvT=np.ascontiguousarray(np.transpose(np.asarray(inp["L0_attn_a_Wqkv"], f), (0, 2, 1))),
        l0a_bqkv=np.ascontiguousarray(np.asarray(inp["L0_attn_a_bqkv"], f)),
        l0a_woT=np.ascontiguousarray(np.transpose(np.asarray(inp["L0_attn_a_Wo"], f), (0, 2, 1))),
        l0a_bo=np.ascontiguousarray(np.asarray(inp["L0_attn_a_bo"], f)),
        l0a_w1T=np.ascontiguousarray(np.transpose(np.asarray(inp["L0_attn_a_W1"], f), (0, 2, 1))),
        l0a_b1r=np.ascontiguousarray(np.asarray(inp["L0_attn_a_b1"], f)),
        l0a_w2T=np.ascontiguousarray(np.transpose(np.asarray(inp["L0_attn_a_W2"], f), (0, 2, 1))),
        l0a_b2r=np.ascontiguousarray(np.asarray(inp["L0_attn_a_b2"], f)),
        l0a_ln1g=np.ascontiguousarray(np.asarray(inp["L0_attn_a_ln1_g"], f)),
        l0a_ln1b=np.ascontiguousarray(np.asarray(inp["L0_attn_a_ln1_b"], f)),
        l0a_ln2g=np.ascontiguousarray(np.asarray(inp["L0_attn_a_ln2_g"], f)),
        l0a_ln2b=np.ascontiguousarray(np.asarray(inp["L0_attn_a_ln2_b"], f)),
        l0a_ngrow=np.asarray(inp["L0_attn_a_norm_in_g"], f).reshape(1, D),
        l0a_nbrow=np.asarray(inp["L0_attn_a_norm_in_b"], f).reshape(1, D),
        l0a_posd=_l0a_pos(),
        l0a_maskb=mask, l0a_pmask7=pm,
        c_ident_in=IDENT128)
    return [dict(base, x=np.ascontiguousarray(xb, f)) for xb in x_batch]


def build_fused():
    p = P("fused")
    nc = p.nc
    x0 = p.dram_in("x", (T, D))
    y_out = p.dram_out("y", (T, D))
    ch = [nc.dram_tensor("xchain%d" % i, [T, D], F32, kind="Internal").ap()
          for i in range(7)]
    A_l0a = declare_l0_attn_a(p)
    A_l0f = declare_l0_ffn(p)
    A_l0b = declare_l0_attn_b(p)
    A_l1a = declare_l1_attn_a(p)
    A_l1f = declare_l1_ffn(p)
    A_l1b = declare_l1_attn_b(p)
    with p.ctx:
        p.common(max_nk=16, max_bias=1024)
        with nc.named_scope("s0_l0attna"):
            build_l0_attn_a(p, (x0, ch[0]), A_l0a)
        with nc.named_scope("s1_l0ffn"):
            build_l0_ffn(p, (ch[0], ch[1]), A_l0f)
        with nc.named_scope("s2_l0attnb"):
            build_l0_attn_b(p, (ch[1], ch[2]), A_l0b)
        with nc.named_scope("s3_l0ffn"):
            build_l0_ffn(p, (ch[2], ch[3]), A_l0f)
        with nc.named_scope("s4_l1attna"):
            build_l1_attn_a(p, (ch[3], ch[4]), A_l1a)
        with nc.named_scope("s5_l1ffn"):
            build_l1_ffn(p, (ch[4], ch[5]), A_l1f)
        with nc.named_scope("s6_l1attnb"):
            build_l1_attn_b(p, (ch[5], ch[6]), A_l1b)
        with nc.named_scope("s7_l1ffn"):
            build_l1_ffn(p, (ch[6], y_out), A_l1f)
    return p


def _weights_of(inmaps_fn, inp):
    m = inmaps_fn([np.zeros((T, D), np.float32)], inp)[0]
    m.pop("x")
    return m


def fused_weightmap(inp):
    w = {}
    for fn in (l0_attn_a_inmaps, l0_ffn_inmaps, l0_attn_b_inmaps,
               l1_attn_a_inmaps, l1_ffn_inmaps, l1_attn_b_inmaps):
        w.update(_weights_of(fn, inp))
    return w


def kernel(**inputs):
    inp = {k: np.asarray(v) for k, v in inputs.items()}
    x = np.ascontiguousarray(np.asarray(inp["L0_attn_a_x"], np.float32))
    nb = x.shape[0]
    p = _get("fused", build_fused)
    w = fused_weightmap(inp)
    maps = [dict(w, x=np.ascontiguousarray(x[b])) for b in range(nb)]
    res = run_bass_kernel_spmd(p.nc, maps, core_ids=list(range(nb)))
    out = np.stack([res.results[b]["y"] for b in range(nb)], 0)
    return np.ascontiguousarray(out.astype(np.float32))


_CACHE = {}


def _get(name, builder):
    if name not in _CACHE:
        _CACHE[name] = builder()
    return _CACHE[name]


def kernel_unfused(**inputs):
    inp = {k: np.asarray(v) for k, v in inputs.items()}
    x = np.ascontiguousarray(np.asarray(inp["L0_attn_a_x"], np.float32))
    nb = x.shape[0]
    cores = list(range(nb))

    def run(p, maps):
        res = run_bass_kernel_spmd(p.nc, maps, core_ids=cores)
        return np.stack([res.results[b]["y"] for b in range(nb)], 0)

    xs = [x[b] for b in range(nb)]
    x = run(_get("l0a", build_l0_attn_a), l0_attn_a_inmaps(xs, inp))
    xs = [x[b] for b in range(nb)]
    x = run(_get("l0f", build_l0_ffn), l0_ffn_inmaps(xs, inp))
    xs = [x[b] for b in range(nb)]
    x = run(_get("l0b", build_l0_attn_b), l0_attn_b_inmaps(xs, inp))
    xs = [x[b] for b in range(nb)]
    x = run(_get("l0f", build_l0_ffn), l0_ffn_inmaps(xs, inp))
    xs = [x[b] for b in range(nb)]
    x = run(_get("l1a", build_l1_attn_a), l1_attn_a_inmaps(xs, inp))
    xs = [x[b] for b in range(nb)]
    x = run(_get("l1f", build_l1_ffn), l1_ffn_inmaps(xs, inp))
    xs = [x[b] for b in range(nb)]
    x = run(_get("l1b", build_l1_attn_b), l1_attn_b_inmaps(xs, inp))
    xs = [x[b] for b in range(nb)]
    x = run(_get("l1f", build_l1_ffn), l1_ffn_inmaps(xs, inp))
    return np.ascontiguousarray(x.astype(np.float32))


# version 10
# speedup vs baseline: 1.1245x; 1.1245x over previous
import numpy as np
from contextlib import ExitStack
import concourse.bass as bass
import concourse.mybir as mybir
from concourse.bass_utils import run_bass_kernel_spmd

F32 = mybir.dt.float32
BF16 = mybir.dt.bfloat16
AF = mybir.ActivationFunctionType
ALU = mybir.AluOpType
AX = mybir.AxisListType

T = 2048
D = 1024
NTT = T // 128
NDC = D // 128
NTB = T // 512
NEG = -3.0e38


class Sem:
    def __init__(self, h):
        self.h = h
        self.v = 0

    def ti(self, instr, n=1):
        instr.then_inc(self.h, n)
        self.v += n
        return self.v


_NCTR = [0]


def uname(s):
    _NCTR[0] += 1
    return "%s%d" % (s, _NCTR[0])


class EC:
    def __init__(self, eng, sem):
        self.e = eng
        self.s = sem

    def do(self, instr):
        self.s.ti(instr)
        self.e.wait_ge(self.s.h, self.s.v)
        return instr


class P:
    def __init__(self, name):
        self.nc = bass.Bass("TRN2", target_bir_lowering=False)
        self.ctx = ExitStack()
        self.name = name
        self.sems = {}
        self.cur_ctx = None

    def sbuf(self, name, shape, dt, ctx=None):
        return (ctx or self.cur_ctx or self.ctx).enter_context(
            self.nc.sbuf_tensor(uname(name), list(shape), dt))

    def psum(self, name, shape, dt=F32, ctx=None):
        return (ctx or self.cur_ctx or self.ctx).enter_context(
            self.nc.psum_tensor(uname(name), list(shape), dt))

    def begin_pool(self):
        self._local = {}
        self._next = 0

    def sem(self, name):
        if not hasattr(self, "_local"):
            self.begin_pool()
        if name not in self._local:
            key = "pool%d" % self._next
            self._next += 1
            if key not in self.sems:
                self.sems[key] = Sem(self.ctx.enter_context(self.nc.semaphore(uname(key))))
            self._local[name] = self.sems[key]
        return self._local[name]

    def dram_in(self, name, shape):
        return self.nc.dram_tensor(name, list(shape), F32, kind="ExternalInput").ap()

    def dram_out(self, name, shape):
        return self.nc.dram_tensor(name, list(shape), F32, kind="ExternalOutput").ap()

    def common(self, max_nk=8, max_bias=4096):
        nc = self.nc
        self.max_nk = max_nk
        self.ident = self.sbuf("c_ident", (128, 128), BF16)
        self.identf = self.sbuf("c_identf", (128, 128), F32)
        self.ones_row = self.sbuf("c_ones", (1, 512), BF16)
        self.zrow = self.sbuf("c_zrow", (1, 64), BF16)
        self.ident_in = self.nc.dram_tensor("c_ident_in", [128, 128], F32,
                                            kind="ExternalInput").ap()
        self.s1 = self.sbuf("st_s1", (128, NTT), F32)
        self.s2 = self.sbuf("st_s2", (128, NTT), F32)
        self.sa = self.sbuf("st_sa", (128, NTT), F32)
        self.sb = self.sbuf("st_sb", (128, NTT), F32)
        self.junk = self.sbuf("st_junk", (128, 1024), F32)
        self.pw = [self.sbuf("pool_wf", (128, max_nk * 128), F32) for _ in range(2)]
        self.pwb = [self.sbuf("pool_wb", (128, max_nk * 128), BF16) for _ in range(2)]
        self.brf = self.sbuf("pool_brf", (1, max_bias), F32)
        self.brb = self.sbuf("pool_brb", (1, max_bias), BF16)
        self.cst = self.sbuf("c_cst", (128, 8), F32)
        self._cst_map = {}
        with nc.Block(no_gpsimd_drain=True) as blk:
            @blk.sync
            def _(sync):
                s = self.sem("c_ld")
                s.ti(sync.dma_start(out=self.identf[:], in_=self.ident_in[:, :]), 16)
                sync.wait_ge(s.h, s.v)

            @blk.vector
            def _(vector):
                vector.memset(self.ones_row[:], 1.0)
                vector.memset(self.zrow[:], 0.0)
        with nc.Block(no_gpsimd_drain=True) as blk:
            @blk.vector
            def _(vector):
                vector.tensor_copy(self.ident[:], self.identf[:])

    def const_col(self, val):
        if val not in self._cst_map:
            idx = len(self._cst_map)
            assert idx < 8
            self._cst_map[val] = idx
            with self.nc.Block() as blk:
                @blk.vector
                def _(vector):
                    vector.memset(self.cst[:, idx:idx + 1], float(val))
        return self.cst[:, self._cst_map[val]:self._cst_map[val] + 1]

    def transpose_bf(self, items, n_psum=4, add_to=False):
        nc = self.nc
        with ExitStack() as lctx:
            pt = [self.psum("tp", (128, 128), BF16, ctx=lctx) for k in range(n_psum)]
            s_t = self.sem("tr_t")
            s_v = self.sem("tr_v")
            bt, bv = s_t.v, s_v.v
            with nc.Block(no_gpsimd_drain=True) as blk:
                @blk.tensor
                def _(tensor):
                    for k, (src, dst) in enumerate(items):
                        if k >= n_psum:
                            tensor.wait_ge(s_v.h, bv + k - n_psum + 1)
                        s_t.ti(tensor.transpose(pt[k % n_psum][:, :], src, self.ident[:]))

                @blk.vector
                def _(vector):
                    for k, (src, dst) in enumerate(items):
                        vector.wait_ge(s_t.h, bt + k + 1)
                        if add_to:
                            s_v.ti(vector.tensor_tensor(dst, dst, pt[k % n_psum][:, :], ALU.add))
                        else:
                            s_v.ti(vector.tensor_copy(dst, pt[k % n_psum][:, :]))

    def ln_stats_stream(self, x_hbm, n=NTT, width=D, eps=1e-5):
        nc = self.nc
        with ExitStack() as lctx:
            xb = [self.sbuf("lnx", (128, width), F32, ctx=lctx) for _ in range(3)]
            s_d = [self.sem("lnst_d%d" % j) for j in range(3)]
            s_s = self.sem("lnst_s")
            s_v = self.sem("lnst_v")
            bd = [sd.v for sd in s_d]
            bs, bv = s_s.v, s_v.v
            with nc.Block(no_gpsimd_drain=True) as blk:
                @blk.sync
                def _(sync):
                    for i in range(n):
                        if i >= 3:
                            sync.wait_ge(s_s.h, bs + i - 2)
                            sync.wait_ge(s_v.h, bv + i - 2)
                        s_d[i % 3].ti(sync.dma_start(out=xb[i % 3][:],
                                      in_=x_hbm[i * 128:(i + 1) * 128, :]), 16)

                @blk.scalar
                def _(scalar):
                    for i in range(n):
                        scalar.wait_ge(s_d[i % 3].h, bd[i % 3] + (i // 3 + 1) * 16)
                        s_s.ti(scalar.activation(out=self.junk[:, 0:width], in_=xb[i % 3][:],
                                                 func=AF.Square, accum_out=self.s2[:, i:i + 1]))
                        scalar.wait_ge(s_s.h, s_s.v)

                @blk.vector
                def _(vector):
                    for i in range(n):
                        vector.wait_ge(s_d[i % 3].h, bd[i % 3] + (i // 3 + 1) * 16)
                        s_v.ti(vector.reduce_sum(self.s1[:, i:i + 1], xb[i % 3][:], axis=AX.X))
        self.ln_combine(n, width, eps)

    def ln_combine(self, n=NTT, width=D, eps=1e-5):
        nc = self.nc
        with nc.Block(no_gpsimd_drain=True) as blk:
            @blk.vector
            def _(vector):
                c = EC(vector, self.sem("vchain"))
                c.do(vector.tensor_scalar_mul(self.s1[:, 0:n], self.s1[:, 0:n], 1.0 / width))
                c.do(vector.tensor_tensor(self.sa[:, 0:n], self.s1[:, 0:n], self.s1[:, 0:n], ALU.mult))
                c.do(vector.tensor_scalar(self.sb[:, 0:n], self.s2[:, 0:n], 1.0 / width, None, ALU.mult))
                c.do(vector.tensor_tensor(self.sa[:, 0:n], self.sb[:, 0:n], self.sa[:, 0:n], ALU.subtract))
        epsc = self.const_col(eps)
        with nc.Block(no_gpsimd_drain=True) as blk:
            @blk.scalar
            def _(scalar):
                scalar.activation(out=self.sa[:, 0:n], in_=self.sa[:, 0:n], func=AF.Sqrt, bias=epsc)
        with nc.Block(no_gpsimd_drain=True) as blk:
            @blk.vector
            def _(vector):
                vector.reciprocal(self.sa[:, 0:n], self.sa[:, 0:n])
        with nc.Block(no_gpsimd_drain=True) as blk:
            @blk.vector
            def _(vector):
                c = EC(vector, self.sem("vchain"))
                c.do(vector.tensor_tensor(self.sb[:, 0:n], self.s1[:, 0:n], self.sa[:, 0:n], ALU.mult))
                c.do(vector.tensor_scalar_mul(self.sb[:, 0:n], self.sb[:, 0:n], -1.0))

    def ln_apply_to_fm(self, x_hbm, y_fm, n=NTT, fm_w=T, keep_tm_bf=None):
        nc = self.nc
        with ExitStack() as lctx:
            xb = [self.sbuf("lax", (128, D), F32, ctx=lctx) for _ in range(2)]
            yb = [self.sbuf("lay", (128, D), F32, ctx=lctx) for _ in range(2)]
            cb = [self.sbuf("lac", (128, D), BF16, ctx=lctx) for _ in range(2)]
            pt = [self.psum("latp", (128, 128), BF16, ctx=lctx) for _ in range(4)]
            s_d = [self.sem("la_d%d" % j) for j in range(2)]
            s_y = self.sem("la_y")
            s_c = self.sem("la_c")
            s_t = self.sem("la_t")
            s_e = self.sem("la_e")
            bd = [sd.v for sd in s_d]
            by, bc, bt, be = s_y.v, s_c.v, s_t.v, s_e.v
            with nc.Block(no_gpsimd_drain=True) as blk:
                @blk.sync
                def _(sync):
                    for i in range(n):
                        if i >= 2:
                            sync.wait_ge(s_y.h, by + i - 1)
                        s_d[i % 2].ti(sync.dma_start(out=xb[i % 2][:],
                                      in_=x_hbm[i * 128:(i + 1) * 128, :]), 16)

                @blk.vector
                def _(vector):
                    def affine(i):
                        vector.wait_ge(s_d[i % 2].h, bd[i % 2] + (i // 2 + 1) * 16)
                        if i >= 2:
                            vector.wait_ge(s_c.h, bc + i - 1)
                        s_y.ti(vector.tensor_scalar(yb[i % 2][:], xb[i % 2][:],
                                                    self.sa[:, i:i + 1], self.sb[:, i:i + 1],
                                                    ALU.mult, ALU.add))

                    def evacs(i):
                        for c in range(NDC):
                            k = i * NDC + c
                            vector.wait_ge(s_t.h, bt + k + 1)
                            s_e.ti(vector.tensor_copy(
                                y_fm[:, c * fm_w + i * 128: c * fm_w + (i + 1) * 128],
                                pt[k % 4][:, :]))
                    affine(0)
                    if n > 1:
                        affine(1)
                    for i in range(n):
                        evacs(i)
                        if i + 2 < n:
                            affine(i + 2)

                @blk.scalar
                def _(scalar):
                    for i in range(n):
                        scalar.wait_ge(s_y.h, by + i + 1)
                        if i >= 2:
                            scalar.wait_ge(s_e.h, be + (i - 1) * NDC)
                        s_c.ti(scalar.copy(cb[i % 2][:], yb[i % 2][:]))
                        if keep_tm_bf is not None:
                            scalar.copy(keep_tm_bf[:, i * D:(i + 1) * D], yb[i % 2][:])
                            scalar.wait_ge(s_c.h, s_c.v)

                @blk.tensor
                def _(tensor):
                    for i in range(n):
                        tensor.wait_ge(s_c.h, bc + i + 1)
                        for c in range(NDC):
                            k = i * NDC + c
                            if k >= 4:
                                tensor.wait_ge(s_e.h, be + k - 3)
                            s_t.ti(tensor.transpose(pt[k % 4][:, :],
                                                    cb[i % 2][:, c * 128:(c + 1) * 128],
                                                    self.ident[:]))

    def finish_stream(self, x_hbm, acc_fm, out_hbm, n=NTT, a_cols=None, b_cols=None):
        nc = self.nc
        with ExitStack() as lctx:
            xb = [self.sbuf("fsx", (128, D), F32, ctx=lctx) for _ in range(3)]
            ob = [self.sbuf("fso", (128, D), F32, ctx=lctx) for _ in range(2)]
            cb = self.sbuf("fsc", (128, NDC * 128), BF16, ctx=lctx)
            pt = [self.psum("fstp", (128, 128), BF16, ctx=lctx) for _ in range(4)]
            s_d = [self.sem("fs_d%d" % j) for j in range(3)]
            s_a = self.sem("fs_a")
            s_cc = self.sem("fs_c")
            s_t = self.sem("fs_t")
            s_ad = self.sem("fs_ad")
            s_o = [self.sem("fs_o%d" % j) for j in range(2)]
            bd = [sd.v for sd in s_d]
            bo = [so.v for so in s_o]
            ba, bc2, bt, bad = s_a.v, s_cc.v, s_t.v, s_ad.v
            with nc.Block() as blk:
                @blk.sync
                def _(sync):
                    for i in range(n):
                        if i >= 3:
                            sync.wait_ge(s_a.h, ba + i - 2)
                        s_d[i % 3].ti(sync.dma_start(out=xb[i % 3][:],
                                      in_=x_hbm[i * 128:(i + 1) * 128, :]), 16)

                @blk.scalar
                def _(scalar):
                    for i in range(n):
                        if i >= 1:
                            scalar.wait_ge(s_t.h, bt + i * NDC)
                        last = None
                        for c in range(NDC):
                            last = scalar.copy(cb[:, c * 128:(c + 1) * 128],
                                               acc_fm[:, c * T + i * 128: c * T + (i + 1) * 128])
                        s_cc.ti(last)

                @blk.vector
                def _(vector):
                    for i in range(n):
                        vector.wait_ge(s_d[i % 3].h, bd[i % 3] + (i // 3 + 1) * 16)
                        if i >= 2:
                            vector.wait_ge(s_o[i % 2].h, bo[i % 2] + (i // 2) * 16)
                        if a_cols is not None:
                            v = vector.tensor_scalar(ob[i % 2][:], xb[i % 3][:],
                                                     a_cols[:, i:i + 1], b_cols[:, i:i + 1],
                                                     ALU.mult, ALU.add)
                        else:
                            v = vector.tensor_copy(ob[i % 2][:], xb[i % 3][:])
                        s_a.ti(v)
                        for c in range(NDC):
                            k = i * NDC + c
                            vector.wait_ge(s_t.h, bt + k + 1)
                            s_ad.ti(vector.tensor_tensor(ob[i % 2][:, c * 128:(c + 1) * 128],
                                                         ob[i % 2][:, c * 128:(c + 1) * 128],
                                                         pt[k % 4][:, :], ALU.add))

                @blk.tensor
                def _(tensor):
                    for i in range(n):
                        tensor.wait_ge(s_cc.h, bc2 + i + 1)
                        tensor.wait_ge(s_a.h, ba + i + 1)
                        for c in range(NDC):
                            k = i * NDC + c
                            if k >= 4:
                                tensor.wait_ge(s_ad.h, bad + k - 3)
                            s_t.ti(tensor.transpose(pt[k % 4][:, :],
                                                    cb[:, c * 128:(c + 1) * 128],
                                                    self.ident[:]))

                @blk.gpsimd
                def _(gpsimd):
                    for i in range(n):
                        gpsimd.wait_ge(s_ad.h, bad + (i + 1) * NDC)
                        s_o[i % 2].ti(gpsimd.dma_start(out=out_hbm[i * 128:(i + 1) * 128, :],
                                      in_=ob[i % 2][:]), 16)
                    gpsimd.wait_ge(s_o[0].h, s_o[0].v)
                    gpsimd.wait_ge(s_o[1].h, s_o[1].v)

    def proj_fm(self, w_hbm, din, dout, x_fm, evac, x_w=T, n_tb=NTB,
                bias_hbm_row=None, evac_on="scalar", pre_dma=None,
                post_scalar=None, post_vector=None, gpsimd_body=None):
        nc = self.nc
        nk = din // 128
        assert nk <= self.max_nk
        nofb = dout // 128
        tbw = x_w // n_tb
        assert tbw <= 512
        s_dma = [self.sem("p_dma%d" % j) for j in range(2)]
        s_bias = self.sem("p_bias")
        s_cast, s_mm, s_ev = (self.sem("p_cast"),
                              self.sem("p_mm"), self.sem("p_ev"))
        b_dma = [sd.v for sd in s_dma]
        b_bias = s_bias.v
        b_cast, b_mm, b_ev = s_cast.v, s_mm.v, s_ev.v
        has_bias = bias_hbm_row is not None
        with ExitStack() as lctx:
            ps = [[self.psum("pp", (128, tbw), F32, ctx=lctx) for _ in range(n_tb)] for _ in range(2)]
            with (nc.Block() if gpsimd_body is not None else nc.Block(no_gpsimd_drain=True)) as blk:
                if gpsimd_body is not None:
                    blk.gpsimd(gpsimd_body)

                @blk.sync
                def _(sync):
                    if pre_dma is not None:
                        for dst, src in pre_dma:
                            s_bias.ti(sync.dma_start(out=dst, in_=src), 16)
                    if has_bias:
                        s_bias.ti(sync.dma_start(out=self.brf[:, 0:dout], in_=bias_hbm_row), 16)
                    for ofb in range(nofb):
                        if ofb >= 2:
                            sync.wait_ge(s_ev.h, b_ev + ofb - 1)
                        for k in range(nk):
                            s_dma[ofb % 2].ti(sync.dma_start(
                                out=self.pw[ofb % 2][:, k * 128:(k + 1) * 128],
                                in_=w_hbm[k * 128:(k + 1) * 128, ofb * 128:(ofb + 1) * 128]), 16)

                @blk.vector
                def _(vector):
                    if has_bias:
                        vector.wait_ge(s_bias.h, b_bias + 16)
                        vector.tensor_copy(self.brb[:, 0:dout], self.brf[:, 0:dout])

                    def cast(ofb):
                        vector.wait_ge(s_dma[ofb % 2].h, b_dma[ofb % 2] + (ofb // 2 + 1) * nk * 16)
                        if ofb >= 2:
                            vector.wait_ge(s_ev.h, b_ev + ofb - 1)
                        s_cast.ti(vector.tensor_copy(self.pwb[ofb % 2][:, 0:nk * 128],
                                                     self.pw[ofb % 2][:, 0:nk * 128]))
                    if evac_on == "vector":
                        cast(0)
                        if nofb > 1:
                            cast(1)
                        for ofb in range(nofb):
                            vector.wait_ge(s_mm.h, b_mm + ofb + 1)
                            last = evac(ofb, ps[ofb % 2], dict(vector=vector, p=self),
                                        s_ev=s_ev)
                            if last is not None:
                                s_ev.ti(last)
                            if ofb + 2 < nofb:
                                cast(ofb + 2)
                    else:
                        for ofb in range(nofb):
                            cast(ofb)

                @blk.tensor
                def _(tensor):
                    for ofb in range(nofb):
                        tensor.wait_ge(s_cast.h, b_cast + ofb + 1)
                        if ofb >= 2:
                            tensor.wait_ge(s_ev.h, b_ev + ofb - 1)
                        last = None
                        for k in range(nk):
                            for tb in range(n_tb):
                                last = tensor.matmul(ps[ofb % 2][tb][:, :],
                                                     self.pwb[ofb % 2][:, k * 128:(k + 1) * 128],
                                                     x_fm[:, k * x_w + tb * tbw: k * x_w + (tb + 1) * tbw],
                                                     start=(k == 0), stop=(k == nk - 1 and not has_bias))
                        if has_bias:
                            for tb in range(n_tb):
                                last = tensor.matmul(ps[ofb % 2][tb][:, :],
                                                     self.brb[0:1, ofb * 128:(ofb + 1) * 128],
                                                     self.ones_row[0:1, 0:tbw],
                                                     start=False, stop=True)
                        s_mm.ti(last)

                if evac_on == "scalar":
                    @blk.scalar
                    def _(scalar):
                        for ofb in range(nofb):
                            scalar.wait_ge(s_mm.h, b_mm + ofb + 1)
                            last = evac(ofb, ps[ofb % 2], dict(scalar=scalar, p=self))
                            s_ev.ti(last)
                            if post_scalar is not None:
                                post_scalar(ofb, scalar)
                    if post_vector is not None:
                        @blk.vector
                        def _(vector):
                            post_vector(vector)

    def proj_tm(self, w_hbm, din, dout, x_fm, out_tm, bias_hbm_row=None, x_w=T,
                out_stride=None, n=NTT):
        nc = self.nc
        if out_stride is None:
            out_stride = dout
        nk = din // 128
        s_dma, s_mm, s_ev = self.sem("t_dma"), self.sem("t_mm"), self.sem("t_ev")
        b_dma, b_mm, b_ev = s_dma.v, s_mm.v, s_ev.v
        has_bias = bias_hbm_row is not None
        with ExitStack() as lctx:
            wb = self.sbuf("wtm", (128, nk * dout), F32, ctx=lctx)
            wbb = self.sbuf("wtmb", (128, nk * dout), BF16, ctx=lctx)
            pt = [self.psum("ptm", (128, dout), F32, ctx=lctx) for _ in range(4)]
            onec = self.sbuf("onec", (1, 128), BF16, ctx=lctx)
            with nc.Block(no_gpsimd_drain=True) as blk:
                @blk.sync
                def _(sync):
                    for k in range(nk):
                        s_dma.ti(sync.dma_start(out=wb[:, k * dout:(k + 1) * dout],
                                                in_=w_hbm[k * 128:(k + 1) * 128, 0:dout]), 16)
                    if has_bias:
                        s_dma.ti(sync.dma_start(out=self.brf[:, 0:dout], in_=bias_hbm_row), 16)
                    sync.wait_ge(s_dma.h, s_dma.v)

                @blk.vector
                def _(vector):
                    vector.memset(onec[:], 1.0)
            with nc.Block(no_gpsimd_drain=True) as blk:
                @blk.vector
                def _(vector):
                    vector.tensor_copy(wbb[:], wb[:])
                    if has_bias:
                        vector.tensor_copy(self.brb[:, 0:dout], self.brf[:, 0:dout])
            with nc.Block(no_gpsimd_drain=True) as blk:
                @blk.tensor
                def _(tensor):
                    for i in range(n):
                        if i >= 4:
                            tensor.wait_ge(s_ev.h, b_ev + i - 3)
                        for k in range(nk):
                            mm = tensor.matmul(pt[i % 4][:, :],
                                               x_fm[:, k * x_w + i * 128: k * x_w + (i + 1) * 128],
                                               wbb[:, k * dout:(k + 1) * dout],
                                               start=(k == 0), stop=(k == nk - 1 and not has_bias))
                        if has_bias:
                            mm = tensor.matmul(pt[i % 4][:, :], onec[0:1, 0:128],
                                               self.brb[0:1, 0:dout],
                                               start=False, stop=True)
                        s_mm.ti(mm)

                @blk.vector
                def _(vector):
                    for i in range(n):
                        vector.wait_ge(s_mm.h, b_mm + i + 1)
                        s_ev.ti(vector.tensor_copy(out_tm[:, i * out_stride:i * out_stride + dout],
                                                   pt[i % 4][:, :]))

    def softmax_tm(self, buf, n, width, stride=None):
        if stride is None:
            stride = width
        nc = self.nc
        with nc.Block(no_gpsimd_drain=True) as blk:
            @blk.vector
            def _(vector):
                for i in range(n):
                    vector.reduce_max(self.s1[:, i:i + 1], buf[:, i * stride:i * stride + width],
                                      axis=AX.X, negate=True)
        with nc.Block(no_gpsimd_drain=True) as blk:
            @blk.scalar
            def _(scalar):
                for i in range(n):
                    scalar.activation(out=buf[:, i * stride:i * stride + width],
                                      in_=buf[:, i * stride:i * stride + width],
                                      func=AF.Exp, bias=self.s1[:, i:i + 1],
                                      accum_out=self.s2[:, i:i + 1])
        with nc.Block(no_gpsimd_drain=True) as blk:
            @blk.vector
            def _(vector):
                vector.reciprocal(self.s2[:, 0:n], self.s2[:, 0:n])
        with nc.Block(no_gpsimd_drain=True) as blk:
            @blk.vector
            def _(vector):
                for i in range(n):
                    vector.tensor_scalar_mul(buf[:, i * stride:i * stride + width],
                                             buf[:, i * stride:i * stride + width],
                                             self.s2[:, i:i + 1])


def declare_l1_ffn(p):
    nc = p.nc
    return (p.dram_in("l1f_ws1", (D, D)), p.dram_in("l1f_bs1", (1, D)),
            p.dram_in("l1f_ws2", (D, D)), p.dram_in("l1f_bs2", (1, D)),
            p.dram_in("l1f_we1", (31, D, D)), p.dram_in("l1f_be1", (31, D)),
            p.dram_in("l1f_we2", (31, D, D)), p.dram_in("l1f_be2", (31, D)),
            p.dram_in("l1f_wr32", (D, 32)), p.dram_in("l1f_br32", (1, 32)),
            nc.dram_tensor("l1f_gates_scratch", [31, T], F32, kind="Internal").ap())


def build_l1_ffn(p=None, xio=None, aps=None):
    standalone = p is None
    if standalone:
        p = P("l1ffn")
    nc = p.nc
    if aps is None:
        aps = declare_l1_ffn(p)
    (ws1, bs1, ws2, bs2, we1, be1, we2, be2, wr, br, gdram_) = aps
    x_in = xio[0] if xio else p.dram_in("x", (T, D))
    x_out = xio[1] if xio else p.dram_out("y", (T, D))
    gdram = aps[-1]

    cm = p.ctx if standalone else ExitStack()
    with cm:
        if standalone:
            p.common(max_bias=1024)
        p.cur_ctx = cm
        p.begin_pool()
        acc_fm = p.sbuf("acc_fm", (128, NDC * T), F32)
        p.sa2 = p.sbuf("st_sa2", (128, NTT), F32)
        icm = ExitStack()
        p.cur_ctx = icm
        y_fm = p.sbuf("y_fm", (128, NDC * T), BF16)
        h_fm = p.sbuf("h_fm", (128, NDC * T), BF16)
        aff_tm = p.sbuf("aff_tm", (128, NTT * 32), F32)
        gat_tm = p.sbuf("gat_tm", (128, NTT * 32), F32)
        m8 = p.sbuf("m8", (128, NTT * 8), F32)
        gat_fm = p.sbuf("gat_fm", (32, T), F32)
        g_bc = p.sbuf("g_bc", (128, T), F32)
        evs = p.sbuf("evs", (128, 512), F32)

        p.ln_stats_stream(x_in)
        with nc.Block(no_gpsimd_drain=True) as blk:
            @blk.vector
            def _(vector):
                vector.tensor_scalar_add(p.sa2[:, 0:NTT], p.sa[:, 0:NTT], 1.0)
        p.ln_apply_to_fm(x_in, y_fm)

        def evac_gelu(ofb, pstiles, eng):
            sc = eng["scalar"]
            last = None
            for tb in range(NTB):
                last = sc.activation(out=h_fm[:, ofb * T + tb * 512: ofb * T + (tb + 1) * 512],
                                     in_=pstiles[tb][:, :], func=AF.Gelu)
            return last
        p.proj_fm(ws1, D, D, y_fm, evac_gelu, bias_hbm_row=bs1[0:1, :])

        def evac_acc(ofb, pstiles, eng):
            sc = eng["scalar"]
            last = None
            for tb in range(NTB):
                last = sc.copy(acc_fm[:, ofb * T + tb * 512: ofb * T + (tb + 1) * 512],
                               pstiles[tb][:, :])
            return last
        p.proj_fm(ws2, D, D, h_fm, evac_acc, bias_hbm_row=bs2[0:1, :])

        p.proj_tm(wr, D, 32, y_fm, aff_tm, bias_hbm_row=br[0:1, :])
        with nc.Block(no_gpsimd_drain=True) as blk:
            @blk.vector
            def _(vector):
                for i in range(NTT):
                    vector.memset(aff_tm[:, i * 32 + 31: i * 32 + 32], NEG)
        p.softmax_tm(aff_tm, NTT, 32)
        with nc.Block(no_gpsimd_drain=True) as blk:
            @blk.vector
            def _(vector):
                for i in range(NTT):
                    vector.max(m8[:, i * 8:(i + 1) * 8], aff_tm[:, i * 32:(i + 1) * 32])
        with nc.Block(no_gpsimd_drain=True) as blk:
            @blk.vector
            def _(vector):
                for i in range(NTT):
                    vector.tensor_scalar(gat_tm[:, i * 32:(i + 1) * 32],
                                         aff_tm[:, i * 32:(i + 1) * 32],
                                         m8[:, i * 8 + 2:i * 8 + 3], None, ALU.is_ge)
        with nc.Block(no_gpsimd_drain=True) as blk:
            @blk.vector
            def _(vector):
                for i in range(NTT):
                    vector.tensor_tensor(gat_tm[:, i * 32:(i + 1) * 32],
                                         gat_tm[:, i * 32:(i + 1) * 32],
                                         aff_tm[:, i * 32:(i + 1) * 32], ALU.mult)
        with ExitStack() as lctx:
            gtp = p.psum("gtp", (32, 128), F32, ctx=lctx)
            s_g, s_gv = p.sem("s_g"), p.sem("s_gv")
            bg, bgv = s_g.v, s_gv.v
            with nc.Block(no_gpsimd_drain=True) as blk:
                @blk.tensor
                def _(tensor):
                    for i in range(NTT):
                        if i >= 1:
                            tensor.wait_ge(s_gv.h, bgv + i)
                        s_g.ti(tensor.transpose(gtp[:, :], gat_tm[:, i * 32:(i + 1) * 32],
                                                p.identf[:]))

                @blk.vector
                def _(vector):
                    for i in range(NTT):
                        vector.wait_ge(s_g.h, bg + i + 1)
                        s_gv.ti(vector.tensor_copy(gat_fm[0:32, i * 128:(i + 1) * 128], gtp[:, :]))
        sdr = p.sem("sdr")
        with nc.Block(no_gpsimd_drain=True) as blk:
            @blk.sync
            def _(sync):
                sdr.ti(sync.dma_start(out=gdram[:, :], in_=gat_fm[0:31, :]), 16)
                sync.wait_ge(sdr.h, sdr.v)

        evs8 = p.sbuf("evs8", (128, 8 * 512), F32)
        for e in range(31):
            p.proj_fm(we1[e], D, D, y_fm, evac_gelu, bias_hbm_row=be1[e:e + 1, :],
                      pre_dma=[(g_bc[:, :],
                                bass.AP(gdram.tensor, gdram.offset + e * T, [[0, 128], [1, T]]))])

            s_mu = p.sem("s_mu")
            s_gp = p.sem("s_gp")
            bmu, bgp = s_mu.v, s_gp.v
            bev0 = p.sem("p_ev").v

            def evac_mult(ofb, pstiles, eng, s_ev=None, bmu=bmu, bgp=bgp):
                vec = eng["vector"]
                last = None
                for tb in range(NTB):
                    k = ofb * NTB + tb
                    if k >= 8:
                        vec.wait_ge(s_gp.h, bgp + (k - 8) // NTB + 1)
                    m = vec.tensor_tensor(evs8[:, (k % 8) * 512:(k % 8 + 1) * 512],
                                          pstiles[tb][:, :],
                                          g_bc[:, tb * 512:(tb + 1) * 512], ALU.mult)
                    if tb == NTB - 1:
                        last = m
                    else:
                        s_mu.ti(m)
                return last

            def gp_body(gpsimd, bmu=bmu, bgp=bgp, bev0=bev0):
                sev = p.sem("p_ev")
                for ofb in range(NDC):
                    for tb in range(NTB):
                        k = ofb * NTB + tb
                        if tb < NTB - 1:
                            gpsimd.wait_ge(s_mu.h, bmu + ofb * (NTB - 1) + tb + 1)
                        else:
                            gpsimd.wait_ge(sev.h, bev0 + ofb + 1)
                        op = gpsimd.tensor_tensor(
                            acc_fm[:, ofb * T + tb * 512: ofb * T + (tb + 1) * 512],
                            acc_fm[:, ofb * T + tb * 512: ofb * T + (tb + 1) * 512],
                            evs8[:, (k % 8) * 512:(k % 8 + 1) * 512], ALU.add)
                        if tb == NTB - 1:
                            s_gp.ti(op)
            p.proj_fm(we2[e], D, D, h_fm, evac_mult,
                      bias_hbm_row=be2[e:e + 1, :], evac_on="vector",
                      gpsimd_body=gp_body)

        icm.close()
        p.cur_ctx = cm
        p.begin_pool()
        p.finish_stream(x_in, acc_fm, x_out, a_cols=p.sa2[:, 0:NTT], b_cols=p.sb[:, 0:NTT])
    p.cur_ctx = None
    return p


IDENT128 = np.eye(128, dtype=np.float32)


def l1_ffn_inmaps(x_batch, inp):
    ws1 = np.ascontiguousarray(inp["L1_ffn_ws1"], np.float32)
    ws2 = np.ascontiguousarray(inp["L1_ffn_ws2"], np.float32)
    we1 = np.ascontiguousarray(inp["L1_ffn_We1"], np.float32)
    we2 = np.ascontiguousarray(inp["L1_ffn_We2"], np.float32)
    wr32 = np.zeros((D, 32), np.float32)
    wr32[:, 0:31] = inp["L1_ffn_Wr"]
    br32 = np.zeros((1, 32), np.float32)
    br32[0, 0:31] = inp["L1_ffn_br"]
    base = dict(l1f_ws1=ws1, l1f_bs1=np.ascontiguousarray(inp["L1_ffn_bs1"], np.float32).reshape(1, D),
                l1f_ws2=ws2, l1f_bs2=np.ascontiguousarray(inp["L1_ffn_bs2"], np.float32).reshape(1, D),
                l1f_we1=we1, l1f_be1=np.ascontiguousarray(inp["L1_ffn_Be1"], np.float32),
                l1f_we2=we2, l1f_be2=np.ascontiguousarray(inp["L1_ffn_Be2"], np.float32),
                l1f_wr32=wr32, l1f_br32=br32, c_ident_in=IDENT128)
    return [dict(base, x=np.ascontiguousarray(xb, np.float32)) for xb in x_batch]




def declare_l0_ffn(p):
    nc = p.nc
    return (
        p.dram_in("l0f_cwT", (D, 40)),
        p.dram_in("l0f_bias40", (1, 40)),
        p.dram_in("l0f_wgu", (40, D, D)),
        p.dram_in("l0f_wd", (32, 512, D)),
        nc.dram_tensor("l0f_gates_scratch", [32, T], F32, kind="Internal").ap(),
    )


def build_l0_ffn(p=None, xio=None, aps=None):
    standalone = p is None
    if standalone:
        p = P("l0ffn")
    nc = p.nc
    if aps is None:
        aps = declare_l0_ffn(p)
    (cwT, bias40, wgu, wd, gdram) = aps
    x_in = xio[0] if xio else p.dram_in("x", (T, D))
    x_out = xio[1] if xio else p.dram_out("y", (T, D))
    cm = p.ctx if standalone else ExitStack()
    with cm:
        if standalone:
            p.common(max_bias=128)
        p.cur_ctx = cm
        p.begin_pool()
        y_fm = p.sbuf("y_fm", (128, NDC * T), BF16)
        acc_fm = p.sbuf("acc_fm", (128, NDC * T), F32)
        sc_tm = p.sbuf("sc_tm", (128, NTT * 40), F32)
        sbp_tm = p.sbuf("sbp_tm", (128, NTT * 40), F32)
        wf_tm = p.sbuf("wf_tm", (128, NTT * 40), F32)
        m8 = p.sbuf("m8", (128, NTT * 8), F32)
        bias_bc = p.sbuf("bias_bc", (128, 40), F32)
        zw = p.sbuf("zw", (128, NTT), F32)
        za = p.sbuf("za", (128, NTT), F32)
        zb = p.sbuf("zb", (128, NTT), F32)

        p.ln_stats_stream(x_in)
        p.ln_apply_to_fm(x_in, y_fm)

        p.proj_tm(cwT, D, 40, y_fm, sc_tm)
        p.softmax_tm(sc_tm, NTT, 40)
        sdr = p.sem("sdr")
        with nc.Block(no_gpsimd_drain=True) as blk:
            @blk.sync
            def _(sync):
                src = bass.AP(bias40.tensor, bias40.offset, [[0, 128], [1, 40]])
                sdr.ti(sync.dma_start(out=bias_bc[:, :], in_=src), 16)
                sync.wait_ge(sdr.h, sdr.v)
        with nc.Block(no_gpsimd_drain=True) as blk:
            @blk.vector
            def _(vector):
                for i in range(NTT):
                    vector.tensor_tensor(sbp_tm[:, i * 40:(i + 1) * 40],
                                         sc_tm[:, i * 40:(i + 1) * 40],
                                         bias_bc[:, :], ALU.add)
        with nc.Block(no_gpsimd_drain=True) as blk:
            @blk.vector
            def _(vector):
                for i in range(NTT):
                    vector.max(m8[:, i * 8:(i + 1) * 8], sbp_tm[:, i * 40:(i + 1) * 40])
        with nc.Block(no_gpsimd_drain=True) as blk:
            @blk.vector
            def _(vector):
                for i in range(NTT):
                    vector.tensor_scalar(wf_tm[:, i * 40:(i + 1) * 40],
                                         sbp_tm[:, i * 40:(i + 1) * 40],
                                         m8[:, i * 8 + 5:i * 8 + 6], None, ALU.is_ge)
        with nc.Block(no_gpsimd_drain=True) as blk:
            @blk.vector
            def _(vector):
                c = EC(vector, p.sem("vchain"))
                for i in range(NTT):
                    c.do(vector.tensor_scalar_mul(sc_tm[:, i * 40:(i + 1) * 40],
                                                  sc_tm[:, i * 40:(i + 1) * 40], 2.5))
                    c.do(vector.tensor_tensor(wf_tm[:, i * 40:(i + 1) * 40],
                                              wf_tm[:, i * 40:(i + 1) * 40],
                                              sc_tm[:, i * 40:(i + 1) * 40], ALU.mult))
                    c.do(vector.reduce_sum(zw[:, i:i + 1],
                                           wf_tm[:, i * 40 + 32:i * 40 + 40], axis=AX.X))
        with nc.Block(no_gpsimd_drain=True) as blk:
            @blk.vector
            def _(vector):
                c = EC(vector, p.sem("vchain"))
                c.do(vector.tensor_tensor(za[:, 0:NTT], p.sa[:, 0:NTT], zw[:, 0:NTT], ALU.mult))
                c.do(vector.tensor_scalar_add(za[:, 0:NTT], za[:, 0:NTT], 1.0))
                c.do(vector.tensor_tensor(zb[:, 0:NTT], p.sb[:, 0:NTT], zw[:, 0:NTT], ALU.mult))
        with ExitStack() as lctx:
            wf_fm = p.sbuf("wf_fm", (32, T), F32, ctx=lctx)
            gtp = p.psum("gtp", (32, 128), F32, ctx=lctx)
            s_g, s_gv = p.sem("s_g"), p.sem("s_gv")
            bg, bgv = s_g.v, s_gv.v
            with nc.Block(no_gpsimd_drain=True) as blk:
                @blk.tensor
                def _(tensor):
                    for i in range(NTT):
                        if i >= 1:
                            tensor.wait_ge(s_gv.h, bgv + i)
                        s_g.ti(tensor.transpose(gtp[:, :], wf_tm[:, i * 40:i * 40 + 32],
                                                p.identf[:]))

                @blk.vector
                def _(vector):
                    for i in range(NTT):
                        vector.wait_ge(s_g.h, bg + i + 1)
                        s_gv.ti(vector.tensor_copy(wf_fm[0:32, i * 128:(i + 1) * 128], gtp[:, :]))
            with nc.Block(no_gpsimd_drain=True) as blk:
                @blk.sync
                def _(sync):
                    sdr.ti(sync.dma_start(out=gdram[:, :], in_=wf_fm[0:32, :]), 16)
                    sync.wait_ge(sdr.h, sdr.v)

        ectx = ExitStack()
        gu_bf = p.sbuf("gu_bf", (128, NDC * T), BF16, ctx=ectx)
        h_fm = p.sbuf("h_fm", (128, 4 * T), BF16, ctx=ectx)
        g_bc = p.sbuf("g_bc", (128, T), F32, ctx=ectx)
        tf2 = p.sbuf("tf2", (128, T), BF16, ctx=ectx)

        def evac_gu(ofb, pstiles, eng):
            sc = eng["scalar"]
            last = None
            for tb in range(NTB):
                last = sc.copy(gu_bf[:, ofb * T + tb * 512: ofb * T + (tb + 1) * 512],
                               pstiles[tb][:, :])
            return last

        for e in range(32):
            s_si = p.sem("s_si")
            bsi = s_si.v

            def post_sc(ofb, scalar, bsi=bsi):
                if ofb < 4:
                    return
                c = ofb - 4
                op = scalar.activation(out=gu_bf[:, c * T:(c + 1) * T],
                                       in_=gu_bf[:, c * T:(c + 1) * T],
                                       func=AF.Silu)
                s_si.ti(op)

            def post_vec(vector, bsi=bsi):
                vch = p.sem("vchain")
                for c in range(4):
                    vector.wait_ge(s_si.h, bsi + c + 1)
                    vch.ti(vector.tensor_tensor(tf2[:, :], gu_bf[:, c * T:(c + 1) * T],
                                                gu_bf[:, (c + 4) * T:(c + 5) * T], ALU.mult))
                    vector.wait_ge(vch.h, vch.v)
                    vch.ti(vector.tensor_tensor(h_fm[:, c * T:(c + 1) * T],
                                                tf2[:, :], g_bc[:, :], ALU.mult))
                    vector.wait_ge(vch.h, vch.v)
            p.proj_fm(wgu[e], D, D, y_fm, evac_gu,
                      pre_dma=[(g_bc[:, :],
                                bass.AP(gdram.tensor, gdram.offset + e * T, [[0, 128], [1, T]]))],
                      post_scalar=post_sc, post_vector=post_vec)

            def evac_dn(ofb, pstiles, eng, s_ev=None, e=e):
                vec = eng["vector"]
                last = None
                for tb in range(NTB):
                    sl = acc_fm[:, ofb * T + tb * 512: ofb * T + (tb + 1) * 512]
                    if e == 0:
                        last = vec.tensor_copy(sl, pstiles[tb][:, :])
                    else:
                        last = vec.tensor_tensor(sl, sl, pstiles[tb][:, :], ALU.add)
                return last
            p.proj_fm(wd[e], 512, D, h_fm, evac_dn, evac_on="vector")

        ectx.close()
        p.finish_stream(x_in, acc_fm, x_out, a_cols=za[:, 0:NTT], b_cols=zb[:, 0:NTT])
    p.cur_ctx = None
    return p


def l0_ffn_inmaps(x_batch, inp):
    cwT = np.ascontiguousarray(inp["L0_ffn_classifier_weight"].T, np.float32)
    bias40 = np.ascontiguousarray(inp["L0_ffn_e_score_correction_bias"], np.float32).reshape(1, 40)
    wgu = np.ascontiguousarray(np.transpose(inp["L0_ffn_gate_up_proj"], (0, 2, 1)), np.float32)
    wd = np.ascontiguousarray(np.transpose(inp["L0_ffn_down_proj"], (0, 2, 1)), np.float32)
    base = dict(l0f_cwT=cwT, l0f_bias40=bias40, l0f_wgu=wgu, l0f_wd=wd, c_ident_in=IDENT128)
    return [dict(base, x=np.ascontiguousarray(xb, np.float32)) for xb in x_batch]


def declare_l0_attn_b(p):
    nc = p.nc
    return (
        p.dram_in("l0b_wqT", (D, D)),
        p.dram_in("l0b_bq", (1, D)),
        p.dram_in("l0b_wkT", (D, D)),
        p.dram_in("l0b_bk", (1, D)),
        p.dram_in("l0b_wvT", (D, D)),
        p.dram_in("l0b_bv", (1, D)),
        p.dram_in("l0b_woT", (D, D)),
        p.dram_in("l0b_bo", (1, D)),
        p.dram_in("l0b_orfT", (64, 256)),
    )


def build_l0_attn_b(p=None, xio=None, aps=None):
    standalone = p is None
    if standalone:
        p = P("l0attnb")
    nc = p.nc
    if aps is None:
        aps = declare_l0_attn_b(p)
    (wqT, bq, wkT, bk, wvT, bv, woT, bo, orfT) = aps
    x_in = xio[0] if xio else p.dram_in("x", (T, D))
    x_out = xio[1] if xio else p.dram_out("y", (T, D))
    DKQ = 0.3535533905932738
    EPS = 1e-6

    cm = p.ctx if standalone else ExitStack()
    with cm:
        if standalone:
            p.common(max_bias=1024)
        p.cur_ctx = cm
        p.begin_pool()
        o_fm = p.sbuf("o_fm", (128, NDC * T), BF16)
        orf_f = p.sbuf("orf_f", (128, 256), F32)
        orf_b = p.sbuf("orf_b", (128, 256), BF16)
        n16 = p.sbuf("n16", (128, 1), BF16)
        onecol = p.sbuf("onecol", (128, 1), BF16)
        sld = p.sem("sld")

        with nc.Block(no_gpsimd_drain=True) as blk:
            @blk.sync
            def _(sync):
                sld.ti(sync.dma_start(out=orf_f[0:64, :], in_=orfT[:, :]), 16)
                sld.ti(sync.dma_start(out=orf_f[64:128, :], in_=orfT[:, :]), 16)
                sync.wait_ge(sld.h, sld.v)

            @blk.vector
            def _(vector):
                vector.memset(n16[:], -1.0 / 16.0)
                vector.memset(onecol[:], 1.0)
        with nc.Block(no_gpsimd_drain=True) as blk:
            @blk.scalar
            def _(scalar):
                scalar.mul(orf_f[:, :], orf_f[:, :], DKQ)
        with nc.Block(no_gpsimd_drain=True) as blk:
            @blk.vector
            def _(vector):
                vector.tensor_copy(orf_b[:, :], orf_f[:, :])

        with ExitStack() as lnctx:
            y_fm = p.sbuf("y_fm", (128, NDC * T), BF16, ctx=lnctx)
            p.ln_stats_stream(x_in)
            p.ln_apply_to_fm(x_in, y_fm)

            with ExitStack() as pctx:
                q_bf = p.sbuf("q_bf", (128, T), BF16, ctx=pctx)
                k_bf = p.sbuf("k_bf", (128, T), BF16, ctx=pctx)
                v_bf = p.sbuf("v_bf", (128, T), BF16, ctx=pctx)
                v_tm = p.sbuf("v_tm", (128, NTT * 128), BF16, ctx=pctx)
                q2 = p.sbuf("q2", (128, T), BF16, ctx=pctx)
                phik = p.sbuf("phik", (128, NTT * 256), BF16, ctx=pctx)
                phiq = p.sbuf("phiq", (128, NTT * 256), BF16, ctx=pctx)
                phiqT = p.sbuf("phiqT", (128, 2 * T), BF16, ctx=pctx)
                bufh = p.sbuf("bufh", (128, 2 * 65), BF16, ctx=pctx)
                exps = [p.sbuf("exps", (128, 256), F32, ctx=pctx) for _ in range(2)]
                rc = p.sbuf("rc", (128, 1), F32, ctx=pctx)
                ob = p.sbuf("ob", (128, 128), BF16, ctx=pctx)

                def evac_to(dst):
                    def ev(ofb, pstiles, eng):
                        sc = eng["scalar"]
                        last = None
                        for tb in range(NTB):
                            last = sc.copy(dst[:, tb * 512:(tb + 1) * 512], pstiles[tb][:, :])
                        return last
                    return ev

                def feature_map(qk, hb, phi_dst):
                    with nc.Block(no_gpsimd_drain=True) as blk:
                        @blk.vector
                        def _(vector):
                            vector.tensor_tensor(q2[hb:hb + 64, :], qk[hb:hb + 64, :],
                                                 qk[hb:hb + 64, :], ALU.mult)
                    with ExitStack() as actx:
                        pp = [p.psum("ppb", (128, 256), F32, ctx=actx) for _ in range(2)]
                        pn = [p.psum("pnb", (128, 1), F32, ctx=actx) for _ in range(2)]
                        ncol = [p.sbuf("ncol", (128, 1), F32, ctx=actx) for _ in range(2)]
                        s_m = p.sem("ab_m")
                        s_n = p.sem("ab_n")
                        s_x = p.sem("ab_x")
                        s_v = p.sem("ab_v")
                        bm, bn, bx, bv_ = s_m.v, s_n.v, s_x.v, s_v.v
                        with nc.Block(no_gpsimd_drain=True) as blk:
                            @blk.tensor
                            def _(tensor):
                                for i in range(NTT):
                                    if i >= 2:
                                        tensor.wait_ge(s_x.h, bx + i - 1)
                                        tensor.wait_ge(s_n.h, bn + 2 * (i - 1))
                                    tensor.matmul(pn[i % 2][:, :],
                                                  q2[hb:hb + 64, i * 128:(i + 1) * 128],
                                                  n16[hb:hb + 64, :], start=True, stop=True)
                                    s_m.ti(tensor.matmul(pp[i % 2][:, :],
                                                         qk[hb:hb + 64, i * 128:(i + 1) * 128],
                                                         orf_b[hb:hb + 64, :], start=True, stop=True))

                            @blk.vector
                            def _(vector):
                                for i in range(NTT):
                                    vector.wait_ge(s_m.h, bm + i + 1)
                                    if i >= 2:
                                        vector.wait_ge(s_x.h, bx + i - 1)
                                    s_n.ti(vector.tensor_copy(ncol[i % 2][:, :], pn[i % 2][:, :]))
                                    vector.wait_ge(s_x.h, bx + i + 1)
                                    s_n.ti(vector.tensor_scalar(
                                        phi_dst[:, i * 256:(i + 1) * 256],
                                        exps[i % 2][:, :], EPS, None, ALU.add))

                            @blk.scalar
                            def _(scalar):
                                for i in range(NTT):
                                    scalar.wait_ge(s_n.h, bn + 2 * i + 1)
                                    if i >= 2:
                                        scalar.wait_ge(s_n.h, bn + 2 * (i - 1) + 2)
                                    s_x.ti(scalar.activation(out=exps[i % 2][:, :],
                                                             in_=pp[i % 2][:, :], func=AF.Exp,
                                                             bias=ncol[i % 2][:, :]))

                for hp in range(8):
                    sl = slice(hp * 128, (hp + 1) * 128)
                    p.proj_fm(wqT[:, sl], D, 128, y_fm, evac_to(q_bf), bias_hbm_row=bq[0:1, sl])
                    p.proj_fm(wkT[:, sl], D, 128, y_fm, evac_to(k_bf), bias_hbm_row=bk[0:1, sl])
                    p.proj_fm(wvT[:, sl], D, 128, y_fm, evac_to(v_bf), bias_hbm_row=bv[0:1, sl])
                    items = [(v_bf[:, i * 128:(i + 1) * 128], v_tm[:, i * 128:(i + 1) * 128])
                             for i in range(NTT)]
                    p.transpose_bf(items)

                    for h2 in range(2):
                        hb = 64 * h2
                        feature_map(k_bf, hb, phik)
                        feature_map(q_bf, hb, phiq)
                        with ExitStack() as actx:
                            pk = [p.psum("pkv", (128, 65), F32, ctx=actx) for _ in range(2)]
                            s_m = p.sem("ab_m2")
                            bm = s_m.v
                            with nc.Block(no_gpsimd_drain=True) as blk:
                                @blk.tensor
                                def _(tensor):
                                    for mh in range(2):
                                        for i in range(NTT):
                                            tensor.matmul(pk[mh][:, 0:64],
                                                          phik[:, i * 256 + mh * 128: i * 256 + (mh + 1) * 128],
                                                          v_tm[:, i * 128:(i + 1) * 128][:, hb:hb + 64],
                                                          start=(i == 0), stop=(i == NTT - 1))
                                        for i in range(NTT):
                                            last = tensor.matmul(pk[mh][:, 64:65],
                                                                 phik[:, i * 256 + mh * 128: i * 256 + (mh + 1) * 128],
                                                                 onecol[:, :],
                                                                 start=(i == 0), stop=(i == NTT - 1))
                                        s_m.ti(last)

                                @blk.vector
                                def _(vector):
                                    for mh in range(2):
                                        vector.wait_ge(s_m.h, bm + mh + 1)
                                        vector.tensor_copy(bufh[:, mh * 65:(mh + 1) * 65], pk[mh][:, :])
                        items = [(phiq[:, i * 256 + mh * 128: i * 256 + (mh + 1) * 128],
                                  phiqT[:, mh * T + i * 128: mh * T + (i + 1) * 128])
                                 for i in range(NTT) for mh in range(2)]
                        p.transpose_bf(items)
                        with ExitStack() as actx:
                            po = [p.psum("po", (128, 65), F32, ctx=actx) for _ in range(2)]
                            pt = [p.psum("pot", (64, 128), BF16, ctx=actx) for _ in range(2)]
                            obr = [p.sbuf("obr", (128, 64), BF16, ctx=actx) for _ in range(2)]
                            s_m = p.sem("ab_m3")
                            s_dv = p.sem("ab_dv")
                            s_tp = p.sem("ab_tp")
                            s_ev = p.sem("ab_ev")
                            bm, bdv, btp, bev = s_m.v, s_dv.v, s_tp.v, s_ev.v
                            with nc.Block(no_gpsimd_drain=True) as blk:
                                @blk.tensor
                                def _(tensor):
                                    for i in range(NTT):
                                        if i >= 2:
                                            tensor.wait_ge(s_dv.h, bdv + i - 1)
                                        for mh in range(2):
                                            mm = tensor.matmul(po[i % 2][:, :],
                                                               phiqT[:, mh * T + i * 128: mh * T + (i + 1) * 128],
                                                               bufh[:, mh * 65:(mh + 1) * 65],
                                                               start=(mh == 0), stop=(mh == 1))
                                        s_m.ti(mm)
                                        tensor.wait_ge(s_dv.h, bdv + i + 1)
                                        if i >= 2:
                                            tensor.wait_ge(s_ev.h, bev + i - 1)
                                        s_tp.ti(tensor.transpose(pt[i % 2][:, :], obr[i % 2][:, :],
                                                                 p.ident[:]))

                                @blk.vector
                                def _(vector):
                                    vch = p.sem("vchain")
                                    for i in range(NTT):
                                        vector.wait_ge(s_m.h, bm + i + 1)
                                        vch.ti(vector.reciprocal(rc[:, :], po[i % 2][:, 64:65]))
                                        vector.wait_ge(vch.h, vch.v)
                                        if i >= 2:
                                            vector.wait_ge(s_tp.h, btp + i - 1)
                                        s_dv.ti(vector.tensor_scalar(obr[i % 2][:, :], po[i % 2][:, 0:64],
                                                                     rc[:, 0:1], None, ALU.mult))
                                        vector.wait_ge(s_tp.h, btp + i + 1)
                                        s_ev.ti(vector.tensor_copy(
                                            o_fm[hb:hb + 64, hp * T + i * 128: hp * T + (i + 1) * 128],
                                            pt[i % 2][:, :]))

            with ExitStack() as wctx:
                acc_fm = p.sbuf("acc_fm", (128, NDC * T), F32, ctx=wctx)

                def evac_acc2(ofb, pstiles, eng):
                    sc = eng["scalar"]
                    last = None
                    for tb in range(NTB):
                        last = sc.copy(acc_fm[:, ofb * T + tb * 512: ofb * T + (tb + 1) * 512],
                                       pstiles[tb][:, :])
                    return last
                p.proj_fm(woT, D, D, o_fm, evac_acc2, bias_hbm_row=bo[0:1, :])
                p.finish_stream(x_in, acc_fm, x_out)
    p.cur_ctx = None
    return p


def l0_attn_b_inmaps(x_batch, inp):
    def t(a):
        return np.ascontiguousarray(np.asarray(a, np.float32).T)
    base = dict(l0b_wqT=t(inp["L0_attn_b_Wq"]), l0b_bq=inp["L0_attn_b_bq"].reshape(1, D).astype(np.float32),
                l0b_wkT=t(inp["L0_attn_b_Wk"]), l0b_bk=inp["L0_attn_b_bk"].reshape(1, D).astype(np.float32),
                l0b_wvT=t(inp["L0_attn_b_Wv"]), l0b_bv=inp["L0_attn_b_bv"].reshape(1, D).astype(np.float32),
                l0b_woT=t(inp["L0_attn_b_Wo"]), l0b_bo=inp["L0_attn_b_bo"].reshape(1, D).astype(np.float32),
                l0b_orfT=t(inp["L0_attn_b_orf"]), c_ident_in=IDENT128)
    return [dict(base, x=np.ascontiguousarray(xb, np.float32)) for xb in x_batch]


def declare_l1_attn_a(p):
    nc = p.nc
    return (
        p.dram_in("l1a_a_row", (1, 64)),
        p.dram_in("l1a_bT", (D, 64)),
        p.dram_in("l1a_cmat", (D, 64)),
        p.dram_in("l1a_dcols", (128, NDC)),
        p.dram_in("l1a_ldt_row", (1, D)),
    )


def build_l1_attn_a(p=None, xio=None, aps=None):
    standalone = p is None
    if standalone:
        p = P("l1attna")
    nc = p.nc
    if aps is None:
        aps = declare_l1_attn_a(p)
    (a_row, bT, cmat, dcols, ldt_row) = aps
    x_in = xio[0] if xio else p.dram_in("x", (T, D))
    x_out = xio[1] if xio else p.dram_out("y", (T, D))
    L = 64

    cm = p.ctx if standalone else ExitStack()
    with cm:
        if standalone:
            p.common(max_bias=128)
        p.cur_ctx = cm
        p.begin_pool()
        acc_fm = p.sbuf("acc_fm", (128, NDC * T), F32)
        u_fm = p.sbuf("u_fm", (128, NDC * T), BF16)
        kT = p.sbuf("kT", (128, NDC * L), F32)
        dcol_sb = p.sbuf("dcol_sb", (128, NDC), F32)
        abT = p.sbuf("abT", (128, NDC * 64), F32)
        pw_ = p.sbuf("pw_", (128, NDC * 64), F32)
        gT = p.sbuf("gT", (128, NDC * 64), F32)
        csb = p.sbuf("csb", (128, 64), F32)
        dt_row = p.sbuf("dt_row", (1, D), F32)
        a_sb = p.sbuf("a_sb", (1, 64), F32)
        onec = p.sbuf("onec", (128, 1), F32)
        onesr = p.sbuf("onesr", (1, 128), F32)
        tscr = p.sbuf("tscr", (128, 64), F32)
        sld = p.sem("sld")

        with nc.Block(no_gpsimd_drain=True) as blk:
            @blk.sync
            def _(sync):
                sld.ti(sync.dma_start(out=dt_row[:, :], in_=ldt_row[:, :]), 16)
                sld.ti(sync.dma_start(out=a_sb[:, :], in_=a_row[:, :]), 16)
                sld.ti(sync.dma_start(out=dcol_sb[:, :], in_=dcols[:, :]), 16)
                sync.wait_ge(sld.h, sld.v)

            @blk.vector
            def _(vector):
                vector.memset(onec[:], 1.0)
                vector.memset(onesr[:], 1.0)
        with nc.Block(no_gpsimd_drain=True) as blk:
            @blk.scalar
            def _(scalar):
                scalar.activation(out=dt_row[:, :], in_=dt_row[:, :], func=AF.Exp)
        with ExitStack() as actx:
            pab = p.psum("pab", (128, 64), F32, ctx=actx)
            s_m, s_x = p.sem("aa_m"), p.sem("aa_x")
            bm, bx = s_m.v, s_x.v
            with nc.Block(no_gpsimd_drain=True) as blk:
                @blk.tensor
                def _(tensor):
                    for c in range(NDC):
                        if c >= 1:
                            tensor.wait_ge(s_x.h, bx + c)
                        s_m.ti(tensor.matmul(pab[:, :], dt_row[0:1, c * 128:(c + 1) * 128],
                                             a_sb[0:1, :], start=True, stop=True))

                @blk.scalar
                def _(scalar):
                    for c in range(NDC):
                        scalar.wait_ge(s_m.h, bm + c + 1)
                        s_x.ti(scalar.activation(out=abT[:, c * 64:(c + 1) * 64],
                                                 in_=pab[:, :], func=AF.Exp))
        with ExitStack() as actx:
            wc8 = p.sbuf("wc8", (128, NDC * 64), F32, ctx=actx)
            pcs = p.psum("pcs2", (1, 64), F32, ctx=actx)
            pcb = p.psum("pcb2", (128, 64), F32, ctx=actx)
            csr = p.sbuf("csr2", (1, 64), F32, ctx=actx)
            s_m = p.sem("aa_m3")
            bm = s_m.v
            with nc.Block(no_gpsimd_drain=True) as blk:
                @blk.sync
                def _(sync):
                    for c in range(NDC):
                        sld.ti(sync.dma_start(out=wc8[:, c * 64:(c + 1) * 64],
                                              in_=cmat[c * 128:(c + 1) * 128, :]), 16)
                    sync.wait_ge(sld.h, sld.v)
            with nc.Block(no_gpsimd_drain=True) as blk:
                @blk.tensor
                def _(tensor):
                    for c in range(NDC):
                        s_m.ti(tensor.matmul(pcs[:, :], onec[:, :], wc8[:, c * 64:(c + 1) * 64],
                                             start=(c == 0), stop=(c == NDC - 1)))

                @blk.vector
                def _(vector):
                    vector.wait_ge(s_m.h, bm + NDC)
                    vector.tensor_copy(csr[:, :], pcs[:, :])
            with nc.Block(no_gpsimd_drain=True) as blk:
                @blk.tensor
                def _(tensor):
                    tensor.matmul(pcb[:, :], onesr[0:1, :], csr[0:1, :], start=True, stop=True)
            with nc.Block(no_gpsimd_drain=True) as blk:
                @blk.vector
                def _(vector):
                    vector.tensor_copy(csb[:, :], pcb[:, :])
        with ExitStack() as actx:
            wb8 = p.sbuf("wb8", (128, NDC * 64), F32, ctx=actx)
            with nc.Block(no_gpsimd_drain=True) as blk:
                @blk.sync
                def _(sync):
                    for c in range(NDC):
                        sld.ti(sync.dma_start(out=wb8[:, c * 64:(c + 1) * 64],
                                              in_=bT[c * 128:(c + 1) * 128, :]), 16)
                    sync.wait_ge(sld.h, sld.v)
            with nc.Block(no_gpsimd_drain=True) as blk:
                @blk.vector
                def _(vector):
                    for c in range(NDC):
                        vector.tensor_tensor(gT[:, c * 64:(c + 1) * 64],
                                             wb8[:, c * 64:(c + 1) * 64], csb[:, :], ALU.mult)
                    vector.memset(pw_[:], 1.0)
        with nc.Block(no_gpsimd_drain=True) as blk:
            @blk.vector
            def _(vector):
                vch = p.sem("vchain")
                for l in range(L):
                    for c in range(NDC):
                        vch.ti(vector.tensor_tensor(tscr[:, :], gT[:, c * 64:(c + 1) * 64],
                                                    pw_[:, c * 64:(c + 1) * 64], ALU.mult))
                        vector.wait_ge(vch.h, vch.v)
                        vch.ti(vector.reduce_sum(kT[:, c * L + l: c * L + l + 1], tscr[:, :], axis=AX.X))
                        vector.wait_ge(vch.h, vch.v)
                    if l < L - 1:
                        for c in range(NDC):
                            vch.ti(vector.tensor_tensor(pw_[:, c * 64:(c + 1) * 64],
                                                        pw_[:, c * 64:(c + 1) * 64],
                                                        abT[:, c * 64:(c + 1) * 64], ALU.mult))
                            vector.wait_ge(vch.h, vch.v)

        p.ln_stats_stream(x_in)
        p.ln_apply_to_fm(x_in, u_fm)

        with nc.Block(no_gpsimd_drain=True) as blk:
            @blk.vector
            def _(vector):
                for c in range(NDC):
                    vector.memset(acc_fm[:, c * T:(c + 1) * T], 0.0)
        with nc.Block(no_gpsimd_drain=True) as blk:
            @blk.vector
            def _(vector):
                vch = p.sem("vchain")
                for l in range(L):
                    sh = 31 - l
                    lo = max(0, -sh)
                    hi = min(T, T - sh)
                    for c in range(NDC):
                        vch.ti(vector.scalar_tensor_tensor(
                            out=acc_fm[:, c * T + lo: c * T + hi],
                            in0=u_fm[:, c * T + lo + sh: c * T + hi + sh],
                            scalar=kT[:, c * L + l: c * L + l + 1],
                            in1=acc_fm[:, c * T + lo: c * T + hi],
                            op0=ALU.mult, op1=ALU.add))
                        vector.wait_ge(vch.h, vch.v)
                for c in range(NDC):
                    vch.ti(vector.scalar_tensor_tensor(
                        out=acc_fm[:, c * T:(c + 1) * T],
                        in0=u_fm[:, c * T:(c + 1) * T],
                        scalar=dcol_sb[:, c:c + 1],
                        in1=acc_fm[:, c * T:(c + 1) * T],
                        op0=ALU.mult, op1=ALU.add))
                    vector.wait_ge(vch.h, vch.v)

        p.finish_stream(x_in, acc_fm, x_out)
    p.cur_ctx = None
    return p


def l1_attn_a_inmaps(x_batch, inp):
    base = dict(l1a_a_row=np.asarray(inp["L1_attn_a_A"], np.float32).reshape(1, 64),
                l1a_bT=np.ascontiguousarray(np.asarray(inp["L1_attn_a_B"], np.float32).T),
                l1a_cmat=np.ascontiguousarray(np.asarray(inp["L1_attn_a_C"], np.float32)),
                l1a_dcols=np.ascontiguousarray(np.asarray(inp["L1_attn_a_D"], np.float32).reshape(NDC, 128).T),
                l1a_ldt_row=np.asarray(inp["L1_attn_a_log_dt"], np.float32).reshape(1, D),
                c_ident_in=IDENT128)
    return [dict(base, x=np.ascontiguousarray(xb, np.float32)) for xb in x_batch]


DI = 2048
NDI = DI // 128


def declare_l1_attn_b(p):
    nc = p.nc
    return (
        p.dram_in("l1b_inpT", (D, 2 * DI)),
        p.dram_in("l1b_wconv", (128, NDI * 4)),
        p.dram_in("l1b_bconv", (128, NDI)),
        p.dram_in("l1b_w264", (DI, 264)),
        p.dram_in("l1b_wdtrow", (1, DI)),
        p.dram_in("l1b_bdtcols", (128, NDI)),
        p.dram_in("l1b_alog", (1, 128)),
        p.dram_in("l1b_nwcols", (128, NDI)),
        p.dram_in("l1b_outT", (DI, D)),
        nc.dram_tensor("l1b_xin_scr", [128, NDI * T], BF16, kind="Internal").ap(),
        nc.dram_tensor("l1b_z_scr", [128, NDI * T], BF16, kind="Internal").ap(),
        nc.dram_tensor("l1b_xc_scr", [128, NDI * T], BF16, kind="Internal").ap(),
        nc.dram_tensor("l1b_bc_scr", [256, T], BF16, kind="Internal").ap(),
        nc.dram_tensor("l1b_rs_scr", [1, T], F32, kind="Internal").ap(),
        nc.dram_tensor("l1b_yh_scr", [128, NDI * T], F32, kind="Internal").ap(),
    )


def build_l1_attn_b(p=None, xio=None, aps=None):
    standalone = p is None
    if standalone:
        p = P("l1attnb")
    nc = p.nc
    if aps is None:
        aps = declare_l1_attn_b(p)
    (inpT, wconv, bconv, w264, wdtrow, bdtcols, alog, nwcols, outT, xind, zd, xcd, bcd, rsd, yhd) = aps
    x_in = xio[0] if xio else p.dram_in("x", (T, D))
    x_out = xio[1] if xio else p.dram_out("y", (T, D))
    F32EPS = 1.1920929e-07

    cm = p.ctx if standalone else ExitStack()
    with cm:
        if standalone:
            p.common(max_nk=16, max_bias=128)
        p.cur_ctx = cm
        p.begin_pool()
        dtr_row = p.sbuf("dtr_row", (1, T), F32)
        dtr_bf = p.sbuf("dtr_bf", (1, T), BF16)
        wdt_bf = p.sbuf("wdt_bf", (1, DI), BF16)
        a_bc = p.sbuf("a_bc", (128, 128), F32)
        bdt_sb = p.sbuf("bdt_sb", (128, NDI), F32)
        bcv_sb = p.sbuf("bcv_sb", (128, NDI), F32)
        wcv_sb = p.sbuf("wcv_sb", (128, NDI * 4), F32)
        nw_sb = p.sbuf("nw_sb", (128, NDI), F32)
        sums = p.sbuf("sums", (1, T), F32)
        onesr = p.sbuf("onesr2", (1, 128), F32)
        onec = p.sbuf("onec2", (128, 1), F32)
        sld = p.sem("sld")

        with nc.Block(no_gpsimd_drain=True) as blk:
            @blk.sync
            def _(sync):
                sld.ti(sync.dma_start(out=bdt_sb[:, :], in_=bdtcols[:, :]), 16)
                sld.ti(sync.dma_start(out=bcv_sb[:, :], in_=bconv[:, :]), 16)
                sld.ti(sync.dma_start(out=wcv_sb[:, :], in_=wconv[:, :]), 16)
                sld.ti(sync.dma_start(out=nw_sb[:, :], in_=nwcols[:, :]), 16)
                sld.ti(sync.dma_start(out=p.junk[0:1, 0:128], in_=alog[:, :]), 16)
                sync.wait_ge(sld.h, sld.v)

            @blk.vector
            def _(vector):
                vector.memset(onesr[:], 1.0)
                vector.memset(onec[:], 1.0)
        with nc.Block(no_gpsimd_drain=True) as blk:
            @blk.scalar
            def _(scalar):
                scalar.activation(out=p.junk[0:1, 128:256], in_=p.junk[0:1, 0:128], func=AF.Exp)
        with nc.Block(no_gpsimd_drain=True) as blk:
            @blk.vector
            def _(vector):
                vector.tensor_scalar_mul(p.junk[0:1, 128:256], p.junk[0:1, 128:256], -1.0)
        with ExitStack() as actx:
            pab = p.psum("pab2", (128, 128), F32, ctx=actx)
            with nc.Block(no_gpsimd_drain=True) as blk:
                @blk.tensor
                def _(tensor):
                    tensor.matmul(pab[:, :], onesr[0:1, :], p.junk[0:1, 128:256],
                                  start=True, stop=True)
            with nc.Block(no_gpsimd_drain=True) as blk:
                @blk.vector
                def _(vector):
                    vector.tensor_copy(a_bc[:, :], pab[:, :])

        with ExitStack() as s1:
            y_fm = p.sbuf("y_fm", (128, NDC * T), BF16, ctx=s1)
            p.ln_stats_stream(x_in)
            p.ln_apply_to_fm(x_in, y_fm)
            z_bf = p.sbuf("z_bf", (128, NDI * T // 2), BF16, ctx=s1)
            xin_bf = p.sbuf("xin_bf", (128, NDI * T // 2), BF16, ctx=s1)

            def mk_evac(dst):
                def ev(ofb, pstiles, eng):
                    sc = eng["scalar"]
                    last = None
                    for tb in range(NTB):
                        last = sc.copy(dst[:, ofb * T + tb * 512: ofb * T + (tb + 1) * 512],
                                       pstiles[tb][:, :])
                    return last
                return ev
            for half in range(2):
                p.proj_fm(inpT[:, half * 1024:(half + 1) * 1024], D, 1024, y_fm,
                          mk_evac(xin_bf))
                with nc.Block(no_gpsimd_drain=True) as blk:
                    @blk.sync
                    def _(sync):
                        for c in range(8):
                            sld.ti(sync.dma_start(out=xind[:, (half * 8 + c) * T:(half * 8 + c + 1) * T],
                                                  in_=xin_bf[:, c * T:(c + 1) * T]), 16)
                        sync.wait_ge(sld.h, sld.v)
                p.proj_fm(inpT[:, DI + half * 1024: DI + (half + 1) * 1024], D, 1024, y_fm,
                          mk_evac(z_bf))
                with nc.Block(no_gpsimd_drain=True) as blk:
                    @blk.sync
                    def _(sync):
                        for c in range(8):
                            sld.ti(sync.dma_start(out=zd[:, (half * 8 + c) * T:(half * 8 + c + 1) * T],
                                                  in_=z_bf[:, c * T:(c + 1) * T]), 16)
                        sync.wait_ge(sld.h, sld.v)

        with ExitStack() as s2:
            xc_bf = p.sbuf("xc_bf", (128, NDI * T), BF16, ctx=s2)
            cvin = [p.sbuf("cvin", (128, T), BF16, ctx=s2) for _ in range(2)]
            cvac = [p.sbuf("cvac", (128, T), F32, ctx=s2) for _ in range(2)]
            s_cd = [p.sem("cv_d%d" % j) for j in range(2)]
            s_cv = p.sem("cv_v")
            s_cs = p.sem("cv_s")
            bcd_ = [x.v for x in s_cd]
            bcv, bcs = s_cv.v, s_cs.v
            with nc.Block(no_gpsimd_drain=True) as blk:
                @blk.sync
                def _(sync):
                    for c in range(NDI):
                        if c >= 2:
                            sync.wait_ge(s_cv.h, bcv + c - 1)
                        s_cd[c % 2].ti(sync.dma_start(out=cvin[c % 2][:, :],
                                                      in_=xind[:, c * T:(c + 1) * T]), 16)

                @blk.vector
                def _(vector):
                    vch = p.sem("vchain")
                    for c in range(NDI):
                        vector.wait_ge(s_cd[c % 2].h, bcd_[c % 2] + (c // 2 + 1) * 16)
                        if c >= 2:
                            vector.wait_ge(s_cs.h, bcs + c - 1)
                        vch.ti(vector.tensor_scalar(cvac[c % 2][:, :], cvin[c % 2][:, :],
                                                    wcv_sb[:, c * 4 + 3:c * 4 + 4], None, ALU.mult))
                        vector.wait_ge(vch.h, vch.v)
                        for j in range(3):
                            sh = 3 - j
                            op = vector.scalar_tensor_tensor(
                                out=cvac[c % 2][:, sh:T],
                                in0=cvin[c % 2][:, 0:T - sh],
                                scalar=wcv_sb[:, c * 4 + j:c * 4 + j + 1],
                                in1=cvac[c % 2][:, sh:T],
                                op0=ALU.mult, op1=ALU.add)
                            if j < 2:
                                vch.ti(op)
                                vector.wait_ge(vch.h, vch.v)
                            else:
                                s_cv.ti(op)

                @blk.scalar
                def _(scalar):
                    for c in range(NDI):
                        scalar.wait_ge(s_cv.h, bcv + c + 1)
                        s_cs.ti(scalar.activation(out=xc_bf[:, c * T:(c + 1) * T],
                                                  in_=cvac[c % 2][:, :], func=AF.Silu,
                                                  bias=bcv_sb[:, c:c + 1]))

            xdb = p.sbuf("xdb", (128, NTT * 264), F32, ctx=s2)
            p.proj_tm(w264, DI, 264, xc_bf, xdb, x_w=T)
            with ExitStack() as s3:
                ptf = [p.psum("ptf", (128, 128), F32, ctx=s3) for _ in range(2)]
                bfm = p.sbuf("bfm", (128, T), BF16, ctx=s3)
                cfm = p.sbuf("cfm", (128, T), BF16, ctx=s3)
                s_t3 = p.sem("s_t3")
                s_v3 = p.sem("s_v3")
                bt3, bv3 = s_t3.v, s_v3.v
                items = []
                for i in range(NTT):
                    items.append((xdb[:, i * 264 + 8:i * 264 + 136], bfm[:, i * 128:(i + 1) * 128], None))
                    items.append((xdb[:, i * 264 + 136:i * 264 + 264], cfm[:, i * 128:(i + 1) * 128], None))
                    items.append((xdb[:, i * 264 + 0:i * 264 + 128], dtr_row[0:1, i * 128:(i + 1) * 128], "row0"))
                with nc.Block(no_gpsimd_drain=True) as blk:
                    @blk.tensor
                    def _(tensor):
                        for k, (src, dst, mode) in enumerate(items):
                            if k >= 2:
                                tensor.wait_ge(s_v3.h, bv3 + k - 1)
                            s_t3.ti(tensor.transpose(ptf[k % 2][:, :], src, p.identf[:]))

                    @blk.vector
                    def _(vector):
                        for k, (src, dst, mode) in enumerate(items):
                            vector.wait_ge(s_t3.h, bt3 + k + 1)
                            if mode == "row0":
                                s_v3.ti(vector.tensor_copy(dst, ptf[k % 2][0:1, :]))
                            else:
                                s_v3.ti(vector.tensor_copy(dst, ptf[k % 2][:, :]))
                with nc.Block(no_gpsimd_drain=True) as blk:
                    @blk.sync
                    def _(sync):
                        sld.ti(sync.dma_start(out=bcd[0:128, :], in_=bfm[:, :]), 16)
                        sld.ti(sync.dma_start(out=bcd[128:256, :], in_=cfm[:, :]), 16)
                        sync.wait_ge(sld.h, sld.v)

                    @blk.vector
                    def _(vector):
                        vector.tensor_copy(dtr_bf[:, :], dtr_row[:, :])
            with nc.Block(no_gpsimd_drain=True) as blk:
                @blk.sync
                def _(sync):
                    for c in range(NDI):
                        sld.ti(sync.dma_start(out=xcd[:, c * T:(c + 1) * T],
                                              in_=xc_bf[:, c * T:(c + 1) * T]), 16)
                    sync.wait_ge(sld.h, sld.v)
        with ExitStack() as swd:
            wdt_f = p.sbuf("wdt_f", (1, DI), F32, ctx=swd)
            with nc.Block(no_gpsimd_drain=True) as blk:
                @blk.sync
                def _(sync):
                    sld.ti(sync.dma_start(out=wdt_f[0:1, :], in_=wdtrow[:, :]), 16)
                    sync.wait_ge(sld.h, sld.v)
            with nc.Block(no_gpsimd_drain=True) as blk:
                @blk.vector
                def _(vector):
                    vector.tensor_copy(wdt_bf[0:1, :], wdt_f[0:1, :])

        ln1c = p.const_col(1.0)
        HT = T // 2
        for dh in range(2):
            c0 = dh * 8
            with ExitStack() as sh:
                h_bound = p.sbuf("h_bound", (128, 8 * 128), F32, ctx=sh)
                for th in range(2):
                    t0 = th * HT
                    with ExitStack() as sq:
                        dt_bf = p.sbuf("dt_bf", (128, 8 * HT), BF16, ctx=sq)
                        dtu_bf = p.sbuf("dtu_bf", (128, 8 * HT), BF16, ctx=sq)
                        yh = p.sbuf("yh", (128, 8 * HT), F32, ctx=sq)
                        bbc = [p.sbuf("bbc", (128, HT), BF16, ctx=sq) for _ in range(2)]
                        cbc = [p.sbuf("cbc", (128, HT), BF16, ctx=sq) for _ in range(2)]
                        dar = [p.sbuf("dar", (128, HT), F32, ctx=sq) for _ in range(2)]
                        dbr = [p.sbuf("dbr", (128, HT), BF16, ctx=sq) for _ in range(2)]
                        hsr = [p.sbuf("hsr", (128, HT), F32, ctx=sq) for _ in range(2)]
                        t1r = [p.sbuf("t1r", (128, HT), BF16, ctx=sq) for _ in range(2)]
                        spa = p.sbuf("spa", (128, 512), F32, ctx=sq)
                        spb = p.sbuf("spb", (128, 512), F32, ctx=sq)
                        xcl = [p.sbuf("xcl", (128, HT), BF16, ctx=sq) for _ in range(2)]

                        with ExitStack() as sdt:
                            pdt = [p.psum("pdt", (128, 512), F32, ctx=sdt) for _ in range(2)]
                            s_m4, s_x4 = p.sem("dt_m"), p.sem("dt_x")
                            s_vd = p.sem("dt_vd")
                            bm4, bx4, bvd = s_m4.v, s_x4.v, s_vd.v
                            units = [(c, tb) for c in range(8) for tb in range(HT // 512)]
                            with nc.Block(no_gpsimd_drain=True) as blk:
                                @blk.tensor
                                def _(tensor):
                                    for k, (c, tb) in enumerate(units):
                                        if k >= 2:
                                            tensor.wait_ge(s_x4.h, bx4 + k - 1)
                                        s_m4.ti(tensor.matmul(
                                            pdt[k % 2][:, :],
                                            wdt_bf[0:1, (c0 + c) * 128:(c0 + c + 1) * 128],
                                            dtr_bf[0:1, t0 + tb * 512: t0 + (tb + 1) * 512],
                                            start=True, stop=True))

                                @blk.scalar
                                def _(scalar):
                                    sch = p.sem("schain")
                                    for k, (c, tb) in enumerate(units):
                                        scalar.wait_ge(s_m4.h, bm4 + k + 1)
                                        if k >= 1:
                                            scalar.wait_ge(s_vd.h, bvd + k)
                                        bcol = bdt_sb[:, c0 + c:c0 + c + 1]
                                        sch.ti(scalar.activation(out=spa[:, :], in_=pdt[k % 2][:, :],
                                                                 func=AF.Abs, bias=bcol))
                                        scalar.wait_ge(sch.h, sch.v)
                                        sch.ti(scalar.activation(out=spa[:, :], in_=spa[:, :],
                                                                 func=AF.Exp, scale=-1.0))
                                        scalar.wait_ge(sch.h, sch.v)
                                        sch.ti(scalar.activation(out=spa[:, :], in_=spa[:, :],
                                                                 func=AF.Ln, bias=ln1c))
                                        scalar.wait_ge(sch.h, sch.v)
                                        s_x4.ti(scalar.activation(out=spb[:, :], in_=pdt[k % 2][:, :],
                                                                  func=AF.Relu, bias=bcol))

                                @blk.vector
                                def _(vector):
                                    for k, (c, tb) in enumerate(units):
                                        vector.wait_ge(s_x4.h, bx4 + k + 1)
                                        s_vd.ti(vector.tensor_tensor(
                                            dt_bf[:, c * HT + tb * 512: c * HT + (tb + 1) * 512],
                                            spa[:, :], spb[:, :], ALU.add))
                        with ExitStack() as sdu:
                            s_d5 = [p.sem("du_d%d" % j) for j in range(2)]
                            s_v5 = p.sem("du_v")
                            bd5 = [x.v for x in s_d5]
                            bv5 = s_v5.v
                            with nc.Block(no_gpsimd_drain=True) as blk:
                                @blk.sync
                                def _(sync):
                                    for c in range(8):
                                        if c >= 2:
                                            sync.wait_ge(s_v5.h, bv5 + c - 1)
                                        s_d5[c % 2].ti(sync.dma_start(out=xcl[c % 2][:, :],
                                                       in_=xcd[:, (c0 + c) * T + t0:(c0 + c) * T + t0 + HT]), 16)

                                @blk.vector
                                def _(vector):
                                    for c in range(8):
                                        vector.wait_ge(s_d5[c % 2].h, bd5[c % 2] + (c // 2 + 1) * 16)
                                        s_v5.ti(vector.tensor_tensor(dtu_bf[:, c * HT:(c + 1) * HT],
                                                                     dt_bf[:, c * HT:(c + 1) * HT],
                                                                     xcl[c % 2][:, :], ALU.mult))
                        with nc.Block(no_gpsimd_drain=True) as blk:
                            @blk.vector
                            def _(vector):
                                for c in range(8):
                                    vector.memset(yh[:, c * HT:(c + 1) * HT], 0.0)

                        s_bc = [p.sem("sc_bc%d" % j) for j in range(2)]
                        s_da = p.sem("sc_da")
                        s_db = p.sem("sc_db")
                        s_hs = p.sem("sc_hs")
                        s_t1 = p.sem("sc_t1")
                        s_ya = p.sem("sc_ya")
                        bbc_ = [x.v for x in s_bc]
                        bda, bdb, bhs, bt1, bya = s_da.v, s_db.v, s_hs.v, s_t1.v, s_ya.v
                        with nc.Block() as blk:
                            @blk.sync
                            def _(sync):
                                for si in range(128):
                                    if si >= 2:
                                        sync.wait_ge(s_db.h, bdb + 8 * (si - 1))
                                        sync.wait_ge(s_t1.h, bt1 + 8 * (si - 1))
                                    src_b = bass.AP(bcd.tensor, bcd.offset + si * T + t0, [[0, 128], [1, HT]])
                                    src_c = bass.AP(bcd.tensor, bcd.offset + (128 + si) * T + t0, [[0, 128], [1, HT]])
                                    s_bc[si % 2].ti(sync.dma_start(out=bbc[si % 2][:, :], in_=src_b), 16)
                                    s_bc[si % 2].ti(sync.dma_start(out=cbc[si % 2][:, :], in_=src_c), 16)

                            @blk.scalar
                            def _(scalar):
                                for si in range(128):
                                    for c in range(8):
                                        k = si * 8 + c
                                        if k >= 2:
                                            scalar.wait_ge(s_hs.h, bhs + k - 1)
                                        s_da.ti(scalar.activation(out=dar[k % 2][:, :],
                                                                  in_=dt_bf[:, c * HT:(c + 1) * HT],
                                                                  func=AF.Exp,
                                                                  scale=a_bc[:, si:si + 1]))

                            @blk.vector
                            def _(vector):
                                for si in range(128):
                                    vector.wait_ge(s_bc[si % 2].h, bbc_[si % 2] + (si // 2 + 1) * 32)
                                    for c in range(8):
                                        k = si * 8 + c
                                        vector.wait_ge(s_da.h, bda + k + 1)
                                        if k >= 2:
                                            vector.wait_ge(s_hs.h, bhs + k - 1)
                                        s_db.ti(vector.tensor_tensor(dbr[k % 2][:, :],
                                                                     dtu_bf[:, c * HT:(c + 1) * HT],
                                                                     bbc[si % 2][:, :], ALU.mult))
                                        vector.wait_ge(s_db.h, s_db.v)
                                        if k >= 2:
                                            vector.wait_ge(s_t1.h, bt1 + k - 1)
                                        init = 0.0 if th == 0 else h_bound[:, c * 128 + si:c * 128 + si + 1]
                                        s_hs.ti(vector.tensor_tensor_scan(
                                            out=hsr[k % 2][:, :], data0=dar[k % 2][:, :],
                                            data1=dbr[k % 2][:, :], initial=init,
                                            op0=ALU.mult, op1=ALU.add))
                                        vector.wait_ge(s_hs.h, s_hs.v)
                                        if th == 0:
                                            vector.tensor_copy(h_bound[:, c * 128 + si:c * 128 + si + 1],
                                                               hsr[k % 2][:, HT - 1:HT])
                                        if k >= 2:
                                            vector.wait_ge(s_ya.h, bya + k - 1)
                                        s_t1.ti(vector.tensor_tensor(t1r[k % 2][:, :],
                                                                     hsr[k % 2][:, :],
                                                                     cbc[si % 2][:, :], ALU.mult))

                            @blk.gpsimd
                            def _(gpsimd):
                                for si in range(128):
                                    for c in range(8):
                                        k = si * 8 + c
                                        gpsimd.wait_ge(s_t1.h, bt1 + k + 1)
                                        s_ya.ti(gpsimd.tensor_tensor(yh[:, c * HT:(c + 1) * HT],
                                                                     yh[:, c * HT:(c + 1) * HT],
                                                                     t1r[k % 2][:, :], ALU.add))

                        with ExitStack() as sy:
                            y2 = p.sbuf("y2", (128, HT), F32, ctx=sy)
                            psm = p.psum("psm", (1, HT), F32, ctx=sy)
                            s_q = p.sem("yq")
                            s_qm = p.sem("yqm")
                            bq, bqm = s_q.v, s_qm.v
                            with nc.Block(no_gpsimd_drain=True) as blk:
                                @blk.scalar
                                def _(scalar):
                                    for c in range(8):
                                        if c >= 1:
                                            scalar.wait_ge(s_qm.h, bqm + c)
                                        s_q.ti(scalar.activation(out=y2[:, :], in_=yh[:, c * HT:(c + 1) * HT],
                                                                 func=AF.Square))

                                @blk.tensor
                                def _(tensor):
                                    for c in range(8):
                                        tensor.wait_ge(s_q.h, bq + c + 1)
                                        for tb in range(HT // 512):
                                            mm = tensor.matmul(psm[:, tb * 512:(tb + 1) * 512],
                                                               onec[:, :],
                                                               y2[:, tb * 512:(tb + 1) * 512],
                                                               start=(c == 0), stop=(c == 7))
                                        s_qm.ti(mm)
                            with nc.Block(no_gpsimd_drain=True) as blk:
                                @blk.vector
                                def _(vector):
                                    if dh == 0:
                                        vector.tensor_copy(sums[0:1, t0:t0 + HT], psm[0:1, :])
                                    else:
                                        vector.tensor_tensor(sums[0:1, t0:t0 + HT],
                                                             sums[0:1, t0:t0 + HT],
                                                             psm[0:1, :], ALU.add)
                        with nc.Block(no_gpsimd_drain=True) as blk:
                            @blk.sync
                            def _(sync):
                                for c in range(8):
                                    sld.ti(sync.dma_start(
                                        out=yhd[:, (c0 + c) * T + t0:(c0 + c) * T + t0 + HT],
                                        in_=yh[:, c * HT:(c + 1) * HT]), 16)
                                sync.wait_ge(sld.h, sld.v)

        epsc2 = p.const_col(F32EPS)
        with nc.Block(no_gpsimd_drain=True) as blk:
            @blk.vector
            def _(vector):
                vector.tensor_scalar_mul(sums[0:1, :], sums[0:1, :], 1.0 / DI)
        with nc.Block(no_gpsimd_drain=True) as blk:
            @blk.scalar
            def _(scalar):
                scalar.activation(out=sums[0:1, :], in_=sums[0:1, :], func=AF.Sqrt,
                                  bias=epsc2[0:1, :])
        with nc.Block(no_gpsimd_drain=True) as blk:
            @blk.vector
            def _(vector):
                vector.reciprocal(sums[0:1, :], sums[0:1, :])
        with nc.Block(no_gpsimd_drain=True) as blk:
            @blk.sync
            def _(sync):
                sld.ti(sync.dma_start(out=rsd[:, :], in_=sums[0:1, :]), 16)
                sync.wait_ge(sld.h, sld.v)

        with ExitStack() as sg:
            acc_fm = p.sbuf("acc_fm", (128, NDC * T), F32, ctx=sg)
            for gh in range(2):
                g0 = gh * 8
                with ExitStack() as sg2:
                    y_bf = p.sbuf("y_bf", (128, 8 * T), BF16, ctx=sg2)
                    rbc = p.sbuf("rbc", (128, T), F32, ctx=sg2)
                    yhl = [p.sbuf("yhl", (128, T), F32, ctx=sg2) for _ in range(2)]
                    zl = [p.sbuf("zl", (128, T), BF16, ctx=sg2) for _ in range(2)]
                    zs = [p.sbuf("zs", (128, T), BF16, ctx=sg2) for _ in range(2)]
                    t2 = p.sbuf("t2", (128, T), BF16, ctx=sg2)
                    s_gd = [p.sem("g_d%d" % j) for j in range(3)]
                    s_gs = p.sem("g_s")
                    s_gv = p.sem("g_v")
                    bgd = [x.v for x in s_gd]
                    bgs, bgv = s_gs.v, s_gv.v
                    with nc.Block(no_gpsimd_drain=True) as blk:
                        @blk.sync
                        def _(sync):
                            src_r = bass.AP(rsd.tensor, rsd.offset, [[0, 128], [1, T]])
                            s_gd[2].ti(sync.dma_start(out=rbc[:, :], in_=src_r), 16)
                            for c8 in range(8):
                                c = g0 + c8
                                if c8 >= 2:
                                    sync.wait_ge(s_gv.h, bgv + 3 * (c8 - 1))
                                s_gd[c8 % 2].ti(sync.dma_start(out=yhl[c8 % 2][:, :],
                                                in_=yhd[:, c * T:(c + 1) * T]), 16)
                                s_gd[c8 % 2].ti(sync.dma_start(out=zl[c8 % 2][:, :],
                                                in_=zd[:, c * T:(c + 1) * T]), 16)

                        @blk.scalar
                        def _(scalar):
                            for c8 in range(8):
                                scalar.wait_ge(s_gd[c8 % 2].h, bgd[c8 % 2] + (c8 // 2 + 1) * 32)
                                if c8 >= 2:
                                    scalar.wait_ge(s_gv.h, bgv + 3 * (c8 - 1))
                                s_gs.ti(scalar.activation(out=zs[c8 % 2][:, :], in_=zl[c8 % 2][:, :],
                                                          func=AF.Silu))

                        @blk.vector
                        def _(vector):
                            vch = p.sem("vchain")
                            vector.wait_ge(s_gd[2].h, bgd[2] + 16)
                            for c8 in range(8):
                                c = g0 + c8
                                vector.wait_ge(s_gd[c8 % 2].h, bgd[c8 % 2] + (c8 // 2 + 1) * 32)
                                vector.wait_ge(s_gs.h, bgs + c8 + 1)
                                vch.ti(vector.tensor_tensor(t2[:, :], yhl[c8 % 2][:, :],
                                                            rbc[:, :], ALU.mult))
                                vector.wait_ge(vch.h, vch.v)
                                vch.ti(vector.tensor_scalar(t2[:, :], t2[:, :],
                                                            nw_sb[:, c:c + 1], None, ALU.mult))
                                vector.wait_ge(vch.h, vch.v)
                                s_gv.ti(vector.tensor_tensor(y_bf[:, c8 * T:(c8 + 1) * T],
                                                             t2[:, :], zs[c8 % 2][:, :], ALU.mult), 3)
                                vector.wait_ge(s_gv.h, s_gv.v)

                    def evac_ob(ofb, pstiles, eng, s_ev=None, gh=gh):
                        vec = eng["vector"]
                        last = None
                        for tb in range(NTB):
                            sl = acc_fm[:, ofb * T + tb * 512: ofb * T + (tb + 1) * 512]
                            if gh == 0:
                                last = vec.tensor_copy(sl, pstiles[tb][:, :])
                            else:
                                last = vec.tensor_tensor(sl, sl, pstiles[tb][:, :], ALU.add)
                        return last
                    p.proj_fm(outT[g0 * 128:(g0 + 8) * 128, :], 1024, D, y_bf, evac_ob,
                              evac_on="vector")
            p.finish_stream(x_in, acc_fm, x_out)
    p.cur_ctx = None
    return p




def l1_attn_b_inmaps(x_batch, inp):
    f = np.float32
    cw = np.asarray(inp["L1_attn_b_conv_w"], f)[:, 0, :]
    wconv = np.ascontiguousarray(cw.reshape(NDI, 128, 4).transpose(1, 0, 2).reshape(128, NDI * 4))
    xpT = np.ascontiguousarray(np.asarray(inp["L1_attn_b_x_proj_w"], f).T)
    w264 = np.zeros((DI, 264), f)
    w264[:, 0] = xpT[:, 0]
    w264[:, 8:136] = xpT[:, 1:129]
    w264[:, 136:264] = xpT[:, 129:257]
    base = dict(
        l1b_inpT=np.ascontiguousarray(np.asarray(inp["L1_attn_b_in_proj_w"], f).T),
        l1b_wconv=wconv,
        l1b_bconv=np.ascontiguousarray(np.asarray(inp["L1_attn_b_conv_b"], f).reshape(NDI, 128).T),
        l1b_w264=w264,
        l1b_wdtrow=np.asarray(inp["L1_attn_b_dt_proj_w"], f).reshape(1, DI),
        l1b_bdtcols=np.ascontiguousarray(np.asarray(inp["L1_attn_b_dt_proj_b"], f).reshape(NDI, 128).T),
        l1b_alog=np.asarray(inp["L1_attn_b_A_log"], f).reshape(1, 128),
        l1b_nwcols=np.ascontiguousarray(np.asarray(inp["L1_attn_b_norm_w"], f).reshape(NDI, 128).T),
        l1b_outT=np.ascontiguousarray(np.asarray(inp["L1_attn_b_out_proj_w"], f).T),
        c_ident_in=IDENT128)
    return [dict(base, x=np.ascontiguousarray(xb, f)) for xb in x_batch]


def declare_l0_attn_a(p):
    nc = p.nc
    return (
        p.dram_in("l0a_wqkvT", (6, D, 3 * D)),
        p.dram_in("l0a_bqkv", (6, 3 * D)),
        p.dram_in("l0a_woT", (6, D, D)),
        p.dram_in("l0a_bo", (6, D)),
        p.dram_in("l0a_w1T", (6, D, 4 * D)),
        p.dram_in("l0a_b1r", (6, 4 * D)),
        p.dram_in("l0a_w2T", (6, 4 * D, D)),
        p.dram_in("l0a_b2r", (6, D)),
        p.dram_in("l0a_ln1g", (6, D)),
        p.dram_in("l0a_ln1b", (6, D)),
        p.dram_in("l0a_ln2g", (6, D)),
        p.dram_in("l0a_ln2b", (6, D)),
        p.dram_in("l0a_ngrow", (1, D)),
        p.dram_in("l0a_nbrow", (1, D)),
        p.dram_in("l0a_posd", (T, D)),
        nc.dram_tensor("l0a_maskb", [128, 1152], mybir.dt.uint8, kind="ExternalInput").ap(),
        nc.dram_tensor("l0a_pmask7", [128, 1], mybir.dt.uint8, kind="ExternalInput").ap(),
        nc.dram_tensor("l0a_xa_scr", [T, D], F32, kind="Internal").ap(),
        nc.dram_tensor("l0a_tmp_scr", [T, D], F32, kind="Internal").ap(),
    )


def build_l0_attn_a(p=None, xio=None, aps=None):
    standalone = p is None
    if standalone:
        p = P("l0attna")
    nc = p.nc
    if aps is None:
        aps = declare_l0_attn_a(p)
    (wqkvT, bqkv, woT, bo, w1T, b1r, w2T, b2r, ln1g, ln1b, ln2g, ln2b, ngrow, nbrow, posd, maskb_in, pmask_in, xad, tmpd) = aps
    x_in = xio[0] if xio else p.dram_in("x", (T, D))
    x_out = xio[1] if xio else p.dram_out("y", (T, D))
    cm = p.ctx if standalone else ExitStack()
    with cm:
        if standalone:
            p.common(max_nk=8, max_bias=1024)
        p.cur_ctx = cm
        p.begin_pool()
        maskb = p.sbuf("maskb", (128, 1152), mybir.dt.uint8)
        pmask = p.sbuf("pmask", (128, 1), mybir.dt.uint8)
        negC = p.sbuf("negC", (128, 1152), F32)
        g_bc = p.sbuf("g_bc", (128, D), F32)
        b_bc = p.sbuf("b_bc", (128, D), F32)
        sld = p.sem("sld")

        with nc.Block(no_gpsimd_drain=True) as blk:
            @blk.sync
            def _(sync):
                sld.ti(sync.dma_start(out=maskb[:, :], in_=maskb_in[:, :]), 16)
                sld.ti(sync.dma_start(out=pmask[:, :], in_=pmask_in[:, :]), 16)
                sync.wait_ge(sld.h, sld.v)

            @blk.vector
            def _(vector):
                vector.memset(negC[:], NEG)

        def bcast(dst, row_hbm):
            with nc.Block(no_gpsimd_drain=True) as blk:
                @blk.sync
                def _(sync):
                    src = bass.AP(row_hbm.tensor, row_hbm.offset, [[0, 128], [1, D]])
                    sld.ti(sync.dma_start(out=dst[:, :], in_=src), 16)
                    sync.wait_ge(sld.h, sld.v)

        def pass1(src_hbm, dst_hbm, acc_fm=None, acc_bf=False, aff=False):
            with ExitStack() as lctx:
                xb = [p.sbuf("p1x", (128, D), F32, ctx=lctx) for _ in range(3)]
                ob = [p.sbuf("p1o", (128, D), F32, ctx=lctx) for _ in range(2)]
                cbf = p.sbuf("p1c", (128, D), BF16, ctx=lctx) if (acc_fm is not None and not acc_bf) else None
                pt = [p.psum("p1t", (128, 128), BF16, ctx=lctx) for _ in range(8)] if acc_fm is not None else []
                s_d = [p.sem("p1_d%d" % j) for j in range(3)]
                s_a = p.sem("p1_a")
                s_cc = p.sem("p1_c")
                s_t = p.sem("p1_t")
                s_ad = p.sem("p1_ad")
                s_st = p.sem("p1_st")
                s_o = [p.sem("p1_o%d" % j) for j in range(2)]
                bd = [sd.v for sd in s_d]
                bo_ = [so.v for so in s_o]
                ba, bc2, bt, bad, bst = s_a.v, s_cc.v, s_t.v, s_ad.v, s_st.v
                with nc.Block() as blk:
                    @blk.sync
                    def _(sync):
                        for i in range(NTT):
                            if i >= 3:
                                sync.wait_ge(s_a.h, ba + i - 2)
                            s_d[i % 3].ti(sync.dma_start(out=xb[i % 3][:],
                                          in_=src_hbm[i * 128:(i + 1) * 128, :]), 16)

                    @blk.scalar
                    def _(scalar):
                        for i in range(NTT):
                            if acc_fm is not None and not acc_bf:
                                if i >= 1:
                                    scalar.wait_ge(s_t.h, bt + i * NDC)
                                last = None
                                for c in range(NDC):
                                    last = scalar.copy(cbf[:, c * 128:(c + 1) * 128],
                                                       acc_fm[:, c * T + i * 128: c * T + (i + 1) * 128])
                                s_cc.ti(last)
                            scalar.wait_ge(s_ad.h, bad + i + 1)
                            s_st.ti(scalar.activation(out=p.junk[:, 0:D], in_=ob[i % 2][:],
                                                      func=AF.Square, accum_out=p.s2[:, i:i + 1]))
                            scalar.wait_ge(s_st.h, s_st.v)

                    if acc_fm is not None:
                        @blk.tensor
                        def _(tensor):
                            for i in range(NTT):
                                if not acc_bf:
                                    tensor.wait_ge(s_cc.h, bc2 + i + 1)
                                if i >= 1:
                                    tensor.wait_ge(s_ad.h, bad + i)
                                for c in range(NDC):
                                    k = i * NDC + c
                                    src_t = (acc_fm[:, c * T + i * 128: c * T + (i + 1) * 128]
                                             if acc_bf else cbf[:, c * 128:(c + 1) * 128])
                                    s_t.ti(tensor.transpose(pt[k % 8][:, :], src_t, p.ident[:]))

                    @blk.vector
                    def _(vector):
                        for i in range(NTT):
                            vector.wait_ge(s_d[i % 3].h, bd[i % 3] + (i // 3 + 1) * 16)
                            if i >= 2:
                                vector.wait_ge(s_o[i % 2].h, bo_[i % 2] + (i // 2) * 16)
                                vector.wait_ge(s_st.h, bst + i - 1)
                            if aff:
                                v = vector.tensor_scalar(ob[i % 2][:], xb[i % 3][:],
                                                         p.sa[:, i:i + 1], p.sb[:, i:i + 1],
                                                         ALU.mult, ALU.add)
                            else:
                                v = vector.tensor_copy(ob[i % 2][:], xb[i % 3][:])
                            s_a.ti(v)
                            vector.wait_ge(s_a.h, s_a.v)
                            if acc_fm is not None:
                                vch = p.sem("vchain")
                                for c in range(NDC):
                                    k = i * NDC + c
                                    vector.wait_ge(s_t.h, bt + k + 1)
                                    op = vector.tensor_tensor(ob[i % 2][:, c * 128:(c + 1) * 128],
                                                              ob[i % 2][:, c * 128:(c + 1) * 128],
                                                              pt[k % 8][:, :], ALU.add)
                                    if c == NDC - 1:
                                        vch.ti(op)
                                        vector.wait_ge(vch.h, vch.v)
                            s_ad.ti(vector.reduce_sum(p.s1[:, i:i + 1], ob[i % 2][:], axis=AX.X))

                    @blk.gpsimd
                    def _(gpsimd):
                        for i in range(NTT):
                            gpsimd.wait_ge(s_ad.h, bad + i + 1)
                            gpsimd.wait_ge(s_st.h, bst + i + 1)
                            s_o[i % 2].ti(gpsimd.dma_start(out=dst_hbm[i * 128:(i + 1) * 128, :],
                                          in_=ob[i % 2][:]), 16)
                        gpsimd.wait_ge(s_o[0].h, s_o[0].v)
                        gpsimd.wait_ge(s_o[1].h, s_o[1].v)
            p.ln_combine()

        def pass2(src_hbm, xfm, g, b, pos=False, store=None):
            with ExitStack() as lctx:
                xb = [p.sbuf("p2x", (128, D), F32, ctx=lctx) for _ in range(3)]
                pb = [p.sbuf("p2p", (128, D), F32, ctx=lctx) for _ in range(2)] if pos else None
                yb = [p.sbuf("p2y", (128, D), F32, ctx=lctx) for _ in range(2)]
                cb = [p.sbuf("p2c", (128, D), BF16, ctx=lctx) for _ in range(2)]
                pt = [p.psum("p2t", (128, 128), BF16, ctx=lctx) for _ in range(4)]
                s_d = [p.sem("p2_d%d" % j) for j in range(3)]
                s_y = p.sem("p2_y")
                s_c = p.sem("p2_c")
                s_t = p.sem("p2_t")
                s_e = p.sem("p2_e")
                s_o = [p.sem("p2_o%d" % j) for j in range(2)]
                bd = [sd.v for sd in s_d]
                bo_ = [so.v for so in s_o]
                by, bcc, bt, be = s_y.v, s_c.v, s_t.v, s_e.v
                with nc.Block() as blk:
                    @blk.sync
                    def _(sync):
                        for i in range(NTT):
                            if pos and i >= 2:
                                sync.wait_ge(s_y.h, by + i - 1)
                            elif i >= 3:
                                sync.wait_ge(s_y.h, by + i - 2)
                            s_d[i % 3].ti(sync.dma_start(out=xb[i % 3][:],
                                          in_=src_hbm[i * 128:(i + 1) * 128, :]), 16)
                            if pos:
                                s_d[i % 3].ti(sync.dma_start(out=pb[i % 2][:],
                                              in_=posd[i * 128:(i + 1) * 128, :]), 16)

                    @blk.vector
                    def _(vector):
                        vch = p.sem("vchain")
                        nper = 32 if pos else 16

                        def mk(i):
                            vector.wait_ge(s_d[i % 3].h, bd[i % 3] + (i // 3 + 1) * nper)
                            if i >= 2:
                                vector.wait_ge(s_c.h, bcc + i - 1)
                                if store is not None:
                                    vector.wait_ge(s_o[i % 2].h, bo_[i % 2] + (i // 2) * 16)
                            vch.ti(vector.tensor_scalar(yb[i % 2][:], xb[i % 3][:],
                                                        p.sa[:, i:i + 1], p.sb[:, i:i + 1],
                                                        ALU.mult, ALU.add))
                            vector.wait_ge(vch.h, vch.v)
                            vch.ti(vector.tensor_tensor(yb[i % 2][:], yb[i % 2][:], g[:, :], ALU.mult))
                            vector.wait_ge(vch.h, vch.v)
                            op = vector.tensor_tensor(yb[i % 2][:], yb[i % 2][:], b[:, :], ALU.add)
                            if pos:
                                vch.ti(op)
                                vector.wait_ge(vch.h, vch.v)
                                op = vector.tensor_tensor(yb[i % 2][:], yb[i % 2][:],
                                                          pb[i % 2][:], ALU.add)
                            s_y.ti(op)
                            vector.wait_ge(s_y.h, s_y.v)

                        mk(0)
                        if NTT > 1:
                            mk(1)
                        for i in range(NTT):
                            for c in range(NDC):
                                k = i * NDC + c
                                vector.wait_ge(s_t.h, bt + k + 1)
                                s_e.ti(vector.tensor_copy(
                                    xfm[:, c * T + i * 128: c * T + (i + 1) * 128],
                                    pt[k % 4][:, :]))
                            if i + 2 < NTT:
                                mk(i + 2)

                    @blk.scalar
                    def _(scalar):
                        for i in range(NTT):
                            scalar.wait_ge(s_y.h, by + i + 1)
                            if i >= 2:
                                scalar.wait_ge(s_e.h, be + (i - 1) * NDC)
                            s_c.ti(scalar.copy(cb[i % 2][:], yb[i % 2][:]))

                    @blk.tensor
                    def _(tensor):
                        for i in range(NTT):
                            tensor.wait_ge(s_c.h, bcc + i + 1)
                            for c in range(NDC):
                                k = i * NDC + c
                                if k >= 4:
                                    tensor.wait_ge(s_e.h, be + k - 3)
                                s_t.ti(tensor.transpose(pt[k % 4][:, :],
                                                        cb[i % 2][:, c * 128:(c + 1) * 128],
                                                        p.ident[:]))

                    if store is not None:
                        @blk.gpsimd
                        def _(gpsimd):
                            for i in range(NTT):
                                gpsimd.wait_ge(s_y.h, by + i + 1)
                                s_o[i % 2].ti(gpsimd.dma_start(out=store[i * 128:(i + 1) * 128, :],
                                              in_=yb[i % 2][:]), 16)
                            gpsimd.wait_ge(s_o[0].h, s_o[0].v)
                            gpsimd.wait_ge(s_o[1].h, s_o[1].v)

        xfm = p.sbuf("xfm", (128, NDC * T), BF16)
        p.ln_stats_stream(x_in)
        pass1(x_in, tmpd, aff=True)
        bcast(g_bc, ngrow[0:1, :])
        bcast(b_bc, nbrow[0:1, :])
        pass2(tmpd, xfm, g_bc, b_bc, pos=True, store=xad)

        for l in range(6):
            with ExitStack() as la:
                q_fm = p.sbuf("q_fm", (128, NDC * T), BF16, ctx=la)
                o_fm = p.sbuf("o_fm", (128, NDC * T), BF16, ctx=la)

                def evac_q(ofb, pstiles, eng):
                    sc = eng["scalar"]
                    last = None
                    for tb in range(NTB):
                        last = sc.mul(q_fm[:, ofb * T + tb * 512: ofb * T + (tb + 1) * 512],
                                      pstiles[tb][:, :], 0.125)
                    return last
                p.proj_fm(wqkvT[l][:, 0:D], D, D, xfm, evac_q,
                          bias_hbm_row=bqkv[l:l + 1, 0:D])

                for hp in range(8):
                    with ExitStack() as lp:
                        k_bf = p.sbuf("k_bf", (128, T), BF16, ctx=lp)
                        v_bf = p.sbuf("v_bf", (128, T), BF16, ctx=lp)
                        v_tm = p.sbuf("v_tm", (128, NTT * 128), BF16, ctx=lp)
                        a_sb = p.sbuf("a_sb", (128, 1153), F32, ctx=lp)
                        a_bf = p.sbuf("a_bf", (128, 1280), BF16, ctx=lp)
                        at_sb = p.sbuf("at_sb", (128, 10 * 128), BF16, ctx=lp)
                        ob_bf = p.sbuf("ob_bf", (128, 64), BF16, ctx=lp)
                        mcol = p.sbuf("mcol", (128, 1), F32, ctx=lp)
                        scol = p.sbuf("scol", (128, 1), F32, ctx=lp)

                        def evac_to(dst):
                            def ev(ofb, pstiles, eng):
                                sc = eng["scalar"]
                                last = None
                                for tb in range(NTB):
                                    last = sc.copy(dst[:, tb * 512:(tb + 1) * 512], pstiles[tb][:, :])
                                return last
                            return ev
                        p.proj_fm(wqkvT[l][:, D + hp * 128: D + (hp + 1) * 128], D, 128,
                                  xfm, evac_to(k_bf),
                                  bias_hbm_row=bqkv[l:l + 1, D + hp * 128: D + (hp + 1) * 128])
                        p.proj_fm(wqkvT[l][:, 2 * D + hp * 128: 2 * D + (hp + 1) * 128], D, 128,
                                  xfm, evac_to(v_bf),
                                  bias_hbm_row=bqkv[l:l + 1, 2 * D + hp * 128: 2 * D + (hp + 1) * 128])
                        items = [(v_bf[:, i * 128:(i + 1) * 128], v_tm[:, i * 128:(i + 1) * 128])
                                 for i in range(NTT)]
                        p.transpose_bf(items)
                        bk_bf = p.sbuf("bk_bf", (128, 1), BF16, ctx=lp)
                        bv_row = p.sbuf("bv_row", (1, 128), BF16, ctx=lp)
                        bkf = p.sbuf("bkf", (128, 1), F32, ctx=lp)
                        bvf = p.sbuf("bvf", (1, 128), F32, ctx=lp)
                        with nc.Block(no_gpsimd_drain=True) as blk:
                            @blk.sync
                            def _(sync):
                                src = bass.AP(bqkv.tensor,
                                              bqkv.offset + l * 3 * D + D + hp * 128, [[1, 128], [1, 1]])
                                sld.ti(sync.dma_start(out=bkf[:, :], in_=src), 16)
                                sld.ti(sync.dma_start(out=bvf[:, :],
                                                      in_=bqkv[l:l + 1, 2 * D + hp * 128: 2 * D + (hp + 1) * 128]), 16)
                                sync.wait_ge(sld.h, sld.v)
                        with nc.Block(no_gpsimd_drain=True) as blk:
                            @blk.vector
                            def _(vector):
                                vector.tensor_copy(bk_bf[:, :], bkf[:, :])
                                vector.tensor_copy(bv_row[:, :], bvf[:, :])

                        for h2 in range(2):
                            hb = 64 * h2
                            h = hp * 2 + h2
                            ch = h // 2
                            with ExitStack() as aa:
                                ps_s = p.psum("ps_s", (128, 1153), F32, ctx=aa)
                                pt_a = [p.psum("pt_a", (128, 128), BF16, ctx=aa) for _ in range(2)]
                                ps_o = p.psum("ps_o", (128, 64), F32, ctx=aa)
                                pt_o = p.psum("pt_o", (64, 128), BF16, ctx=aa)
                                s_sc = p.sem("at_sc")
                                s_sel = p.sem("at_sel")
                                s_ex = p.sem("at_ex")
                                s_tp = p.sem("at_tp")
                                s_te = p.sem("at_te")
                                s_av = p.sem("at_av")
                                s_oe = p.sem("at_oe")
                                s_ot = p.sem("at_ot")
                                s_of = p.sem("at_of")
                                bsc, bsel, bex, btp, bte, bav = (s_sc.v, s_sel.v, s_ex.v,
                                                                 s_tp.v, s_te.v, s_av.v)
                                boe, bot, bof = s_oe.v, s_ot.v, s_of.v
                                s_rc = p.sem("at_rc")
                                brc = s_rc.v
                                qts = list(range(NTT))
                                meta = []
                                tp_before = []
                                tot = 0
                                for qt in qts:
                                    nkt = min(qt + 1, 9)
                                    W = nkt * 128
                                    haspast = qt <= 7
                                    meta.append((qt, nkt, W, haspast))
                                    tp_before.append(tot)
                                    tot += nkt + (1 if haspast else 0)
                                with nc.Block(no_gpsimd_drain=True) as blk:
                                    @blk.tensor
                                    def _(tensor):
                                        for mi, (qt, nkt, W, haspast) in enumerate(meta):
                                            ks = max(0, qt - 8) * 128
                                            q_sl = q_fm[hb:hb + 64,
                                                        ch * T + qt * 128: ch * T + (qt + 1) * 128]
                                            if mi >= 1:
                                                tensor.wait_ge(s_sel.h, bsel + mi)
                                            nmm = (W + 511) // 512
                                            for mmi in range(nmm):
                                                w0 = mmi * 512
                                                w1 = min(W, w0 + 512)
                                                mm = tensor.matmul(
                                                    ps_s[:, w0:w1],
                                                    q_sl,
                                                    k_bf[hb:hb + 64, ks + w0: ks + w1],
                                                    start=True, stop=True)
                                            if haspast:
                                                mm = tensor.matmul(ps_s[:, W:W + 1], q_sl,
                                                                   bk_bf[hb:hb + 64, :],
                                                                   start=True, stop=True)
                                            s_sc.ti(mm)
                                            nblk = nkt + (1 if haspast else 0)
                                            tensor.wait_ge(s_ex.h, bex + mi + 1)
                                            for bkt in range(nblk):
                                                kg = tp_before[mi] + bkt
                                                if kg >= 2:
                                                    tensor.wait_ge(s_te.h, bte + kg - 1)
                                                s_tp.ti(tensor.transpose(
                                                    pt_a[kg % 2][:, :],
                                                    a_bf[:, bkt * 128:(bkt + 1) * 128],
                                                    p.ident[:]))
                                            tensor.wait_ge(s_te.h, bte + tp_before[mi] + nblk)
                                            if mi >= 1:
                                                tensor.wait_ge(s_oe.h, boe + mi)
                                            for kt in range(nkt):
                                                mm = tensor.matmul(
                                                    ps_o[:, :],
                                                    at_sb[:, kt * 128:(kt + 1) * 128],
                                                    v_tm[:, (ks // 128 + kt) * 128:
                                                         (ks // 128 + kt + 1) * 128][:, hb:hb + 64],
                                                    start=(kt == 0), stop=(kt == nkt - 1 and not haspast))
                                            if haspast:
                                                mm = tensor.matmul(ps_o[:, :],
                                                                   at_sb[0:1, nkt * 128: nkt * 128 + 128],
                                                                   bv_row[0:1, hb:hb + 64],
                                                                   start=False, stop=True)
                                            s_av.ti(mm)
                                            tensor.wait_ge(s_oe.h, boe + mi + 1)
                                            if mi >= 1:
                                                tensor.wait_ge(s_of.h, bof + mi)
                                            s_ot.ti(tensor.transpose(pt_o[:, :], ob_bf[:, :],
                                                                     p.ident[:]))

                                    @blk.vector
                                    def _(vector):
                                        vch = p.sem("vchain")
                                        for mi, (qt, nkt, W, haspast) in enumerate(meta):
                                            Wp = W + (1 if haspast else 0)
                                            off = (1024 - qt * 128) if qt <= 7 else 0
                                            vector.wait_ge(s_sc.h, bsc + mi + 1)
                                            vch.ti(vector.select(a_sb[:, 0:W], maskb[:, off:off + W],
                                                                 ps_s[:, 0:W], negC[:, 0:W], add_drain=True))
                                            vector.wait_ge(vch.h, vch.v)
                                            if haspast:
                                                if qt == 7:
                                                    vch.ti(vector.select(a_sb[:, W:W + 1], pmask[:, :],
                                                                         ps_s[:, W:W + 1], negC[:, 0:1], add_drain=True))
                                                else:
                                                    vch.ti(vector.tensor_copy(a_sb[:, W:W + 1],
                                                                              ps_s[:, W:W + 1]))
                                                vector.wait_ge(vch.h, vch.v)
                                            if mi >= 1:
                                                vector.wait_ge(s_ex.h, bex + mi)
                                            s_sel.ti(vector.reduce_max(mcol[:, :], a_sb[:, 0:Wp],
                                                                       axis=AX.X, negate=True))
                                            vector.wait_ge(s_sel.h, s_sel.v)
                                            vector.wait_ge(s_ex.h, bex + mi + 1)
                                            s_rc.ti(vector.reciprocal(scol[:, :], p.s2[:, mi % 2:mi % 2 + 1]))
                                            vector.wait_ge(s_rc.h, s_rc.v)
                                            vector.wait_ge(s_av.h, bav + mi + 1)
                                            s_oe.ti(vector.tensor_scalar(ob_bf[:, :], ps_o[:, :],
                                                                         scol[:, 0:1], None, ALU.mult))
                                            vector.wait_ge(s_oe.h, s_oe.v)
                                            vector.wait_ge(s_ot.h, bot + mi + 1)
                                            s_of.ti(vector.tensor_copy(
                                                o_fm[hb:hb + 64, ch * T + qt * 128: ch * T + (qt + 1) * 128],
                                                pt_o[:, :]))

                                    @blk.scalar
                                    def _(scalar):
                                        for mi, (qt, nkt, W, haspast) in enumerate(meta):
                                            Wp = W + (1 if haspast else 0)
                                            scalar.wait_ge(s_sel.h, bsel + mi + 1)
                                            if mi >= 1:
                                                scalar.wait_ge(s_tp.h, btp + tp_before[mi])
                                            if mi >= 2:
                                                scalar.wait_ge(s_rc.h, brc + mi - 1)
                                            s_ex.ti(scalar.activation(out=a_bf[:, 0:Wp],
                                                                      in_=a_sb[:, 0:Wp],
                                                                      func=AF.Exp, bias=mcol[:, 0:1],
                                                                      accum_out=p.s2[:, mi % 2:mi % 2 + 1]))
                                            nblk = nkt + (1 if haspast else 0)
                                            if mi >= 1:
                                                scalar.wait_ge(s_av.h, bav + mi)
                                            for bkt in range(nblk):
                                                kg = tp_before[mi] + bkt
                                                scalar.wait_ge(s_tp.h, btp + kg + 1)
                                                s_te.ti(scalar.copy(
                                                    at_sb[:, bkt * 128:(bkt + 1) * 128],
                                                    pt_a[kg % 2][:, :]))

                with ExitStack() as lw:
                    wo_bf = p.sbuf("wo_bf", (128, NDC * T), BF16, ctx=lw)
                    p.proj_fm(woT[l], D, D, o_fm, _mk_evac_copy(p, wo_bf),
                              bias_hbm_row=bo[l:l + 1, :])
                    pass1(xad, tmpd, acc_fm=wo_bf, acc_bf=True)
                bcast(g_bc, ln1g[l:l + 1, :])
                bcast(b_bc, ln1b[l:l + 1, :])
                pass2(tmpd, xfm, g_bc, b_bc, store=xad)

            with ExitStack() as lf:
                ff_acc = p.sbuf("ff_acc", (128, NDC * T), F32, ctx=lf)
                h_q = p.sbuf("h_q", (128, NDC * T), BF16, ctx=lf)

                def evac_gelu_f(ofb, pstiles, eng):
                    sc = eng["scalar"]
                    last = None
                    for tb in range(NTB):
                        last = sc.activation(out=h_q[:, ofb * T + tb * 512: ofb * T + (tb + 1) * 512],
                                             in_=pstiles[tb][:, :], func=AF.Gelu)
                    return last
                for qr in range(4):
                    p.proj_fm(w1T[l][:, qr * D:(qr + 1) * D], D, D, xfm, evac_gelu_f,
                              bias_hbm_row=b1r[l:l + 1, qr * D:(qr + 1) * D])

                    def evac_ff(ofb, pstiles, eng, s_ev=None, qr=qr):
                        vec = eng["vector"]
                        last = None
                        for tb in range(NTB):
                            sl = ff_acc[:, ofb * T + tb * 512: ofb * T + (tb + 1) * 512]
                            if qr == 0:
                                last = vec.tensor_copy(sl, pstiles[tb][:, :])
                            else:
                                last = vec.tensor_tensor(sl, sl, pstiles[tb][:, :], ALU.add)
                        return last
                    p.proj_fm(w2T[l][qr * D:(qr + 1) * D, :], D, D, h_q, evac_ff,
                              bias_hbm_row=(b2r[l:l + 1, :] if qr == 0 else None),
                              evac_on="vector")
                pass1(xad, tmpd, acc_fm=ff_acc)
            bcast(g_bc, ln2g[l:l + 1, :])
            bcast(b_bc, ln2b[l:l + 1, :])
            pass2(tmpd, xfm, g_bc, b_bc, store=xad)

        with ExitStack() as lz:
            xb = [p.sbuf("fzx", (128, D), F32, ctx=lz) for _ in range(2)]
            ab = [p.sbuf("fza", (128, D), F32, ctx=lz) for _ in range(2)]
            s_d = [p.sem("fz_d%d" % j) for j in range(2)]
            s_v = p.sem("fz_v")
            s_o = [p.sem("fz_o%d" % j) for j in range(2)]
            bd = [sd.v for sd in s_d]
            bo_ = [so.v for so in s_o]
            bv = s_v.v
            with nc.Block() as blk:
                @blk.sync
                def _(sync):
                    for i in range(NTT):
                        if i >= 2:
                            sync.wait_ge(s_v.h, bv + i - 1)
                            sync.wait_ge(s_o[i % 2].h, bo_[i % 2] + (i // 2) * 16)
                        s_d[i % 2].ti(sync.dma_start(out=xb[i % 2][:],
                                      in_=x_in[i * 128:(i + 1) * 128, :]), 16)
                        s_d[i % 2].ti(sync.dma_start(out=ab[i % 2][:],
                                      in_=xad[i * 128:(i + 1) * 128, :]), 16)

                @blk.vector
                def _(vector):
                    for i in range(NTT):
                        vector.wait_ge(s_d[i % 2].h, bd[i % 2] + (i // 2 + 1) * 32)
                        if i >= 2:
                            vector.wait_ge(s_o[i % 2].h, bo_[i % 2] + (i // 2) * 16)
                        s_v.ti(vector.tensor_tensor(ab[i % 2][:], ab[i % 2][:],
                                                    xb[i % 2][:], ALU.add))
                        vector.wait_ge(s_v.h, s_v.v)

                @blk.gpsimd
                def _(gpsimd):
                    for i in range(NTT):
                        gpsimd.wait_ge(s_v.h, bv + i + 1)
                        s_o[i % 2].ti(gpsimd.dma_start(out=x_out[i * 128:(i + 1) * 128, :],
                                      in_=ab[i % 2][:]), 16)
                    gpsimd.wait_ge(s_o[0].h, s_o[0].v)
                    gpsimd.wait_ge(s_o[1].h, s_o[1].v)
    p.cur_ctx = None
    return p


def _mk_evac_copy(p, dst):
    def ev(ofb, pstiles, eng):
        sc = eng["scalar"]
        last = None
        for tb in range(NTB):
            last = sc.copy(dst[:, ofb * T + tb * 512: ofb * T + (tb + 1) * 512],
                           pstiles[tb][:, :])
        return last
    return ev


def _l0a_masks():
    mask = np.zeros((128, 1152), np.uint8)
    for i in range(128):
        lo = i + 25
        hi = i + 1024
        mask[i, max(0, lo):min(1152, hi + 1)] = 1
    pm = np.zeros((128, 1), np.uint8)
    pm[0:104, 0] = 1
    return mask, pm


def _l0a_pos():
    t = np.arange(T, dtype=np.float64)[:, None]
    a = np.arange(512, dtype=np.float64)[None, :]
    phase = t / (10000.0 ** (a / 511.0))
    return np.concatenate([np.cos(phase), np.sin(phase)], axis=1).astype(np.float32)


def l0_attn_a_inmaps(x_batch, inp):
    f = np.float32
    mask, pm = _l0a_masks()
    base = dict(
        l0a_wqkvT=np.ascontiguousarray(np.transpose(np.asarray(inp["L0_attn_a_Wqkv"], f), (0, 2, 1))),
        l0a_bqkv=np.ascontiguousarray(np.asarray(inp["L0_attn_a_bqkv"], f)),
        l0a_woT=np.ascontiguousarray(np.transpose(np.asarray(inp["L0_attn_a_Wo"], f), (0, 2, 1))),
        l0a_bo=np.ascontiguousarray(np.asarray(inp["L0_attn_a_bo"], f)),
        l0a_w1T=np.ascontiguousarray(np.transpose(np.asarray(inp["L0_attn_a_W1"], f), (0, 2, 1))),
        l0a_b1r=np.ascontiguousarray(np.asarray(inp["L0_attn_a_b1"], f)),
        l0a_w2T=np.ascontiguousarray(np.transpose(np.asarray(inp["L0_attn_a_W2"], f), (0, 2, 1))),
        l0a_b2r=np.ascontiguousarray(np.asarray(inp["L0_attn_a_b2"], f)),
        l0a_ln1g=np.ascontiguousarray(np.asarray(inp["L0_attn_a_ln1_g"], f)),
        l0a_ln1b=np.ascontiguousarray(np.asarray(inp["L0_attn_a_ln1_b"], f)),
        l0a_ln2g=np.ascontiguousarray(np.asarray(inp["L0_attn_a_ln2_g"], f)),
        l0a_ln2b=np.ascontiguousarray(np.asarray(inp["L0_attn_a_ln2_b"], f)),
        l0a_ngrow=np.asarray(inp["L0_attn_a_norm_in_g"], f).reshape(1, D),
        l0a_nbrow=np.asarray(inp["L0_attn_a_norm_in_b"], f).reshape(1, D),
        l0a_posd=_l0a_pos(),
        l0a_maskb=mask, l0a_pmask7=pm,
        c_ident_in=IDENT128)
    return [dict(base, x=np.ascontiguousarray(xb, f)) for xb in x_batch]


def build_fused():
    p = P("fused")
    nc = p.nc
    x0 = p.dram_in("x", (T, D))
    y_out = p.dram_out("y", (T, D))
    ch = [nc.dram_tensor("xchain%d" % i, [T, D], F32, kind="Internal").ap()
          for i in range(7)]
    A_l0a = declare_l0_attn_a(p)
    A_l0f = declare_l0_ffn(p)
    A_l0b = declare_l0_attn_b(p)
    A_l1a = declare_l1_attn_a(p)
    A_l1f = declare_l1_ffn(p)
    A_l1b = declare_l1_attn_b(p)
    with p.ctx:
        p.common(max_nk=16, max_bias=1024)
        with nc.named_scope("s0_l0attna"):
            build_l0_attn_a(p, (x0, ch[0]), A_l0a)
        with nc.named_scope("s1_l0ffn"):
            build_l0_ffn(p, (ch[0], ch[1]), A_l0f)
        with nc.named_scope("s2_l0attnb"):
            build_l0_attn_b(p, (ch[1], ch[2]), A_l0b)
        with nc.named_scope("s3_l0ffn"):
            build_l0_ffn(p, (ch[2], ch[3]), A_l0f)
        with nc.named_scope("s4_l1attna"):
            build_l1_attn_a(p, (ch[3], ch[4]), A_l1a)
        with nc.named_scope("s5_l1ffn"):
            build_l1_ffn(p, (ch[4], ch[5]), A_l1f)
        with nc.named_scope("s6_l1attnb"):
            build_l1_attn_b(p, (ch[5], ch[6]), A_l1b)
        with nc.named_scope("s7_l1ffn"):
            build_l1_ffn(p, (ch[6], y_out), A_l1f)
    return p


def _weights_of(inmaps_fn, inp):
    m = inmaps_fn([np.zeros((T, D), np.float32)], inp)[0]
    m.pop("x")
    return m


def fused_weightmap(inp):
    w = {}
    for fn in (l0_attn_a_inmaps, l0_ffn_inmaps, l0_attn_b_inmaps,
               l1_attn_a_inmaps, l1_ffn_inmaps, l1_attn_b_inmaps):
        w.update(_weights_of(fn, inp))
    return w


def kernel(**inputs):
    inp = {k: np.asarray(v) for k, v in inputs.items()}
    x = np.ascontiguousarray(np.asarray(inp["L0_attn_a_x"], np.float32))
    nb = x.shape[0]
    p = _get("fused", build_fused)
    w = fused_weightmap(inp)
    maps = [dict(w, x=np.ascontiguousarray(x[b])) for b in range(nb)]
    res = run_bass_kernel_spmd(p.nc, maps, core_ids=list(range(nb)))
    out = np.stack([res.results[b]["y"] for b in range(nb)], 0)
    return np.ascontiguousarray(out.astype(np.float32))


_CACHE = {}


def _get(name, builder):
    if name not in _CACHE:
        _CACHE[name] = builder()
    return _CACHE[name]


def kernel_unfused(**inputs):
    inp = {k: np.asarray(v) for k, v in inputs.items()}
    x = np.ascontiguousarray(np.asarray(inp["L0_attn_a_x"], np.float32))
    nb = x.shape[0]
    cores = list(range(nb))

    def run(p, maps):
        res = run_bass_kernel_spmd(p.nc, maps, core_ids=cores)
        return np.stack([res.results[b]["y"] for b in range(nb)], 0)

    xs = [x[b] for b in range(nb)]
    x = run(_get("l0a", build_l0_attn_a), l0_attn_a_inmaps(xs, inp))
    xs = [x[b] for b in range(nb)]
    x = run(_get("l0f", build_l0_ffn), l0_ffn_inmaps(xs, inp))
    xs = [x[b] for b in range(nb)]
    x = run(_get("l0b", build_l0_attn_b), l0_attn_b_inmaps(xs, inp))
    xs = [x[b] for b in range(nb)]
    x = run(_get("l0f", build_l0_ffn), l0_ffn_inmaps(xs, inp))
    xs = [x[b] for b in range(nb)]
    x = run(_get("l1a", build_l1_attn_a), l1_attn_a_inmaps(xs, inp))
    xs = [x[b] for b in range(nb)]
    x = run(_get("l1f", build_l1_ffn), l1_ffn_inmaps(xs, inp))
    xs = [x[b] for b in range(nb)]
    x = run(_get("l1b", build_l1_attn_b), l1_attn_b_inmaps(xs, inp))
    xs = [x[b] for b in range(nb)]
    x = run(_get("l1f", build_l1_ffn), l1_ffn_inmaps(xs, inp))
    return np.ascontiguousarray(x.astype(np.float32))
